# version 04
# baseline (speedup 1.0000x reference)
import contextlib
import numpy as np
import ml_dtypes
import concourse.bass as bass
import concourse.mybir as mybir
from concourse.bass_utils import run_bass_kernel_spmd

F32 = mybir.dt.float32
BF16 = mybir.dt.bfloat16
I32 = mybir.dt.int32
I16 = mybir.dt.int16
AF = mybir.ActivationFunctionType
ALU = mybir.AluOpType
AX = mybir.AxisListType

B, L, D = 4, 8192, 1024
NCORE = 8
NTOK = 4096
HALO = 16
T0 = 256
NT0 = NTOK // T0
TC0 = T0 // 128
D3 = 3072
LN_EPS = 1e-5


def bf(x):
    return np.ascontiguousarray(x).astype(ml_dtypes.bfloat16)


class Prog:
    STREAMS = {"tensor": "tensor", "vector": "vector", "scalar": "scalar",
               "gpsimd": "gpsimd", "sdma": "sync", "qdma": "scalar",
               "gdma": "gpsimd"}
    COMPUTE = ("vector", "scalar", "gpsimd")
    ENGINES = tuple(STREAMS)

    def __init__(self, nc, ctx):
        self.nc = nc
        self.steps = []
        self.cnt = {s: 0 for s in self.STREAMS}
        self.sems = {s: ctx.enter_context(nc.semaphore("sem_" + s)) for s in self.STREAMS}
        self.last_wait = {}

    def emit(self, stream, fn, deps=(), dk=1):
        dl = []
        is_comp = stream in self.COMPUTE
        for t in deps:
            if t is None:
                continue
            s2, v = t
            if v <= 0:
                continue
            if s2 == stream and (is_comp or stream == "tensor"):
                continue
            k = (stream, s2)
            if self.last_wait.get(k, 0) >= v:
                continue
            self.last_wait[k] = v
            dl.append((s2, v))
        selfw = self.cnt[stream] if is_comp else None
        self.cnt[stream] += dk
        tok = (stream, self.cnt[stream])
        sems = self.sems

        def step(e, fn=fn, dl=dl, stream=stream, dk=dk, selfw=selfw):
            if selfw:
                e.wait_ge(sems[stream], selfw)
            for s2, v in dl:
                e.wait_ge(sems[s2], v)
            r = fn(e)
            r.then_inc(sems[stream], dk)

        self.steps.append((self.STREAMS[stream], step))
        return tok

    def fence(self, stream):
        return (stream, self.cnt[stream])

    def finalize(self, final_waits=()):
        sems = self.sems

        def fin(e):
            for s2, v in final_waits:
                e.wait_ge(sems[s2], v)
        self.steps.append(("sync", fin))
        with self.nc.Block() as block:
            for en in ("sync", "tensor", "vector", "scalar", "gpsimd"):
                def mk(en):
                    def _(e):
                        for e2, st in self.steps:
                            if e2 == en:
                                st(e)
                    return _
                getattr(block, en)(mk(en))


def ap3(t, off, dims):
    a = t.ap()
    return bass.AP(a.tensor, a.offset + off, [a.ap[0]] + [list(d) for d in dims])


class KB:
    def __init__(self, cfg):
        self.cfg = cfg
        self.nc = bass.Bass("TRN2", target_bir_lowering=False)
        self.ctx = contextlib.ExitStack()
        self.p = Prog(self.nc, self.ctx)
        self.dram = {}
        self.taps = {}

    def dt(self, name, shape, dtype, kind=None):
        if name in self.dram:
            return self.dram[name]
        t = self.nc.dram_tensor(name, list(shape), dtype, **({"kind": kind} if kind else {}))
        self.dram[name] = t
        return t

    def sb(self, name, shape, dtype):
        return self.ctx.enter_context(self.nc.sbuf_tensor(name, list(shape), dtype))

    def ps(self, name, shape, dtype):
        return self.ctx.enter_context(self.nc.psum_tensor(name, list(shape), dtype))

    def mm(self, psum_ap, pairs, deps=(), tp=False):
        def fn(e):
            r = None
            n = len(pairs)
            for i, (l, rh) in enumerate(pairs):
                r = e.matmul(psum_ap, l, rh, start=(i == 0), stop=(i == n - 1),
                             is_transpose=tp or None)
            return r
        return self.p.emit("tensor", fn, deps=deps)

    def trans(self, psum_t, items, eye, deps=()):
        def fn(e):
            r = None
            for src, off, ncols in items:
                dst = ap3(psum_t, off, [[1, ncols]])
                r = e.matmul(dst, src, eye, start=True, stop=True, is_transpose=True)
            return r
        return self.p.emit("tensor", fn, deps=deps)

    def act(self, dst, src, func, bias=None, scale=1.0, deps=(), accum=None):
        def fn(e):
            kw = {}
            if bias is not None:
                kw["bias"] = bias
            if accum is not None:
                kw["accum_out"] = accum
            return e.activation(dst, src, func, scale=scale, **kw)
        return self.p.emit("scalar", fn, deps=deps)

    def v(self, fn, deps=()):
        return self.p.emit("vector", fn, deps=deps)

    DMAQ = {"sync": "sdma", "scalar": "qdma", "gpsimd": "gdma"}

    def dma(self, q, dst, src, deps=(), accum=False):
        if accum:
            return self.p.emit(self.DMAQ.get(q, q),
                               lambda e: e.dma_start(dst, src, accum_op=ALU.add),
                               deps=deps, dk=16)
        return self.p.emit(self.DMAQ.get(q, q), lambda e: e.dma_start(dst, src),
                           deps=deps, dk=16)

    def tap(self, name, cols_per_tile, ntiles, dtype=F32):
        if self.cfg.get("dev") and name not in self.taps:
            self.taps[name] = self.dt("tap_" + name, [128, cols_per_tile * ntiles], dtype, "ExternalOutput")
        return self.taps.get(name)

    def tap_dma(self, name, k, sbuf_ap, cols, deps):
        t = self.taps.get(name)
        if t is not None:
            self.dma("scalar", t[:, k * cols:(k + 1) * cols], sbuf_ap, deps=deps)

    def ln_tok(self, x_ap3, nsub, width, sqbuf, stats, scol, out3, eps, deps):
        p = self.p
        ssum = stats[:, scol:scol + nsub]
        s2 = stats[:, scol + nsub:scol + 2 * nsub]
        mean = stats[:, scol + 2 * nsub:scol + 3 * nsub]
        var = stats[:, scol + 3 * nsub:scol + 4 * nsub]
        rstd = stats[:, scol + 4 * nsub:scol + 5 * nsub]
        nmean = stats[:, scol + 5 * nsub:scol + 6 * nsub]
        sq3 = sqbuf.rearrange("p (c d) -> p c d", d=width)
        t_sq = self.act(sqbuf, x_ap3, AF.Square, deps=deps)
        t1 = self.v(lambda e: e.tensor_reduce(ssum, x_ap3, axis=AX.X, op=ALU.add), deps=deps)
        t2 = self.v(lambda e: e.tensor_reduce(s2, sq3, axis=AX.X, op=ALU.add), deps=[t_sq])
        t3 = self.v(lambda e: e.tensor_scalar(mean, ssum, 1.0 / width, None, op0=ALU.mult), deps=[t1])
        t4 = self.v(lambda e: e.tensor_scalar(var, s2, 1.0 / width, None, op0=ALU.mult), deps=[t2])
        t5 = self.v(lambda e: e.tensor_tensor(nmean, mean, mean, op=ALU.mult), deps=[t3])
        t6 = self.v(lambda e: e.tensor_tensor(var, var, nmean, op=ALU.subtract), deps=[t4, t5])
        t7 = self.v(lambda e: e.tensor_scalar(var, var, 1.0, eps, op0=ALU.mult, op1=ALU.add), deps=[t6])
        t8 = self.v(lambda e: e.reciprocal(var, var), deps=[t7])
        t9 = self.act(rstd, var, AF.Sqrt, deps=[t8])
        t10 = self.v(lambda e: e.tensor_scalar(nmean, mean, -1.0, None, op0=ALU.mult), deps=[t5])
        toks = []
        for c in range(nsub):
            tk = self.v(lambda e, c=c: e.tensor_scalar(
                out3[:, c], x_ap3[:, c], nmean[:, c:c + 1], rstd[:, c:c + 1],
                op0=ALU.add, op1=ALU.mult), deps=[t9, t10])
            toks.append(tk)
        return toks[-1], t_sq

    def quant(self, xn3, nblk, xs, axs, ti, q_g2, q_g4, q_m, q_s, qsc, xd, deps):
        n = nblk * 32
        amax = qsc[:, 0:nblk]
        scale = qsc[:, nblk:2 * nblk]
        rscale = qsc[:, 2 * nblk:3 * nblk]
        xsv = xs[:, 0:n].rearrange("p (b k) -> p b k", k=32)
        t1 = self.v(lambda e: e.tensor_reduce(amax, xn3, axis=AX.X, op=ALU.max,
                                              apply_absolute_value=True), deps=deps)
        t2 = self.v(lambda e: e.tensor_scalar(scale, amax, 1.0 / 7.5, 1e-30,
                                              op0=ALU.mult, op1=ALU.max), deps=[t1])
        t3 = self.v(lambda e: e.reciprocal(rscale, scale), deps=[t2])
        rsb = ap3(qsc, 2 * nblk, [[1, nblk], [0, 32]])
        t4 = self.v(lambda e: e.tensor_tensor(xsv, xn3, rsb, op=ALU.mult), deps=[t3])
        t5 = self.v(lambda e: e.scalar_tensor_tensor(axs[:, 0:n], xs[:, 0:n], -1.0, xs[:, 0:n],
                                                     op0=ALU.mult, op1=ALU.max), deps=[t4])
        tg2 = self.v(lambda e: e.tensor_scalar(q_g2[:, 0:n], axs[:, 0:n], 2.0, None, op0=ALU.is_ge), deps=[t5])
        tg4 = self.v(lambda e: e.tensor_scalar(q_g4[:, 0:n], axs[:, 0:n], 4.0, None, op0=ALU.is_ge), deps=[t5])
        tm1 = self.v(lambda e: e.tensor_scalar(q_m[:, 0:n], q_g2[:, 0:n], 1.0, None, op0=ALU.add), deps=[tg2])
        tm2 = self.v(lambda e: e.tensor_scalar(q_s[:, 0:n], q_g4[:, 0:n], 1.0, None, op0=ALU.add), deps=[tg4])
        tst = self.v(lambda e: e.scalar_tensor_tensor(q_m[:, 0:n], q_m[:, 0:n], 0.125, q_s[:, 0:n],
                                                      op0=ALU.mult, op1=ALU.mult), deps=[tm1, tm2])
        tu = self.v(lambda e: e.tensor_scalar(q_g2[:, 0:n], q_g2[:, 0:n], -0.5, 1.0,
                                              op0=ALU.mult, op1=ALU.add), deps=[tm1])
        tw = self.v(lambda e: e.tensor_scalar(q_g4[:, 0:n], q_g4[:, 0:n], -0.5, 1.0,
                                              op0=ALU.mult, op1=ALU.add), deps=[tm2])
        trs = self.v(lambda e: e.tensor_tensor(q_g4[:, 0:n], q_g2[:, 0:n], q_g4[:, 0:n], op=ALU.mult), deps=[tu, tw])
        tt_ = self.v(lambda e: e.scalar_tensor_tensor(axs[:, 0:n], xs[:, 0:n], 8.0, q_g4[:, 0:n],
                                                      op0=ALU.mult, op1=ALU.mult), deps=[trs, t5])
        tni = self.act(ti[:, 0:n], axs[:, 0:n], AF.Copy, bias=-0.5, deps=[tt_])
        tnf = self.v(lambda e: e.tensor_copy(xs[:, 0:n], ti[:, 0:n]), deps=[tni, tt_])
        tmk = self.v(lambda e: e.scalar_tensor_tensor(q_s[:, 0:n], xs[:, 0:n], 0.5, axs[:, 0:n],
                                                      op0=ALU.add, op1=ALU.is_lt), deps=[tnf])
        tnq = self.v(lambda e: e.tensor_tensor(xs[:, 0:n], xs[:, 0:n], q_s[:, 0:n], op=ALU.add), deps=[tmk])
        scb = ap3(qsc, nblk, [[1, nblk], [0, 32]])
        axv = axs[:, 0:n].rearrange("p (b k) -> p b k", k=32)
        qmv = q_m[:, 0:n].rearrange("p (b k) -> p b k", k=32)
        tF = self.v(lambda e: e.tensor_tensor(axv, qmv, scb, op=ALU.mult), deps=[tst, tnf])
        txd = self.v(lambda e: e.tensor_tensor(xd[:, 0:n], xs[:, 0:n], axs[:, 0:n], op=ALU.mult), deps=[tnq, tF])
        return txd


def build_layer0(kb, dev=False):
    nc, p = kb.nc, kb.p
    cfg = kb.cfg
    fused = cfg.get("fused", False)
    NTOK0 = L if fused else NTOK
    NTL = NTOK0 // T0
    es0 = contextlib.ExitStack()
    sb0 = lambda n_, sh, dt_: es0.enter_context(nc.sbuf_tensor(n_, list(sh), dt_))
    ps0 = lambda n_, sh, dt_: es0.enter_context(nc.psum_tensor(n_, list(sh), dt_))
    x_in = kb.dt("x", [HALO + NTOK0, D], F32, "ExternalInput")
    w_in_d = kb.dt("w_in", [D, D3], F32, "ExternalInput")
    w_out_d = kb.dt("w_out", [D, D], F32, "ExternalInput")
    fw_d = kb.dt("fw", [D, D], F32, "ExternalInput")
    b_in_d = kb.dt("b_in_c", [128, 24], F32, "ExternalInput")
    sw_d = kb.dt("short_w_c", [128, 72], F32, "ExternalInput")
    sbc_d = kb.dt("short_b_c", [128, 24], F32, "ExternalInput")
    iw_d = kb.dt("inner_w_c", [128, 32], F32, "ExternalInput")
    ib_d = kb.dt("inner_b_c", [128, 8], F32, "ExternalInput")
    bo_d = kb.dt("b_out_c", [128, 8], F32, "ExternalInput")
    fb_d = kb.dt("fb_c", [128, 8], F32, "ExternalInput")
    eye32_d = kb.dt("eye32", [128, 128], F32, "ExternalInput")
    eye16_d = kb.dt("eye16", [128, 128], BF16, "ExternalInput")
    hm_d = kb.dt("hmask", [128, 1], F32, "ExternalInput")
    okind = "ExternalOutput" if (dev or kb.cfg.get("io")) else None
    x2_d = kb.dt("x2_d", [NTOK0, D], F32, okind)
    xn1t32_d = kb.dt("xn1t32_d", [D, NTOK0], F32, okind)
    xn1t16_d = kb.dt("xn1t16_d", [D, NTOK0], BF16, okind)
    if fused:
        xhB_d = kb.dt("xhB", [HALO, D], F32, "ExternalInput")
        kb.dt("hmaskB", [128, 1], F32, "ExternalInput")

    w_in_sb = sb0("w_in_sb", [128, 8 * D3], BF16)
    w_out_sb = sb0("w_out_sb", [128, 8 * D], BF16)
    fwdT = sb0("fwdT", [128, 8 * D], BF16)
    b_in_c = sb0("b_in_cs", [128, 24], F32)
    sw_c = sb0("sw_cs", [128, 72], F32)
    sb_c = sb0("sb_cs", [128, 24], F32)
    iw_c = sb0("iw_cs", [128, 32], F32)
    ib_c = sb0("ib_cs", [128, 8], F32)
    bo_c = sb0("bo_cs", [128, 8], F32)
    fb_c = sb0("fb_cs", [128, 8], F32)
    eye32 = sb0("eye32_s", [128, 128], F32)
    eye16 = sb0("eye16_s", [128, 128], BF16)
    hmask = sb0("hmask_s", [128, 1], F32)
    hmaskB = sb0("hmaskB_s", [128, 1], F32) if fused else None

    t_win = kb.dma("gpsimd", w_in_sb[:, :], w_in_d[:, :].rearrange("(kc p) e -> p kc e", p=128))
    t_wout = kb.dma("gpsimd", w_out_sb[:, :], w_out_d[:, :].rearrange("(kc p) e -> p kc e", p=128))
    for nm, dst in [("b_in_c", b_in_c), ("short_w_c", sw_c), ("short_b_c", sb_c),
                    ("inner_w_c", iw_c), ("inner_b_c", ib_c), ("b_out_c", bo_c),
                    ("fb_c", fb_c), ("eye32", eye32), ("eye16", eye16), ("hmask", hmask)] + \
                   ([("hmaskB", hmaskB)] if fused else []):
        kb.dma("sync", dst[:, :], kb.dram[nm][:, :])
    t_consts = p.fence("sdma")

    pT16 = ps0("pT16", [128, 1024], BF16)
    pT32 = ps0("pT32", [128, 512], F32)
    pM = [ps0("pM0", [128, 512], F32), ps0("pM1", [128, 512], F32)]
    pC = [ps0("pC0", [128, 512], F32), ps0("pC1", [128, 512], F32)]
    dgS = sb0("dgS", [128, 72 * 128], BF16)
    dgI = sb0("dgI", [128, 32 * 128], BF16)
    t_dg = []
    for j in range(72):
        t_dg.append(kb.v(lambda e, j=j: e.tensor_scalar(
            dgS[:, j * 128:(j + 1) * 128], eye16[:, :], sw_c[:, j:j + 1], None,
            op0=ALU.mult), deps=[t_consts]))
    for j in range(32):
        t_dg.append(kb.v(lambda e, j=j: e.tensor_scalar(
            dgI[:, j * 128:(j + 1) * 128], eye16[:, :], iw_c[:, j:j + 1], None,
            op0=ALU.mult), deps=[t_consts]))
    t_dg = t_dg[-1]

    x_t = [sb0("x_t0", [128, TC0 * D], F32), sb0("x_t1", [128, TC0 * D], F32)]
    xn = sb0("xn", [128, TC0 * D], F32)
    stats = sb0("stats", [128, 64], F32)
    qsc = sb0("qsc", [128, 3 * TC0 * 32], F32)
    xd = sb0("xd", [128, TC0 * D], BF16)
    xnT = sb0("xnT", [128, 8 * T0], BF16)
    xdT = sb0("xdT", [128, 8 * T0], BF16)
    x1e = sb0("x1e", [128, 24 * (T0 + 2)], BF16)
    qb_ = sb0("qb_", [128, 8 * T0], BF16)
    kbf = sb0("kbf", [128, 8 * T0], BF16)
    vbf = sb0("vbf", [128, 8 * T0], BF16)
    vve = sb0("vve", [128, 8 * (T0 + 3)], BF16)
    qs1 = vve
    s_b = sb0("s_b", [128, 8 * T0], F32)
    g_b = sb0("g_b", [128, 8 * T0], F32)
    ti_b = sb0("ti_b", [128, TC0 * D], I16)
    x1sl = sb0("x1sl", [128, 24 * 2], BF16)
    vvsl = sb0("vvsl", [128, 8 * 3], BF16)
    x1slB = sb0("x1slB", [128, 24 * 2], BF16) if fused else None
    vvslB = sb0("vvslB", [128, 8 * 3], BF16) if fused else None
    xn1t32 = sb0("xn1t32", [128, 8 * T0], F32)
    W1E = T0 + 2
    VVE = T0 + 3

    dbg = sb0("dbg", [128, 2048], F32) if dev else None
    tapstate = {"last": None}

    def dtap(name, src_ap, ncols, deps):
        if not dev:
            return
        d = kb.dt("tap_" + name, [128, ncols], F32, "ExternalOutput")
        tcp = kb.v(lambda e: e.tensor_copy(dbg[:, 0:ncols], src_ap),
                   deps=list(deps) + [tapstate["last"]])
        tapstate["last"] = kb.dma("scalar", d[:, :], dbg[:, 0:ncols], deps=[tcp])

    NBW = 32
    with nc.sbuf_tensor("fw_pt", [128, D], F32) as fw_pt, \
         nc.sbuf_tensor("fwd_pt", [128, D], BF16) as fwd_pt:
        ev_tok = None
        tq = None
        fw_r = fw_d[:, :].rearrange("(ec p) d -> p ec d", p=128)
        for ps_ in range(8):
            t_fw = kb.dma("sync", fw_pt[:, :].rearrange("p (c d) -> p c d", d=D),
                          fw_r[:, ps_:ps_ + 1, :], deps=[tq])
            fw3 = fw_pt[:, :].rearrange("p (b k) -> p b k", k=32)
            pe_fq = p.fence("tensor")
            tq = kb.quant(fw3, NBW, s_b, g_b, ti_b, qb_, kbf, vbf, qs1, qsc, fwd_pt,
                          deps=[t_fw, pe_fq])
            ec = ps_
            for dq in range(2):
                items = [(fwd_pt[:, (dq * 4 + j) * 128: (dq * 4 + j) * 128 + 128],
                          j * 128, 128) for j in range(4)]

                def fnq(e, items=items):
                    r = None
                    for src, off, ncols in items:
                        r = e.matmul(ap3(pT16, off, [[1, ncols]]), src, eye16[:, :],
                                     start=True, stop=True, is_transpose=True)
                    return r
                ttr = p.emit("tensor", fnq, deps=[tq, t_consts, ev_tok])
                dst = ap3(fwdT, (dq * 4) * D + ec * 128, [[D, 4], [1, 128]])
                src = ap3(pT16, 0, [[128, 4], [1, 128]])
                ev_tok = kb.v(lambda e, dst=dst, src=src: e.tensor_copy(dst, src), deps=[ttr])
    t_fwdT = ev_tok
    pre_f = [p.fence(e) for e in Prog.ENGINES]

    xh = x_t[0]
    xnh = x_t[1]
    sqh = s_b
    stats_h = sb0("stats_h", [16, 8], F32)
    xnTh = sb0("xnTh", [128, 8 * 16], BF16)
    x1h = sb0("x1h", [128, 24 * 16], BF16)
    uch = sb0("uch", [128, 24 * 14], BF16)
    vvh = sb0("vvh", [128, 8 * 14], BF16)
    tmpc = sb0("tmpc", [128, T0], BF16)

    def emit_halo(x_src, mask_col, x1sl_t, vvsl_t, pre):
     t_xh = kb.dma("sync", xh[0:16, 0:D], x_src, deps=pre)
     xh3 = xh[0:16, 0:D].rearrange("p (c d) -> p c d", d=D)
     xnh3 = xnh[0:16, 0:D].rearrange("p (c d) -> p c d", d=D)
     t_lnh, _ = kb.ln_tok(xh3, 1, D, sqh[0:16, 0:D], stats_h, 0, xnh3, LN_EPS,
                          deps=pre_f + [t_xh])
     tok_tr = None
     for dq in range(2):
        items = [(xnh[0:16, (dq * 4 + j) * 128:(dq * 4 + j) * 128 + 128], j * 16, 16) for j in range(4)]

        def fnh(e, items=items):
            r = None
            for src, off, ncols in items:
                r = e.matmul(ap3(pT32, off, [[1, ncols]]), src, eye32[0:16, 0:16], start=True,
                             stop=True, is_transpose=True)
            return r
        th = p.emit("tensor", fnh, deps=[t_lnh, t_consts, tok_tr])
        dstp = ap3(xnTh, dq * 4 * 16, [[16, 4], [1, 16]])
        srcp = ap3(pT32, 0, [[16, 4], [1, 16]])
        tok_tr = kb.v(lambda e, d=dstp, s=srcp: e.tensor_copy(d, s), deps=[th])
     ev_h = []
     prev_ev = None
     for mc in range(24):
        bank = pM[mc % 2]
        pairs = [(w_in_sb[:, kc * D3 + mc * 128: kc * D3 + mc * 128 + 128],
                  xnTh[:, kc * 16:(kc + 1) * 16]) for kc in range(8)]
        dep_bank = ev_h[mc - 2] if mc >= 2 else None
        tmm = kb.mm(ap3(bank, 0, [[1, 16]]), pairs, deps=[tok_tr, t_win, dep_bank])
        tev = kb.act(x1h[:, mc * 16:(mc + 1) * 16], ap3(bank, 0, [[1, 16]]), AF.Identity,
                     bias=b_in_c[:, mc:mc + 1], deps=[tmm])
        ev_h.append(tev)
     t_x1h = kb.v(lambda e: e.tensor_scalar(x1h[:, :], x1h[:, :], mask_col, None, op0=ALU.mult),
                 deps=ev_h + [t_consts])
     tc3 = None
     for cc in range(24):
        dst = uch[:, cc * 14:(cc + 1) * 14]
        ta = kb.v(lambda e, cc=cc, dst=dst: e.tensor_scalar(
            dst, x1h[:, cc * 16:cc * 16 + 14], sw_c[:, cc * 3:cc * 3 + 1],
            sb_c[:, cc:cc + 1], op0=ALU.mult, op1=ALU.add), deps=[t_x1h])
        for j in (1, 2):
            kb.v(lambda e, cc=cc, j=j: e.tensor_scalar(
                tmpc[:, 0:14], x1h[:, cc * 16 + j:cc * 16 + j + 14],
                sw_c[:, cc * 3 + j:cc * 3 + j + 1], None, op0=ALU.mult), deps=[])
            tc3 = kb.v(lambda e, dst=dst: e.tensor_tensor(dst, dst, tmpc[:, 0:14], op=ALU.add),
                       deps=[])
     vvh3 = vvh[:, :].rearrange("p (c t) -> p c t", t=14)
     tv1 = kb.v(lambda e: e.tensor_tensor(vvh3, ap3(uch, 8 * 14, [[14, 8], [1, 14]]),
                                         ap3(uch, 16 * 14, [[14, 8], [1, 14]]), op=ALU.mult), deps=[tc3])
     tv2 = kb.v(lambda e: e.tensor_scalar(vvh[:, :], vvh[:, :], mask_col, None, op0=ALU.mult),
               deps=[tv1])
     t_sl1 = kb.v(lambda e: e.tensor_copy(ap3(x1sl_t, 0, [[2, 24], [1, 2]]),
                                         ap3(x1h, 14, [[16, 24], [1, 2]])), deps=[t_x1h])
     t_sl2 = kb.v(lambda e: e.tensor_copy(ap3(vvsl_t, 0, [[3, 8], [1, 3]]),
                                         ap3(vvh, 11, [[14, 8], [1, 3]])), deps=[tv2])

     return tok_tr, t_sl1, t_sl2

    tok_tr, t_sl1, t_sl2 = emit_halo(x_in[0:HALO, :], hmask[:, 0:1], x1sl, vvsl,
                                     pre_f + [p.fence("sdma")])
    if fused:
        tokB, tB_sl1, tB_sl2 = emit_halo(xhB_d[:, :], hmaskB[:, 0:1], x1slB, vvslB,
                                         [p.fence("vector"), p.fence("tensor"), p.fence("scalar")])
        tok_tr = tokB
    halo_f = [p.fence("vector"), p.fence("tensor"), p.fence("scalar")]
    if dev:
        dtap("x1h", x1h[:, :], 384, [t_x1h])
        dtap("uch", uch[:, :], 336, [tc3])
        dtap("vvh", vvh[:, :], 112, [tv2])
        dtap("x1sl_h", x1sl[:, :], 48, [t_sl1])
        dtap("vvsl_h", vvsl[:, :], 24, [t_sl2])
    NB = TC0 * 32
    prev = {"pT32_free": tok_tr}
    last_store = []
    for k in range(NTL):
        xb = x_t[k % 2]
        r0 = HALO + k * T0
        t_x = kb.dma("sync", xb[:, :].rearrange("p (c d) -> p c d", d=D),
                     x_in[r0:r0 + T0, :].rearrange("(c p) d -> p c d", p=128),
                     deps=(pre_f + halo_f if k < 2 else prev.get(("x_free", k % 2), [])))
        x3 = xb[:, :].rearrange("p (c d) -> p c d", d=D)
        xn3 = xn[:, :].rearrange("p (c d) -> p c d", d=D)
        t_ln, _ = kb.ln_tok(x3, TC0, D, s_b[:, 0:TC0 * D], stats, 0, xn3, LN_EPS,
                            deps=[t_x, prev.get("pe")])
        if k == 0:
            dtap("xn0", xn[:, :], 2048, [t_ln])
        xnq = xn[:, :].rearrange("p (b k) -> p b k", k=32)
        t_qd = kb.quant(xnq, NB, s_b, g_b, ti_b, qb_, kbf, vbf, qs1, qsc, xd,
                        deps=[t_ln, prev.get("pe")])
        if k == 0:
            dtap("xd0", xd[:, :], 2048, [t_qd])
        t_xnT = []
        tr_dep = prev.get("pT32_free")
        first_ev = prev.get("pe")
        for tc in range(TC0):
            for dq in range(2):
                items = [(xn[:, tc * D + (dq * 4 + j) * 128: tc * D + (dq * 4 + j) * 128 + 128],
                          j * 128, 128) for j in range(4)]

                def ftr(e, items=items):
                    r = None
                    for src, off, ncols in items:
                        r = e.matmul(ap3(pT32, off, [[1, ncols]]), src, eye32[:, :],
                                     start=True, stop=True, is_transpose=True)
                    return r
                ttr = p.emit("tensor", ftr, deps=[t_ln, tr_dep])
                dstp = ap3(xnT, dq * 4 * T0 + tc * 128, [[T0, 4], [1, 128]])
                tr_dep = kb.act(dstp, ap3(pT32, 0, [[128, 4], [1, 128]]), AF.Copy,
                                deps=[ttr, first_ev])
                first_ev = None
                t_xnT.append(tr_dep)
        t_xdT = []
        trd_dep = prev.get("pT16_free")
        for tc in range(TC0):
            items = [(xd[:, tc * D + j * 128: tc * D + j * 128 + 128], j * 128, 128) for j in range(8)]

            def ftd(e, items=items):
                r = None
                for src, off, ncols in items:
                    r = e.matmul(ap3(pT16, off, [[1, ncols]]), src, eye16[:, :],
                                 start=True, stop=True, is_transpose=True)
                return r
            ttr = p.emit("tensor", ftd, deps=[t_qd, trd_dep, t_fwdT])
            dstp = ap3(xdT, tc * 128, [[T0, 8], [1, 128]])
            trd_dep = kb.v(lambda e, d=dstp: e.tensor_copy(d, ap3(pT16, 0, [[128, 8], [1, 128]])),
                           deps=[ttr])
            t_xdT.append(trd_dep)
        prev["pT16_free"] = trd_dep

        seam = fused and (k == NTOK // T0)
        rs_x1 = x1slB if seam else x1sl
        rs_dep = tB_sl1 if seam else t_sl1
        t_rs = kb.v(lambda e, rs_x1=rs_x1: e.tensor_copy(ap3(x1e, 0, [[W1E, 24], [1, 2]]),
                                                         ap3(rs_x1, 0, [[2, 24], [1, 2]])),
                    deps=[rs_dep, prev.get("conv_done")])
        ev_toks = []
        mm_prev = [None, None]
        for mc in range(24):
            bank = pM[mc % 2]
            pairs = [(w_in_sb[:, kc * D3 + mc * 128: kc * D3 + mc * 128 + 128],
                      xnT[:, kc * T0:(kc + 1) * T0]) for kc in range(8)]
            tmm = kb.mm(ap3(bank, 0, [[1, T0]]), pairs,
                        deps=[t_xnT[-1], t_win, mm_prev[mc % 2], prev.get("conv_done") if mc < 2 else None])
            tev = kb.act(ap3(x1e, mc * W1E + 2, [[1, T0]]), ap3(bank, 0, [[1, T0]]),
                         AF.Identity, bias=b_in_c[:, mc:mc + 1],
                         deps=[tmm, t_rs if mc == 0 else None, prev.get("x1sl_done")])
            mm_prev[mc % 2] = tev
            ev_toks.append(tev)
        grp = {0: qb_, 1: kbf, 2: vbf}
        t_conv = []
        cdep = {0: prev.get("pC0f"), 1: prev.get("pC1f")}
        for g in range(3):
            gb = grp[g]
            last = None
            for l in range(8):
                cc = g * 8 + l
                bank = pC[cc % 2]

                def fcv(e, cc=cc, bank=bank):
                    r = None
                    for j in range(3):
                        r = e.matmul(ap3(bank, 0, [[1, T0]]),
                                     dgS[:, (cc * 3 + j) * 128:(cc * 3 + j) * 128 + 128],
                                     ap3(x1e, cc * W1E + j, [[1, T0]]),
                                     start=(j == 0), stop=(j == 2))
                    return r
                tmm = p.emit("tensor", fcv, deps=[ev_toks[cc], t_rs, t_dg, cdep[cc % 2]])
                last = kb.v(lambda e, cc=cc, l=l, gb=gb, bank=bank: e.tensor_scalar(
                    gb[:, l * T0:(l + 1) * T0], ap3(bank, 0, [[1, T0]]),
                    1.0, sb_c[:, cc:cc + 1], op0=ALU.mult, op1=ALU.add),
                    deps=[tmm, t_qd])
                cdep[cc % 2] = last
            t_conv.append(last)
        if k == 0:
            dtap("x1e0", ap3(x1e, 2, [[W1E, 8], [1, T0]]), 2048, ev_toks)
            dtap("q0", qb_[:, :], 2048, [t_conv[0]])
            dtap("k0", kbf[:, :], 2048, [t_conv[1]])
            dtap("v0c", vbf[:, :], 2048, [t_conv[2]])
        t_sl1 = kb.v(lambda e: e.tensor_copy(ap3(x1sl, 0, [[2, 24], [1, 2]]),
                                             ap3(x1e, T0, [[W1E, 24], [1, 2]])),
                     deps=ev_toks[-1:])
        prev["x1sl_done"] = t_sl1
        prev["conv_done"] = t_conv[2]
        t_vv = kb.v(lambda e: e.tensor_tensor(ap3(vve, 3, [[VVE, 8], [1, T0]]),
                                              kbf[:, :].rearrange("p (c t) -> p c t", t=T0),
                                              vbf[:, :].rearrange("p (c t) -> p c t", t=T0),
                                              op=ALU.mult), deps=[t_conv[1], t_conv[2], prev.get("v2_done")])
        rs_vv = vvslB if seam else vvsl
        rs_dep2 = tB_sl2 if seam else t_sl2
        t_vr = kb.v(lambda e, rs_vv=rs_vv: e.tensor_copy(ap3(vve, 0, [[VVE, 8], [1, 3]]),
                                                         ap3(rs_vv, 0, [[3, 8], [1, 3]])),
                    deps=[rs_dep2, t_vv])
        t_sl2 = kb.v(lambda e: e.tensor_copy(ap3(vvsl, 0, [[3, 8], [1, 3]]),
                                             ap3(vve, T0, [[VVE, 8], [1, 3]])), deps=[t_vv])
        b1 = None
        for l in range(8):
            bank = pC[l % 2]

            def fci(e, l=l, bank=bank):
                r = None
                for j in range(4):
                    r = e.matmul(ap3(bank, 0, [[1, T0]]),
                                 dgI[:, (l * 4 + j) * 128:(l * 4 + j) * 128 + 128],
                                 ap3(vve, l * VVE + j, [[1, T0]]),
                                 start=(j == 0), stop=(j == 3))
                return r
            tmm = p.emit("tensor", fci, deps=[t_vr, t_sl2, cdep[l % 2]])
            b1 = kb.v(lambda e, l=l, bank=bank: e.tensor_scalar(
                kbf[:, l * T0:(l + 1) * T0], ap3(bank, 0, [[1, T0]]),
                1.0, ib_c[:, l:l + 1], op0=ALU.mult, op1=ALU.add), deps=[tmm])
            cdep[l % 2] = b1
        prev["pC0f"], prev["pC1f"] = cdep[0], cdep[1]
        t_y = kb.v(lambda e: e.tensor_tensor(vbf[:, :], qb_[:, :], kbf[:, :], op=ALU.mult),
                   deps=[b1, t_conv[0]])
        prev["v2_done"] = t_y
        if k == 0:
            dtap("vve0", ap3(vve, 0, [[VVE, 8], [1, T0]]), 2048, [t_vr, t_vv])
            dtap("y0", vbf[:, :], 2048, [t_y])

        mm_prev2 = [None, None]
        sg_toks = []
        for which, (wsb, rhs, dstb, biasc, func, wdep) in enumerate([
                (w_out_sb, vbf, s_b, bo_c, AF.Identity, t_wout),
                (fwdT, xdT, g_b, fb_c, AF.Silu, t_fwdT)]):
            for mc in range(8):
                bank = pM[mc % 2]
                pairs = [(wsb[:, kc * D + mc * 128: kc * D + mc * 128 + 128],
                          rhs[:, kc * T0:(kc + 1) * T0]) for kc in range(8)]
                dep_in = t_y if which == 0 else t_xdT[-1]
                tmm = kb.mm(ap3(bank, 0, [[1, T0]]), pairs,
                            deps=[dep_in, wdep, mm_prev2[mc % 2]])
                tev = kb.act(dstb[:, mc * T0:(mc + 1) * T0], ap3(bank, 0, [[1, T0]]),
                             func, bias=biasc[:, mc:mc + 1], deps=[tmm, t_qd])
                mm_prev2[mc % 2] = tev
                sg_toks.append(tev)
        if k == 0:
            dtap("s0", s_b[:, :], 2048, sg_toks[:8])
            dtap("g0", g_b[:, :], 2048, sg_toks[8:])
        t_p = kb.v(lambda e: e.tensor_tensor(s_b[:, :], s_b[:, :], g_b[:, :], op=ALU.mult),
                   deps=sg_toks)
        t_pT = []
        trp_dep = prev.get("pT32_free")
        for tc in range(TC0):
            for dq in range(2):
                items = [(s_b[:, (dq * 4 + j) * T0 + tc * 128: (dq * 4 + j) * T0 + tc * 128 + 128],
                          j * 128, 128) for j in range(4)]

                def ftp(e, items=items):
                    r = None
                    for src, off, ncols in items:
                        r = e.matmul(ap3(pT32, off, [[1, ncols]]), src, eye32[:, :],
                                     start=True, stop=True, is_transpose=True)
                    return r
                ttr = p.emit("tensor", ftp, deps=[t_p, trp_dep])
                dstp = ap3(xn, tc * D + dq * 4 * 128, [[128, 4], [1, 128]])
                trp_dep = kb.v(lambda e, d=dstp: e.tensor_copy(d, ap3(pT32, 0, [[128, 4], [1, 128]])),
                               deps=[ttr, t_qd])
                t_pT.append(trp_dep)
        prev["pT32_free"] = trp_dep
        if k == 0:
            dtap("p0", s_b[:, :], 2048, [t_p])
            dtap("ptok0", xn[:, :], 2048, [t_pT[-1]])
        t_x2 = kb.v(lambda e, xb=xb: e.tensor_tensor(xb[:, :], xb[:, :], xn[:, :], op=ALU.add),
                    deps=[t_pT[-1]])
        if k == 0:
            dtap("x2t0", xb[:, :], 2048, [t_x2])
        t_st_x2 = kb.dma("scalar", x2_d[k * T0:(k + 1) * T0, :].rearrange("(c p) d -> p c d", p=128),
                         xb[:, :].rearrange("p (c d) -> p c d", d=D), deps=[t_x2])
        xn1v = xn[:, :].rearrange("p (c d) -> p c d", d=D)
        t_ln1, t_sq1 = kb.ln_tok(x3, TC0, D, g_b[:, 0:TC0 * D], stats, 20, xn1v, LN_EPS,
                                 deps=[t_x2, t_p, prev.get("pe")])
        t_tr1 = []
        tr1_dep = prev.get("pT32_free")
        ev_extra = prev.get("xn1t_free", [])
        for tc in range(TC0):
            for dq in range(2):
                items = [(xn[:, tc * D + (dq * 4 + j) * 128: tc * D + (dq * 4 + j) * 128 + 128],
                          j * 128, 128) for j in range(4)]

                def ft1(e, items=items):
                    r = None
                    for src, off, ncols in items:
                        r = e.matmul(ap3(pT32, off, [[1, ncols]]), src, eye32[:, :],
                                     start=True, stop=True, is_transpose=True)
                    return r
                ttr = p.emit("tensor", ft1, deps=[t_ln1, tr1_dep])
                dstp = ap3(xn1t32, dq * 4 * T0 + tc * 128, [[T0, 4], [1, 128]])
                tr1_dep = kb.v(lambda e, d=dstp: e.tensor_copy(d, ap3(pT32, 0, [[128, 4], [1, 128]])),
                               deps=[ttr] + list(ev_extra))
                ev_extra = []
                t_tr1.append(tr1_dep)
        prev["pT32_free"] = tr1_dep
        st1 = kb.dma("scalar", xn1t32_d[:, k * T0:(k + 1) * T0].rearrange("(dc p) t -> p dc t", p=128),
                     xn1t32[:, :].rearrange("p (dc t) -> p dc t", t=T0), deps=[tr1_dep])
        st2 = kb.dma("gpsimd", xn1t16_d[:, k * T0:(k + 1) * T0].rearrange("(dc p) t -> p dc t", p=128),
                     xn1t32[:, :].rearrange("p (dc t) -> p dc t", t=T0), deps=[tr1_dep])
        prev[("x_free", k % 2)] = [t_st_x2, t_ln1, st1]
        prev["xn1t_free"] = [st1, st2]
        prev["pe"] = p.fence("tensor")
        last_store = [t_st_x2, st1, st2]
    es0.close()
    return last_store

T1 = 512
NT1 = L // T1
SC = float(64.0 ** -0.25)
LN16 = float(np.log(16.0))
EPS1 = 1e-6


def build_l1attn(kb, dev=False, na=NT1, nb=8):
    nc, p = kb.nc, kb.p
    skip = kb.cfg.get("skip", set())
    fused = kb.cfg.get("fused", False)
    es2 = contextlib.ExitStack()
    if fused:
        xnf_d = kb.dram["xn1t16_d"]
        xn32_d = kb.dram["xn1t32_d"]
    else:
        xnf_d = kb.dt("xnf", [D, L], BF16, "ExternalInput")
        xn32_d = kb.dt("xn32", [D, NTOK], F32, "ExternalInput")
    for nm in ["wq", "wk", "wv", "wo"]:
        kb.dt(nm, [D, D], F32, "ExternalInput")
    omT_d = kb.dt("omT", [128, 256], F32, "ExternalInput")
    e16_d = kb.dt("e16f", [128, 128], F32, "ExternalInput")
    for nm in ["ln1g_c", "ln1b_c", "ln2g_c", "ln2b_c", "b2_c"]:
        kb.dt(nm, [128, 8], F32, "ExternalInput")
    kb.dt("b1_c", [128, 32], F32, "ExternalInput")
    w1_d = kb.dt("w1", [D, 4 * D], F32, "ExternalInput")
    w2_d = kb.dt("w2", [4 * D, D], F32, "ExternalInput")
    kb.dt("eye16", [128, 128], BF16, "ExternalInput")
    kb.dt("eye32", [128, 128], F32, "ExternalInput")
    kb.dt("onesb", [128, 1], BF16, "ExternalInput")
    kb.dt("onesf", [1, 128], F32, "ExternalInput")
    okind = "ExternalOutput" if (dev or kb.cfg.get("io")) else None
    s_d = kb.dt("s_d", [D, NTOK], F32, okind)
    phase_dep = kb.cfg.get("phase_dep_l2", [])
    qT_dram = kb.dt("qT_sc", [D, NTOK], BF16)
    sqq_dram = kb.dt("sqq_sc", [128, 8 * 64], F32)

    es2sb = lambda n_, sh, dt_: es2.enter_context(kb.nc.sbuf_tensor('l2_' + n_, list(sh), dt_))
    omT = es2sb("omT_s", [128, 256], BF16)
    e16f = es2sb("e16f_s", [128, 128], F32)
    eye16 = es2sb("eye16_s", [128, 128], BF16)
    eye32 = es2sb("eye32_s", [128, 128], F32)
    onesb = es2sb("onesb_s", [128, 1], BF16)
    onesf = es2sb("onesf_s", [1, 128], F32)
    lnv = {nm: es2sb(nm + "_s", [128, 8], F32)
           for nm in ["ln1g_c", "ln1b_c", "ln2g_c", "ln2b_c", "b2_c"]}
    b1c = es2sb("b1c_s", [128, 32], F32)
    kb.dma("gpsimd", omT[:, :], omT_d[:, :], deps=phase_dep)
    first_sd = list(phase_dep)
    for nm, dst in ([("e16f", e16f), ("eye16", eye16), ("eye32", eye32),
                     ("onesb", onesb), ("onesf", onesf), ("b1_c", b1c)] +
                    [(nm, lnv[nm]) for nm in lnv]):
        kb.dma("sync", dst[:, :], kb.dram[nm][:, :], deps=first_sd)
        first_sd = []
    t_w = [p.fence("gdma"), p.fence("sdma")] + list(phase_dep)

    qTt = es2sb("qTt", [128, 8 * T1], BF16)
    fscr = es2sb("fscr", [128, 8 * T1], F32)
    bscr = es2sb("bscr", [128, 8 * T1], BF16)
    bsc2 = es2sb("bsc2", [128, 8 * T1], BF16)
    sqc = es2sb("sqc", [128, 64], F32)
    sqcq = es2sb("sqcq", [128, 64], F32)
    stb = es2sb("stb", [128, 16], F32)
    cng = es2sb("cng", [128, 16], F32)
    kv_bf = es2sb("kv_bfuf", [128, 5 * 455], BF16)

    def dtap(name, src_ap, ncols, deps, dt_=F32):
        if not dev:
            return
        d = kb.dt("tap_" + name, [128, ncols], dt_, "ExternalOutput")
        kb.dma("qdma", d[:, :], src_ap, deps=deps)

    es_a = contextlib.ExitStack()
    with es_a:
        _e = es_a.enter_context
        pk0 = _e(nc.psum_tensor("pkv0", [128, 512], F32))
        pk1 = _e(nc.psum_tensor("pkv1", [128, 512], F32))
        pk2 = _e(nc.psum_tensor("pkv2", [128, 512], F32))
        pk3 = _e(nc.psum_tensor("pkv3", [128, 512], F32))
        pk4 = _e(nc.psum_tensor("pkv4", [128, 512], F32))
        pA = _e(nc.psum_tensor("pA", [128, 512], F32))
        pB = _e(nc.psum_tensor("pB", [128, 512], F32))
        pC = _e(nc.psum_tensor("pC", [128, 512], F32))
        wqs = _e(nc.sbuf_tensor("wqs", [128, 8 * D], BF16))
        wks = _e(nc.sbuf_tensor("wks", [128, 8 * D], BF16))
        wvs = _e(nc.sbuf_tensor("wvs", [128, 8 * D], BF16))
        xnb0 = _e(nc.sbuf_tensor("xnb0", [128, 8 * T1], BF16))
        xnb1 = _e(nc.sbuf_tensor("xnb1", [128, 8 * T1], BF16))
        kT = _e(nc.sbuf_tensor("kT", [128, 8 * T1], BF16))
        vab = _e(nc.sbuf_tensor("vab", [128, 4 * 1040], BF16))
        sqrow = _e(nc.sbuf_tensor("sqrow", [16, T1], F32))
        kv_sb = _e(nc.sbuf_tensor("kv_sbuf", [128, 5 * 455], F32))
        xnb = [xnb0, xnb1]
        for nm, dst in [("wq", wqs), ("wk", wks), ("wv", wvs)]:
            kb.dma("gpsimd", dst[:, :], kb.dram[nm][:, :].rearrange("(kc p) e -> p kc e", p=128))
        t_w = t_w + [p.fence("gdma")]
        pkv = [pk0, pk1, pk2, pk3, pk4]
        prev = {}
        kv_last = None
        for ti in range(na):
            own = ti < nb
            xb = xnb[ti % 2]
            t_x = kb.dma("sync", xb[:, :].rearrange("p (kc t) -> p kc t", t=T1),
                         xnf_d[:, ti * T1:(ti + 1) * T1].rearrange("(kc p) t -> p kc t", p=128),
                         deps=[prev.get(("xfree", ti % 2))] + (t_w if ti == 0 else []))
            rot = [pA, pB]
            rotdep = {0: [prev.get("rA")], 1: [prev.get("rB")]}

            for which in (["q", "k"] if own else ["k"]):
                wsb = wqs if which == "q" else wks
                dstT = qTt if which == "q" else kT
                dst_sq = sqcq if which == "q" else sqc
                ev = []
                for mc in range(8):
                    bank = rot[mc % 2]
                    pairs = [(wsb[:, kc * D + mc * 128: kc * D + mc * 128 + 128],
                              xb[:, kc * T1:(kc + 1) * T1]) for kc in range(8)]
                    if "mm" in skip:
                        continue
                    tmm = kb.mm(ap3(bank, 0, [[1, T1]]), pairs,
                                deps=[t_x] + rotdep[mc % 2])
                    e1 = e2 = tmm
                    if "e1" not in skip:
                        e1 = kb.act(fscr[:, mc * T1:(mc + 1) * T1], ap3(bank, 0, [[1, T1]]),
                                    AF.Square, scale=SC, deps=[tmm])
                    if "e2" not in skip:
                        e2 = kb.v(lambda e, bank=bank, dstT=dstT, mc=mc: e.tensor_scalar(
                            dstT[:, mc * T1:(mc + 1) * T1], ap3(bank, 0, [[1, T1]]),
                            SC, None, op0=ALU.mult), deps=[tmm, e1, prev.get("kTfree") if which == "k" else prev.get("qTfree")])
                    rotdep[mc % 2] = [e1, e2]
                    ev.append((e1, e2))
                if "mm" in skip:
                    continue
                if "sq" in skip:
                    continue
                sq_mm = None
                for mc in range(8):
                    sq_mm = p.emit("tensor", lambda e, mc=mc: e.matmul(
                        pC[0:16, 0:T1], e16f[:, mc * 16:(mc + 1) * 16],
                        fscr[:, mc * T1:(mc + 1) * T1], start=(mc == 0), stop=(mc == 7)),
                        deps=[ev[mc][0], prev.get("pCfree")])
                e3 = kb.v(lambda e: e.tensor_copy(sqrow[:, :], pC[0:16, 0:T1]), deps=[sq_mm])
                tr = None
                for j in range(4):
                    tr = p.emit("tensor", lambda e, j=j: e.matmul(
                        pC[:, 256 + j * 16:256 + (j + 1) * 16],
                        sqrow[:, j * 128:(j + 1) * 128], eye32[0:16, 0:16],
                        start=True, stop=True, is_transpose=True), deps=[e3, tr])
                e4 = kb.v(lambda e, dst_sq=dst_sq: e.tensor_scalar(
                    dst_sq[:, :], pC[:, 256:320], 0.5, LN16, op0=ALU.mult, op1=ALU.add),
                    deps=[tr])
                prev["pCfree"] = e4
                if which == "q":
                    stq = kb.dma("qdma", qT_dram[:, ti * T1:(ti + 1) * T1].rearrange("(kc p) t -> p kc t", p=128),
                                 qTt[:, :].rearrange("p (kc t) -> p kc t", t=T1), deps=[ev[7][1]])
                    stsq = kb.dma("qdma", sqq_dram[:, ti * 64:(ti + 1) * 64], sqcq[:, :], deps=[e4])
                    prev["qTfree"] = stq
            if "v" in skip:
                prev[("xfree", ti % 2)] = p.fence("tensor")
                continue
            evv = []
            for tc in range(4):
                tm1 = kb.v(lambda e, tc=tc: e.memset(
                    ap3(vab, tc * 1040 + 64, [[65, 16], [1, 1]]), 1.0),
                    deps=[prev.get("vabfree")])
                for nh in range(2):
                    bank = rot[(tc * 2 + nh) % 2]
                    pairs = [(xb[:, kc * T1 + tc * 128: kc * T1 + tc * 128 + 128],
                              wvs[:, kc * D + nh * 512: kc * D + nh * 512 + 512]) for kc in range(8)]
                    tmm = kb.mm(ap3(bank, 0, [[1, 512]]), pairs,
                                deps=[t_x] + rotdep[(tc * 2 + nh) % 2])
                    dst = ap3(vab, tc * 1040 + nh * 8 * 65, [[65, 8], [1, 64]])
                    tev = kb.v(lambda e, dst=dst, bank=bank: e.tensor_copy(
                        dst, ap3(bank, 0, [[64, 8], [1, 64]])),
                        deps=[tmm, prev.get("vabfree")])
                    rotdep[(tc * 2 + nh) % 2] = [tev]
                    evv.append(tev)
            t_vaug = evv[-1]
            for tc in (() if "pk" in skip else range(4)):
                prj = []
                for h in range(16):
                    bank = rot[h % 2]
                    lh = kT[(h % 2) * 64:(h % 2) * 64 + 64,
                            (h // 2) * T1 + tc * 128:(h // 2) * T1 + tc * 128 + 128]
                    omx = omT[(h % 2) * 64:(h % 2) * 64 + 64, :]
                    tmm = p.emit("tensor", lambda e, lh=lh, bank=bank, omx=omx: e.matmul(
                        ap3(bank, 0, [[1, 256]]), lh, omx, start=True, stop=True),
                        deps=rotdep[h % 2] + [ev[7][1]])
                    tev = kb.v(lambda e, h=h, bank=bank: e.tensor_scalar(
                        fscr[:, h * 256:(h + 1) * 256], ap3(bank, 0, [[1, 256]]),
                        1.0, None, op0=ALU.mult, op1=ALU.max, accum_out=stb[:, h:h + 1]),
                        deps=[tmm, prev.get("fscrfree")])
                    rotdep[h % 2] = [tev]
                    prj.append(tev)
                tcn = kb.v(lambda e, tc=tc: e.scalar_tensor_tensor(
                    cng[:, :], stb[:, :], -1.0, sqc[:, tc * 16:(tc + 1) * 16],
                    op0=ALU.mult, op1=ALU.subtract), deps=prj + [prev.get("cngfree")])
                texp = None
                for h in range(16):
                    texp = kb.act(fscr[:, h * 256:(h + 1) * 256], fscr[:, h * 256:(h + 1) * 256],
                                  AF.Exp, bias=cng[:, h:h + 1], deps=[tcn])
                prev["cngfree"] = texp
                tpk = kb.v(lambda e: e.tensor_scalar(bscr[:, :16 * 256], fscr[:, :16 * 256],
                                                     1e-6, None, op0=ALU.add), deps=[texp])
                prev["fscrfree"] = tpk
                for h in ([] if "kv" in skip else range(16)):
                    for j in range(2):
                        s_ = 2 * h + j
                        bank = pkv[s_ // 7]
                        col = (s_ % 7) * 65
                        stflag = (ti == 0 and tc == 0 and (s_ % 7) == 0)
                        last = (ti == na - 1 and tc == 3)
                        kv_last = p.emit("tensor", lambda e, h=h, j=j, bank=bank, col=col,
                                         stflag=stflag, last=last, tc=tc: e.matmul(
                            ap3(bank, col, [[1, 65]]),
                            bscr[:, h * 256 + j * 128: h * 256 + j * 128 + 128],
                            vab[:, tc * 1040 + h * 65: tc * 1040 + h * 65 + 65],
                            start=stflag, stop=last, skip_group_check=True),
                            deps=[tpk, t_vaug])
            prev[("xfree", ti % 2)] = p.fence("tensor")
            prev["vabfree"] = kv_last
            prev["kTfree"] = kv_last
            prev["rA"], prev["rB"] = rotdep[0][0], rotdep[1][0]
        if "kv" in skip or "pk" in skip or "v" in skip:
            kv_last = p.fence("tensor")
        evk = []
        for b_ in range(5):
            ncols = 455 if b_ < 4 else 260
            evk.append(kb.v(lambda e, b_=b_, ncols=ncols: e.tensor_copy(
                kv_sb[:, b_ * 455:b_ * 455 + ncols],
                ap3(pkv[b_], 0, [[1, ncols]])), deps=[kv_last]))
        t_kv = kb.v(lambda e: e.tensor_copy(kv_bf[:, 0:2080], kv_sb[:, 0:2080]), deps=evk)
        if dev:
            dtap("kv", kv_sb[:, 0:2048], 2048, [t_kv])

    af = [p.fence(s_) for s_ in Prog.STREAMS]

    def kvs(h, j):
        s_ = 2 * h + j
        return kv_bf[:, (s_ // 7) * 455 + (s_ % 7) * 65: (s_ // 7) * 455 + (s_ % 7) * 65 + 65]

    es_b = contextlib.ExitStack()
    with es_b:
        _e = es_b.enter_context
        qA = _e(nc.psum_tensor("qA", [128, 512], F32))
        qB = _e(nc.psum_tensor("qB", [128, 512], F32))
        qX = _e(nc.psum_tensor("qX", [128, 1024], BF16))
        pH0 = _e(nc.psum_tensor("pH0", [128, 512], F32))
        pH1 = _e(nc.psum_tensor("pH1", [128, 512], F32))
        pY0 = _e(nc.psum_tensor("pY0", [128, 512], F32))
        pY1 = _e(nc.psum_tensor("pY1", [128, 512], F32))
        wos = _e(nc.sbuf_tensor("wos", [128, 8 * D], BF16))
        xnT32 = _e(nc.sbuf_tensor("xnT32", [128, 8 * T1], F32))
        pqT = _e(nc.sbuf_tensor("pqT", [128, 16 * 256], BF16))
        attn = _e(nc.sbuf_tensor("attn", [128, 4 * 1024], BF16))
        y1 = _e(nc.sbuf_tensor("y1", [128, 8 * T1], F32))
        xa = _e(nc.sbuf_tensor("xa", [128, 8 * T1], F32))
        num_sb = _e(nc.sbuf_tensor("num_sb", [128, 16 * 65], F32))
        rd16 = _e(nc.sbuf_tensor("rd16", [128, 16], F32))
        w1g0 = _e(nc.sbuf_tensor("w1g0", [128, 8 * 512], BF16))
        w1g1 = _e(nc.sbuf_tensor("w1g1", [128, 8 * 512], BF16))
        w2m0 = _e(nc.sbuf_tensor("w2m0", [128, 32 * 128], BF16))
        w2m1 = _e(nc.sbuf_tensor("w2m1", [128, 32 * 128], BF16))
        h1_sb = _e(nc.sbuf_tensor("h1_sb", [128, 32 * T1], BF16))
        rsb = _e(nc.sbuf_tensor("rsb", [1, 5 * T1], F32))
        w1g = [w1g0, w1g1]
        w2m = [w2m0, w2m1]
        pH = [pH0, pH1]
        pY = [pY0, pY1]
        t_wo = kb.dma("gpsimd", wos[:, :], kb.dram["wo"][:, :].rearrange("(kc p) e -> p kc e", p=128),
                      deps=af)
        prev = {"rA": None, "rB": None}
        rows_mean = rsb[0:1, 0:T1]
        rows_sq = rsb[0:1, T1:2 * T1]
        rows_var = rsb[0:1, 2 * T1:3 * T1]
        rows_rstd = rsb[0:1, 3 * T1:4 * T1]
        rows_mr = rsb[0:1, 4 * T1:5 * T1]

        def ln_T(yf, ybf, sqbf, gname, bname, outf, outbf, deps_in, eps=EPS1):
            tb = kb.v(lambda e: e.tensor_copy(ybf[:, :], yf[:, :]), deps=deps_in)
            tq = kb.act(sqbf[:, :], yf[:, :], AF.Square, deps=deps_in)
            m1 = None
            for mc in range(8):
                m1 = p.emit("tensor", lambda e, mc=mc: e.matmul(
                    qA[0:1, 0:T1], onesb[:, :], ybf[:, mc * T1:(mc + 1) * T1],
                    start=(mc == 0), stop=(mc == 7)), deps=[tb, prev.get("rA")])
            m2 = None
            for mc in range(8):
                m2 = p.emit("tensor", lambda e, mc=mc: e.matmul(
                    qB[0:1, 0:T1], onesb[:, :], sqbf[:, mc * T1:(mc + 1) * T1],
                    start=(mc == 0), stop=(mc == 7)), deps=[tq, prev.get("rB")])
            r1 = kb.v(lambda e: e.tensor_scalar(rows_mean, qA[0:1, 0:T1], 1.0 / D, None,
                                                op0=ALU.mult), deps=[m1])
            r2 = kb.v(lambda e: e.tensor_scalar(rows_sq, qB[0:1, 0:T1], 1.0 / D, None,
                                                op0=ALU.mult), deps=[m2])
            r3 = kb.v(lambda e: e.tensor_tensor(rows_var, rows_mean, rows_mean, op=ALU.mult),
                      deps=[r1])
            r4 = kb.v(lambda e: e.tensor_tensor(rows_var, rows_sq, rows_var, op=ALU.subtract),
                      deps=[r2, r3])
            r5 = kb.v(lambda e, eps=eps: e.tensor_scalar(rows_var, rows_var, 1.0, eps,
                                                         op0=ALU.mult, op1=ALU.add), deps=[r4])
            r6 = kb.v(lambda e: e.reciprocal(rows_var, rows_var), deps=[r5])
            r7 = kb.act(rows_rstd, rows_var, AF.Sqrt, deps=[r6])
            r8 = kb.v(lambda e: e.tensor_tensor(rows_mr, rows_mean, rows_rstd, op=ALU.mult),
                      deps=[r7])
            b1_ = p.emit("tensor", lambda e: e.matmul(qA[:, 0:T1], onesf[:, :],
                                                      rows_rstd, start=True, stop=True),
                         deps=[r7, r1])
            b2_ = p.emit("tensor", lambda e: e.matmul(qB[:, 0:T1], onesf[:, :],
                                                      rows_mr, start=True, stop=True),
                         deps=[r8, r2])
            tf = None
            for mc in range(8):
                c1 = kb.v(lambda e, mc=mc: e.tensor_tensor(
                    fscr[:, mc * T1:(mc + 1) * T1], yf[:, mc * T1:(mc + 1) * T1],
                    qA[:, 0:T1], op=ALU.mult), deps=[b1_])
                c2 = kb.v(lambda e, mc=mc: e.tensor_tensor(
                    fscr[:, mc * T1:(mc + 1) * T1], fscr[:, mc * T1:(mc + 1) * T1],
                    qB[:, 0:T1], op=ALU.subtract), deps=[b2_])
                c3 = kb.v(lambda e, mc=mc: e.tensor_scalar(
                    outf[:, mc * T1:(mc + 1) * T1], fscr[:, mc * T1:(mc + 1) * T1],
                    lnv[gname][:, mc:mc + 1], lnv[bname][:, mc:mc + 1],
                    op0=ALU.mult, op1=ALU.add), deps=[c2])
                tf = kb.v(lambda e, mc=mc: e.tensor_copy(
                    outbf[:, mc * T1:(mc + 1) * T1], outf[:, mc * T1:(mc + 1) * T1]),
                    deps=[c3])
            prev["rA"] = tf
            prev["rB"] = tf
            return tf

        st_s = t_kv
        for ti in range(nb):
            t_x32 = kb.dma("sync", xnT32[:, :].rearrange("p (kc t) -> p kc t", t=T1),
                           xn32_d[:, ti * T1:(ti + 1) * T1].rearrange("(kc p) t -> p kc t", p=128),
                           deps=[prev.get("x32free")] + (af if ti == 0 else []))
            t_q = kb.dma("sync", qTt[:, :].rearrange("p (kc t) -> p kc t", t=T1),
                         qT_dram[:, ti * T1:(ti + 1) * T1].rearrange("(kc p) t -> p kc t", p=128),
                         deps=[prev.get("qfree")] + (af if ti == 0 else []))
            t_sq = kb.dma("sync", sqcq[:, :], sqq_dram[:, ti * 64:(ti + 1) * 64],
                          deps=[prev.get("sqfree")] + (af if ti == 0 else []))
            rot = [qA, qB]
            rotdep = {0: [prev.get("rA")], 1: [prev.get("rB")]}
            for tc in range(4):
                prj = []
                for h in range(16):
                    bank = rot[h % 2]
                    lh = qTt[(h % 2) * 64:(h % 2) * 64 + 64,
                             (h // 2) * T1 + tc * 128:(h // 2) * T1 + tc * 128 + 128]
                    omx = omT[(h % 2) * 64:(h % 2) * 64 + 64, :]
                    tmm = p.emit("tensor", lambda e, lh=lh, bank=bank, omx=omx: e.matmul(
                        ap3(bank, 0, [[1, 256]]), lh, omx, start=True, stop=True),
                        deps=rotdep[h % 2] + [t_q])
                    tev = kb.v(lambda e, h=h, bank=bank: e.tensor_scalar(
                        fscr[:, h * 256:(h + 1) * 256], ap3(bank, 0, [[1, 256]]),
                        1.0, None, op0=ALU.mult, op1=ALU.max, accum_out=stb[:, h:h + 1]),
                        deps=[tmm, prev.get("fscrfree"), t_sq])
                    rotdep[h % 2] = [tev]
                    prj.append(tev)
                tcn = kb.v(lambda e, tc=tc: e.scalar_tensor_tensor(
                    cng[:, :], stb[:, :], -1.0, sqcq[:, tc * 16:(tc + 1) * 16],
                    op0=ALU.mult, op1=ALU.subtract), deps=prj + [prev.get("cngfree")])
                texp = None
                for h in range(16):
                    texp = kb.act(fscr[:, h * 256:(h + 1) * 256], fscr[:, h * 256:(h + 1) * 256],
                                  AF.Exp, bias=cng[:, h:h + 1], deps=[tcn])
                prev["cngfree"] = texp
                tpq = kb.v(lambda e: e.tensor_scalar(bscr[:, :16 * 256], fscr[:, :16 * 256],
                                                     1e-6, None, op0=ALU.add), deps=[texp])
                prev["fscrfree"] = tpq
                tevp = None
                for bq in range(4):
                    ttr = None
                    for sl in range(8):
                        idx = bq * 8 + sl
                        h_, j_ = idx // 2, idx % 2
                        src = bscr[:, h_ * 256 + j_ * 128: h_ * 256 + j_ * 128 + 128]
                        ttr = p.emit("tensor", lambda e, src=src, sl=sl: e.matmul(
                            qX[:, sl * 128:(sl + 1) * 128], src, eye16[:, :],
                            start=True, stop=True, is_transpose=True),
                            deps=[tpq, prev.get("qXfree"), ttr])
                    tevp = kb.v(lambda e, bq=bq: e.tensor_copy(
                        pqT[:, bq * 1024:(bq + 1) * 1024], qX[:, 0:1024]), deps=[ttr])
                    prev["qXfree"] = tevp
                nev = []
                for h in range(16):
                    bank = rot[h % 2]
                    tmm = p.emit("tensor", lambda e, h=h, bank=bank: e.matmul(
                        ap3(bank, 0, [[1, 65]]), pqT[:, (2 * h) * 128:(2 * h) * 128 + 128],
                        kvs(h, 0), start=True, stop=False, skip_group_check=True),
                        deps=[tevp, t_kv] + rotdep[h % 2])
                    tmm = p.emit("tensor", lambda e, h=h, bank=bank: e.matmul(
                        ap3(bank, 0, [[1, 65]]), pqT[:, (2 * h + 1) * 128:(2 * h + 1) * 128 + 128],
                        kvs(h, 1), start=False, stop=True, skip_group_check=True),
                        deps=[])
                    tev = kb.v(lambda e, h=h, bank=bank: e.tensor_copy(
                        num_sb[:, h * 65:(h + 1) * 65], ap3(bank, 0, [[1, 65]])),
                        deps=[tmm])
                    rotdep[h % 2] = [tev]
                    nev.append(tev)
                trd = kb.v(lambda e: e.reciprocal(rd16[:, :], ap3(num_sb, 64, [[65, 16]])),
                           deps=nev)
                for h in range(16):
                    kb.v(lambda e, h=h, tc=tc: e.tensor_scalar(
                        attn[:, tc * 1024 + h * 64: tc * 1024 + h * 64 + 64],
                        num_sb[:, h * 65:h * 65 + 64], rd16[:, h:h + 1], None,
                        op0=ALU.mult), deps=[trd])
            t_attn = p.fence("vector")
            tat = None
            for tc in range(4):
                for dq in range(2):
                    ttr = None
                    for jj in range(4):
                        asrc = attn[:, tc * 1024 + (dq * 4 + jj) * 128: tc * 1024 + (dq * 4 + jj) * 128 + 128]
                        ttr = p.emit("tensor", lambda e, asrc=asrc, jj=jj: e.matmul(
                            qX[:, jj * 128:(jj + 1) * 128], asrc, eye16[:, :],
                            start=True, stop=True, is_transpose=True),
                            deps=[t_attn, prev.get("qXfree"), ttr])
                    dstp = ap3(bsc2, (dq * 4) * T1 + tc * 128, [[T1, 4], [1, 128]])
                    tat = kb.v(lambda e, dstp=dstp: e.tensor_copy(dstp, ap3(qX, 0, [[128, 4], [1, 128]])),
                               deps=[ttr])
                    prev["qXfree"] = tat
            wv_ev = []
            for mc in range(8):
                bank = rot[mc % 2]
                pairs = [(wos[:, kc * D + mc * 128: kc * D + mc * 128 + 128],
                          bsc2[:, kc * T1:(kc + 1) * T1]) for kc in range(8)]
                tmm = kb.mm(ap3(bank, 0, [[1, T1]]), pairs, deps=[tat, t_wo] + rotdep[mc % 2])
                tev = kb.v(lambda e, mc=mc, bank=bank: e.tensor_tensor(
                    y1[:, mc * T1:(mc + 1) * T1], ap3(bank, 0, [[1, T1]]),
                    xnT32[:, mc * T1:(mc + 1) * T1], op=ALU.add), deps=[tmm, t_x32])
                rotdep[mc % 2] = [tev]
                wv_ev.append(tev)
            prev["rA"], prev["rB"] = rotdep[0][0], rotdep[1][0]
            prev["x32free"] = wv_ev[-1]
            t_xa = ln_T(y1, attn[:, 0:8 * T1] if False else bscr, pqT[:, 0:8 * T1] if False else bsc2,
                        "ln1g_c", "ln1b_c", xa, bscr, wv_ev)
            h_ev = []
            for m in range(32):
                g_ = m // 4
                if m % 4 == 0:
                    w1t = kb.dma("gpsimd", w1g[g_ % 2][:, :].rearrange("p (kc c) -> p kc c", c=512),
                                 w1_d[:, g_ * 512:(g_ + 1) * 512].rearrange("(kc p) c -> p kc c", p=128),
                                 deps=[prev.get(("w1gfree", g_ % 2))])
                    prev[("w1gld", g_ % 2)] = w1t
                mloc = m % 4
                wb1 = w1g[g_ % 2]
                ph = pH[m % 2]
                pairs = [(wb1[:, kc * 512 + mloc * 128: kc * 512 + mloc * 128 + 128],
                          bscr[:, kc * T1:(kc + 1) * T1]) for kc in range(8)]
                tmm = kb.mm(ap3(ph, 0, [[1, T1]]), pairs,
                            deps=[t_xa, prev.get(("w1gld", g_ % 2)), prev.get(("hfree", m % 2))])
                the = kb.act(h1_sb[:, m * T1:(m + 1) * T1], ap3(ph, 0, [[1, T1]]), AF.Gelu,
                             bias=b1c[:, m:m + 1], deps=[tmm, prev.get("h1sbfree")])
                prev[("hfree", m % 2)] = the
                h_ev.append(the)
                if m % 4 == 3:
                    prev[("w1gfree", g_ % 2)] = p.fence("tensor")
            y2ev = []
            for mp in range(8):
                w2t = kb.dma("gpsimd", w2m[mp % 2][:, :].rearrange("p (mh c) -> p mh c", c=128),
                             w2_d[:, mp * 128:(mp + 1) * 128].rearrange("(mh p) c -> p mh c", p=128),
                             deps=[prev.get(("w2mfree", mp % 2))])
                py = pY[mp % 2]
                w2b = w2m[mp % 2]

                def f2(e, py=py, w2b=w2b):
                    r = None
                    for m_ in range(32):
                        r = e.matmul(ap3(py, 0, [[1, T1]]),
                                     w2b[:, m_ * 128:(m_ + 1) * 128],
                                     h1_sb[:, m_ * T1:(m_ + 1) * T1],
                                     start=(m_ == 0), stop=(m_ == 31))
                    return r
                tmm = p.emit("tensor", f2, deps=[h_ev[-1], w2t, prev.get(("pYfree", mp % 2))])
                tev = kb.v(lambda e, mp=mp, py=py: e.scalar_tensor_tensor(
                    y1[:, mp * T1:(mp + 1) * T1], ap3(py, 0, [[1, T1]]),
                    lnv["b2_c"][:, mp:mp + 1], xa[:, mp * T1:(mp + 1) * T1],
                    op0=ALU.add, op1=ALU.add), deps=[tmm])
                prev[("pYfree", mp % 2)] = tev
                prev[("w2mfree", mp % 2)] = tev
                y2ev.append(tev)
            prev["h1sbfree"] = p.fence("tensor")
            t_s = ln_T(y1, bscr, bsc2, "ln2g_c", "ln2b_c", xa, bscr, y2ev)
            st_s = kb.dma("qdma", s_d[:, ti * T1:(ti + 1) * T1].rearrange("(kc p) t -> p kc t", p=128),
                          xa[:, :].rearrange("p (kc t) -> p kc t", t=T1), deps=[t_s])
            if dev and ti == 0:
                dtap("attn0", attn[:, :], 4096, [t_attn], BF16)
                dtap("y2_0", y1[:, :], 4096, y2ev + [st_s])
            prev["qfree"] = st_s
            prev["sqfree"] = st_s
            prev["sfree"] = st_s
    es2.close()
    return st_s


def _conv_w_c(w, nch):
    K = w.shape[1]
    return np.ascontiguousarray(w.reshape(nch // 128, 128, K).transpose(1, 0, 2).reshape(128, -1)).astype(np.float32)


def _vec_c(b):
    return np.ascontiguousarray(b.reshape(-1, 128).T).astype(np.float32)


def make_l0_inputs(inputs, core):
    b, h = core // 2, core % 2
    x = np.asarray(inputs["L0_attn_u"], np.float32)
    t0 = h * NTOK
    if h == 0:
        xin = np.concatenate([np.zeros((HALO, D), np.float32), x[b, 0:NTOK]], axis=0)
    else:
        xin = x[b, t0 - HALO:t0 + NTOK]
    m = {
        "x": np.ascontiguousarray(xin),
        "w_in": np.asarray(inputs["L0_attn_W_in"], np.float32),
        "w_out": np.asarray(inputs["L0_attn_W_out"], np.float32),
        "fw": np.asarray(inputs["L0_ffn_weight"], np.float32),
        "b_in_c": _vec_c(np.asarray(inputs["L0_attn_b_in"], np.float32)),
        "short_w_c": _conv_w_c(np.asarray(inputs["L0_attn_short_w"], np.float32), D3),
        "short_b_c": _vec_c(np.asarray(inputs["L0_attn_short_b"], np.float32)),
        "inner_w_c": _conv_w_c(np.asarray(inputs["L0_attn_inner_w"], np.float32), D),
        "inner_b_c": _vec_c(np.asarray(inputs["L0_attn_inner_b"], np.float32)),
        "b_out_c": _vec_c(np.asarray(inputs["L0_attn_b_out"], np.float32)),
        "fb_c": _vec_c(np.asarray(inputs["L0_ffn_bias"], np.float32)),
        "eye32": np.eye(128, dtype=np.float32),
        "eye16": np.eye(128).astype(ml_dtypes.bfloat16),
        "hmask": np.full((128, 1), 0.0 if h == 0 else 1.0, np.float32),
    }
    return m


def build_moe(kb, dev=False):
    nc, p = kb.nc, kb.p
    fused = kb.cfg.get("fused", False)
    es3 = contextlib.ExitStack()
    es3sb = lambda n_, sh, dt_: es3.enter_context(nc.sbuf_tensor('l3_' + n_, list(sh), dt_))
    if fused:
        xn_d = kb.dram["xn1t16_d"]
        xn32_d = kb.dram["xn1t32_d"]
        x2_d = kb.dram["x2_d"]
        s_d = kb.dram["s_d"]
    else:
        xn_d = kb.dt("xn16", [D, NTOK], BF16, "ExternalInput")
        xn32_d = kb.dt("xn32m", [D, NTOK], F32, "ExternalInput")
        x2_d = kb.dt("x2", [NTOK, D], F32, "ExternalInput")
        s_d = kb.dt("s_in", [D, NTOK], F32, "ExternalInput")
    wgT_d = kb.dt("wgT", [D, 8], F32, "ExternalInput")
    w1_d = kb.dt("mw1", [8 * D, 4 * D], F32, "ExternalInput")
    w2_d = kb.dt("mw2", [8 * 4 * D, D], F32, "ExternalInput")
    b1_d = kb.dt("mb1_c", [128, 8 * 32], F32, "ExternalInput")
    b2_d = kb.dt("mb2e", [8, D], F32, "ExternalInput")
    eye32_d = kb.dt("eye32", [128, 128], F32, "ExternalInput")
    x3_d = kb.dt("x3", [NTOK, D], F32, "ExternalOutput")
    acc_d = kb.dt("acc_sc", [NTOK, D], F32)

    wgs = es3sb("wgs", [128, 8 * 8], F32)
    b2sb = es3sb("b2sb", [8, D], F32)
    b1c = es3sb("mb1c_s", [128, 8 * 32], F32)
    eye32 = es3sb("eye32_s", [128, 128], F32)
    cw = es3sb("cw", [128, 32 * 8], F32)
    gts = es3sb("gts", [128, 32], F32)
    e8 = es3sb("e8", [128, 8], F32)
    pm8 = es3sb("pm8", [128, 8], F32)
    m81 = es3sb("m81", [128, 8], F32)
    m82 = es3sb("m82", [128, 8], F32)
    gcol = es3sb("gcol", [128, 4], F32)
    ystage = es3sb("ystage", [128, 4 * D], F32)
    pdep3 = list(kb.cfg.get("phase_dep_l3", []))
    kb.dma("sync", wgs[:, :], wgT_d[:, :].rearrange("(kc p) e -> p kc e", p=128), deps=pdep3)
    kb.dma("sync", b2sb[:, :], b2_d[:, :])
    kb.dma("sync", b1c[:, :], b1_d[:, :])
    kb.dma("sync", eye32[:, :], eye32_d[:, :])
    t_w = [p.fence("gdma"), p.fence("sdma")] + pdep3
    accdma = {}

    with nc.psum_tensor("mH0", [128, 512], F32) as mH0, \
         nc.psum_tensor("mH1", [128, 512], F32) as mH1, \
         nc.psum_tensor("mY0", [128, 512], F32) as mY0, \
         nc.psum_tensor("mY1", [128, 512], F32) as mY1, \
         nc.psum_tensor("mY2", [128, 512], F32) as mY2, \
         nc.psum_tensor("mY3", [128, 512], F32) as mY3, \
         nc.sbuf_tensor("xg0", [128, 8 * 512], BF16) as xg0, \
         nc.sbuf_tensor("xg1", [128, 8 * 512], BF16) as xg1:
        pH = [mH0, mH1]
        pY = [mY0, mY1, mY2, mY3]
        xg = [xg0, xg1]
        prev = {}
        es_g = contextlib.ExitStack()
        cwT = es_g.enter_context(nc.sbuf_tensor("cwT", [8, 32 * 128], F32))
        grow = es_g.enter_context(nc.sbuf_tensor("grow", [8, 512], F32))
        xg32 = es_g.enter_context(nc.sbuf_tensor("xg32", [128, 8 * 512], F32))
        for tg in range(8):
            t_x = kb.dma("sync", xg32[:, :].rearrange("p (kc t) -> p kc t", t=512),
                         xn32_d[:, tg * 512:(tg + 1) * 512].rearrange("(kc p) t -> p kc t", p=128),
                         deps=t_w + [prev.get("xg32free")])
            gmm = None
            for kc in range(8):
                gmm = p.emit("tensor", lambda e, kc=kc: e.matmul(
                    mH0[0:8, 0:512], wgs[:, kc * 8:(kc + 1) * 8],
                    xg32[:, kc * 512:(kc + 1) * 512], start=(kc == 0), stop=(kc == 7)),
                    deps=[t_x, prev.get("growfree")])
            tg1 = kb.v(lambda e: e.tensor_copy(grow[:, :], mH0[0:8, 0:512]), deps=[gmm])
            prev["growfree"] = tg1
            ttr = None
            for j in range(4):
                ttr = p.emit("tensor", lambda e, j=j: e.matmul(
                    mH1[:, j * 8:(j + 1) * 8], grow[:, j * 128:(j + 1) * 128],
                    eye32[0:8, 0:8], start=True, stop=True, is_transpose=True),
                    deps=[tg1, prev.get("mh1free"), ttr])
            tg2 = kb.v(lambda e: e.tensor_copy(gts[:, :], mH1[:, 0:32]), deps=[ttr])
            prev["mh1free"] = tg2
            for c in range(4):
                gch = tg * 4 + c
                sl = gts[:, c * 8:(c + 1) * 8]
                cws = cw[:, gch * 8:(gch + 1) * 8]
                t1 = kb.v(lambda e, sl=sl: e.tensor_reduce(gcol[:, 0:1], sl, axis=AX.X,
                                                           op=ALU.max, negate=True), deps=[tg2])
                t2 = kb.act(e8[:, :], sl, AF.Exp, bias=gcol[:, 0:1], deps=[t1])
                t3 = kb.v(lambda e: e.tensor_reduce(gcol[:, 1:2], e8[:, :], axis=AX.X,
                                                    op=ALU.add), deps=[t2])
                t4 = kb.v(lambda e: e.reciprocal(gcol[:, 1:2], gcol[:, 1:2]), deps=[t3])
                t5 = kb.v(lambda e, cws=cws: e.tensor_scalar(cws, e8[:, :], gcol[:, 1:2],
                                                             None, op0=ALU.mult), deps=[t4])
                t6 = kb.v(lambda e, cws=cws: e.tensor_reduce(gcol[:, 2:3], cws, axis=AX.X,
                                                             op=ALU.max), deps=[t5])
                t7 = kb.v(lambda e, cws=cws: e.tensor_scalar(m81[:, :], cws, gcol[:, 2:3],
                                                             None, op0=ALU.is_ge), deps=[t6])
                t8 = kb.v(lambda e: e.tensor_scalar(m82[:, :], m81[:, :], -1.0, 1.0,
                                                    op0=ALU.mult, op1=ALU.add), deps=[t7])
                t9 = kb.v(lambda e, cws=cws: e.tensor_tensor(pm8[:, :], cws, m82[:, :],
                                                             op=ALU.mult), deps=[t8])
                t10 = kb.v(lambda e: e.tensor_reduce(gcol[:, 3:4], pm8[:, :], axis=AX.X,
                                                     op=ALU.max), deps=[t9])
                t11 = kb.v(lambda e: e.tensor_scalar(m82[:, :], pm8[:, :], gcol[:, 3:4],
                                                     None, op0=ALU.is_ge), deps=[t10])
                t12 = kb.v(lambda e: e.tensor_tensor(m81[:, :], m81[:, :], m82[:, :],
                                                     op=ALU.add), deps=[t11])
                t13 = kb.v(lambda e, cws=cws: e.tensor_tensor(cws, cws, m81[:, :],
                                                              op=ALU.mult), deps=[t12])
                ttr2 = p.emit("tensor", lambda e, cws=cws: e.matmul(
                    mH1[0:8, 40:168], cws, eye32[:, :], start=True, stop=True,
                    is_transpose=True), deps=[t13, prev.get("cwtfree")])
                tcv = kb.v(lambda e, gch=gch: e.tensor_copy(
                    cwT[:, gch * 128:(gch + 1) * 128], mH1[0:8, 40:168]), deps=[ttr2])
                prev["cwtfree"] = tcv
                bmm = p.emit("tensor", lambda e, gch=gch, c=c: e.matmul(
                    pY[c % 2][:, 0:512], cwT[0:8, gch * 128:(gch + 1) * 128],
                    b2sb[:, 0:512], start=True, stop=True),
                    deps=[tcv, prev.get(("pyfree", c % 2))])
                bmm2 = p.emit("tensor", lambda e, gch=gch, c=c: e.matmul(
                    pY[2 + c % 2][:, 0:512], cwT[0:8, gch * 128:(gch + 1) * 128],
                    b2sb[:, 512:1024], start=True, stop=True),
                    deps=[])
                ev1 = kb.v(lambda e, c=c: e.tensor_copy(ystage[:, c * D:c * D + 512],
                                                        pY[c % 2][:, 0:512]), deps=[bmm, prev.get("ystagefree")])
                ev2 = kb.v(lambda e, c=c: e.tensor_copy(ystage[:, c * D + 512:(c + 1) * D],
                                                        pY[2 + c % 2][:, 0:512]), deps=[bmm2])
                prev[("pyfree", c % 2)] = ev1
                prev[("pyfree", 2 + c % 2)] = ev2
                prev["mh1free"] = tcv
            tini = kb.dma("gpsimd", acc_d[tg * 512:(tg + 1) * 512, :].rearrange("(c p) d -> p c d", p=128),
                          ystage[:, :].rearrange("p (c d) -> p c d", d=D),
                          deps=[prev[("pyfree", 0)], prev[("pyfree", 1)],
                                prev[("pyfree", 2)], prev[("pyfree", 3)]])
            accdma[tg] = tini
            prev["ystagefree"] = tini
            prev["xg32free"] = p.fence("tensor")
        prev[("hfree", 0)] = prev[("hfree", 1)] = p.fence("vector")
        if dev:
            acc0_d = kb.dt("acc0_dev", [NTOK, D], F32, "ExternalOutput")
            cw_d = kb.dt("cw_dev", [128, 256], F32, "ExternalOutput")
            kb.dma("qdma", cw_d[:, :], cw[:, :], deps=[p.fence("vector")])
            for tg in range(8):
                snap = kb.dma("qdma", acc0_d[tg * 512:(tg + 1) * 512, :],
                              acc_d[tg * 512:(tg + 1) * 512, :], deps=[accdma[tg]])
                accdma[tg] = snap
        es_g.close()
        with nc.sbuf_tensor("w1e", [128, 8 * 4 * D], BF16) as w1e, \
             nc.sbuf_tensor("w2e", [128, 32 * D], BF16) as w2e, \
             nc.sbuf_tensor("h1m", [128, 32 * 512], BF16) as h1m:
            for ex in range(8):
                pef = prev.get("expf")
                tw1 = [kb.dma("gpsimd",
                              ap3(w1e, half * 4 * 4 * D, [[4 * D, 4], [1, 4 * D]]),
                              w1_d[ex * D + half * 512:ex * D + (half + 1) * 512, :]
                              .rearrange("(kc p) m -> p kc m", p=128),
                              deps=[pef]) for half in range(2)]
                tw2 = [kb.dma("gpsimd",
                              ap3(w2e, half * 16 * D, [[D, 16], [1, D]]),
                              w2_d[ex * 4 * D + half * 2048:ex * 4 * D + (half + 1) * 2048, :]
                              .rearrange("(kc p) d -> p kc d", p=128),
                              deps=[pef]) for half in range(2)]
                for t_ in range(8):
                    xb = xg[t_ % 2]
                    t_x = kb.dma("sync", xb[:, :].rearrange("p (kc t) -> p kc t", t=512),
                                 xn_d[:, t_ * 512:(t_ + 1) * 512].rearrange("(kc p) t -> p kc t", p=128),
                                 deps=[prev.get(("xfree", t_ % 2))])
                    hev = []
                    for m in range(32):
                        ph = pH[m % 2]
                        pairs = [(w1e[:, kc * 4 * D + m * 128: kc * 4 * D + m * 128 + 128],
                                  xb[:, kc * 512:(kc + 1) * 512]) for kc in range(8)]
                        tmm = kb.mm(ap3(ph, 0, [[1, 512]]), pairs,
                                    deps=[t_x, tw1[0], tw1[1], prev.get(("hfree", m % 2))])
                        the = kb.act(h1m[:, m * 512:(m + 1) * 512], ap3(ph, 0, [[1, 512]]),
                                     AF.Gelu, bias=b1c[:, ex * 32 + m:ex * 32 + m + 1],
                                     deps=[tmm, prev.get("h1mfree")])
                        prev[("hfree", m % 2)] = the
                        hev.append(the)
                    for tc in range(4):
                        for dh in range(2):
                            py = pY[2 * (tc % 2) + dh]

                            def f2(e, tc=tc, dh=dh, py=py):
                                r = None
                                for kc in range(32):
                                    r = e.matmul(ap3(py, 0, [[1, 512]]),
                                                 h1m[:, kc * 512 + tc * 128: kc * 512 + tc * 128 + 128],
                                                 w2e[:, kc * D + dh * 512: kc * D + dh * 512 + 512],
                                                 start=(kc == 0), stop=(kc == 31))
                                return r
                            tmm = p.emit("tensor", f2,
                                         deps=[hev[-1], tw2[0], tw2[1], prev.get(("pyfree", 2 * (tc % 2) + dh))])
                            tev = kb.v(lambda e, tc=tc, dh=dh, py=py, t_=t_, ex=ex: e.tensor_scalar(
                                ystage[:, tc * D + dh * 512: tc * D + (dh + 1) * 512],
                                ap3(py, 0, [[1, 512]]),
                                cw[:, (t_ * 4 + tc) * 8 + ex:(t_ * 4 + tc) * 8 + ex + 1],
                                None, op0=ALU.mult),
                                deps=[tmm, prev.get("ystagefree")])
                            prev[("pyfree", 2 * (tc % 2) + dh)] = tev
                    if dev and ex == 0 and t_ == 0:
                        ys_d = kb.dt("ys00_dev", [128, 4 * D], F32, "ExternalOutput")
                        h1_d = kb.dt("h1m_dev", [128, 4 * 512], F32, "ExternalOutput")
                        kb.dma("qdma", ys_d[:, :], ystage[:, :],
                               deps=[prev[("pyfree", 0)], prev[("pyfree", 1)],
                                     prev[("pyfree", 2)], prev[("pyfree", 3)]])
                        tqh = kb.v(lambda e: e.tensor_copy(ystage[:, 0:0 + 2048] if False else ystage[:, 0:2048], h1m[:, 0:2048]), deps=[hev[-1]]) if False else None
                        h1tap = kb.dt("h1tap", [128, 2048], BF16, "ExternalOutput")
                        kb.dma("qdma", h1tap[:, :], h1m[:, 0:2048], deps=[hev[-1]])
                    tacc = kb.dma("gpsimd", acc_d[t_ * 512:(t_ + 1) * 512, :].rearrange("(c p) d -> p c d", p=128),
                                  ystage[:, :].rearrange("p (c d) -> p c d", d=D),
                                  deps=[accdma.get(t_), prev[("pyfree", 0)], prev[("pyfree", 1)],
                                        prev[("pyfree", 2)], prev[("pyfree", 3)]] +
                                       ([p.fence("qdma")] if (dev and ex == 0 and t_ == 0) else []), accum=True)
                    accdma[t_] = tacc
                    prev["ystagefree"] = tacc
                    prev["h1mfree"] = p.fence("tensor")
                    prev[("xfree", t_ % 2)] = prev["h1mfree"]
                prev["expf"] = p.fence("tensor")
        with nc.sbuf_tensor("sT", [128, 8 * 512], F32) as sT, \
             nc.sbuf_tensor("stok", [128, 4 * D], F32) as stok, \
             nc.sbuf_tensor("x2t", [128, 4 * D], F32) as x2t:
            last = None
            for t_ in range(8):
                t_acc = kb.dma("sync", ystage[:, :].rearrange("p (c d) -> p c d", d=D),
                               acc_d[t_ * 512:(t_ + 1) * 512, :].rearrange("(c p) d -> p c d", p=128),
                               deps=[accdma[t_], prev.get("ystagefree"), prev.get("ystagefree2"), last])
                t_s = kb.dma("sync", sT[:, :].rearrange("p (kc t) -> p kc t", t=512),
                             s_d[:, t_ * 512:(t_ + 1) * 512].rearrange("(kc p) t -> p kc t", p=128),
                             deps=[prev.get("sTfree")])
                t_x2 = kb.dma("sync", x2t[:, :].rearrange("p (c d) -> p c d", d=D),
                              x2_d[t_ * 512:(t_ + 1) * 512, :].rearrange("(c p) d -> p c d", p=128),
                              deps=[prev.get("x2tfree")])
                tg_ = kb.act(ystage[:, :], ystage[:, :], AF.Silu, deps=[t_acc])
                trd = prev.get("ph0free")
                hguard = [prev.get(("hfree", 0)), prev.get(("hfree", 1))]
                for tc in range(4):
                    for dq in range(2):
                        ttr = None
                        for j in range(4):
                            ssrc = sT[:, (dq * 4 + j) * 512 + tc * 128: (dq * 4 + j) * 512 + tc * 128 + 128]
                            ttr = p.emit("tensor", lambda e, ssrc=ssrc, j=j: e.matmul(
                                mH0[:, j * 128:(j + 1) * 128], ssrc, eye32[:, :],
                                start=True, stop=True, is_transpose=True),
                                deps=[t_s, trd, ttr] + hguard)
                        dstp = ap3(stok, tc * D + dq * 4 * 128, [[128, 4], [1, 128]])
                        trd = kb.v(lambda e, dstp=dstp: e.tensor_copy(
                            dstp, ap3(mH0, 0, [[128, 4], [1, 128]])), deps=[ttr])
                prev["ph0free"] = trd
                tm1 = kb.v(lambda e: e.tensor_tensor(ystage[:, :], ystage[:, :], stok[:, :],
                                                     op=ALU.mult), deps=[tg_, trd])
                tm2 = kb.v(lambda e: e.tensor_tensor(x2t[:, :], x2t[:, :], ystage[:, :],
                                                     op=ALU.add), deps=[tm1, t_x2])
                last = kb.dma("qdma", x3_d[t_ * 512:(t_ + 1) * 512, :].rearrange("(c p) d -> p c d", p=128),
                              x2t[:, :].rearrange("p (c d) -> p c d", d=D), deps=[tm2])
                prev["ystagefree2"] = tm2
                prev["sTfree"] = last
                prev["x2tfree"] = last
    es3.close()
    return last



def make_e16f():
    e = np.zeros((128, 128), np.float32)
    for mc in range(8):
        for p_ in range(128):
            e[p_, mc * 16 + 2 * mc + p_ // 64] = 1.0
    return e


def make_l2_inputs(inputs, core, xn1t16_all, xn1t32_all):
    b, h = core // 2, core % 2
    m = {}
    if xn1t16_all[core] is not None:
        own = np.asarray(xn1t16_all[core])
        partner = np.asarray(xn1t16_all[2 * b + (1 - h)])
        m["xnf"] = np.ascontiguousarray(np.concatenate([own, partner], axis=1))
        m["xn32"] = np.ascontiguousarray(np.asarray(xn1t32_all[core]))
    m.update({
        "wq": np.ascontiguousarray(np.asarray(inputs["L1_attn_wq"], np.float32).reshape(D, D)),
        "wk": np.ascontiguousarray(np.asarray(inputs["L1_attn_wk"], np.float32).reshape(D, D)),
        "wv": np.ascontiguousarray(np.asarray(inputs["L1_attn_wv"], np.float32).reshape(D, D)),
        "wo": np.asarray(inputs["L1_attn_wo"], np.float32),
        "omT": np.ascontiguousarray(np.concatenate([np.asarray(inputs["L1_attn_omega"], np.float32).T] * 2, axis=0)),
        "e16f": make_e16f(),
        "ln1g_c": _vec_c(np.asarray(inputs["L1_attn_ln1_g"], np.float32)),
        "ln1b_c": _vec_c(np.asarray(inputs["L1_attn_ln1_b"], np.float32)),
        "ln2g_c": _vec_c(np.asarray(inputs["L1_attn_ln2_g"], np.float32)),
        "ln2b_c": _vec_c(np.asarray(inputs["L1_attn_ln2_b"], np.float32)),
        "b1_c": _vec_c(np.asarray(inputs["L1_attn_b1"], np.float32)),
        "b2_c": _vec_c(np.asarray(inputs["L1_attn_b2"], np.float32)),
        "w1": np.asarray(inputs["L1_attn_w1"], np.float32),
        "w2": np.asarray(inputs["L1_attn_w2"], np.float32),
        "eye16": np.eye(128).astype(ml_dtypes.bfloat16),
        "eye32": np.eye(128, dtype=np.float32),
        "onesb": np.ones((128, 1), ml_dtypes.bfloat16),
        "onesf": np.ones((1, 128), np.float32),
    })
    return m


def build_l2_program(dev=False, na=NT1, nb=8, skip=(), io=False):
    kb = KB({"dev": dev, "skip": set(skip), "io": io})
    with kb.ctx:
        build_l1attn(kb, dev=dev, na=na, nb=nb)
        kb.p.finalize(final_waits=[kb.p.fence(e) for e in Prog.ENGINES])
    return kb


def make_l3_inputs(inputs, core, xn1t16_all, x2_all, s_all, x32m_all):
    b1 = np.asarray(inputs["L1_ffn_b1"], np.float32)
    m = {}
    if xn1t16_all[core] is not None:
        m["xn16"] = np.ascontiguousarray(np.asarray(xn1t16_all[core]))
        m["xn32m"] = np.ascontiguousarray(np.asarray(x32m_all[core], np.float32))
        m["x2"] = np.ascontiguousarray(np.asarray(x2_all[core], np.float32))
        m["s_in"] = np.ascontiguousarray(np.asarray(s_all[core], np.float32))
    m.update({
        "wgT": np.ascontiguousarray(np.asarray(inputs["L1_ffn_w_gate"], np.float32).T),
        "mw1": np.ascontiguousarray(np.asarray(inputs["L1_ffn_W1"], np.float32).reshape(8 * D, 4 * D)),
        "mw2": np.ascontiguousarray(np.asarray(inputs["L1_ffn_W2"], np.float32).reshape(8 * 4 * D, D)),
        "mb1_c": np.concatenate([_vec_c(b1[e]) for e in range(8)], axis=1),
        "mb2e": np.asarray(inputs["L1_ffn_b2"], np.float32),
        "eye32": np.eye(128, dtype=np.float32),
    })
    return m


def build_l3_program(dev=False):
    kb = KB({"dev": dev})
    with kb.ctx:
        build_moe(kb, dev=dev)
        kb.p.finalize(final_waits=[kb.p.fence(e) for e in Prog.ENGINES])
    return kb


def build_l0_program(dev=False, io=False):
    kb = KB({"dev": dev, "io": io})
    with kb.ctx:
        last = build_layer0(kb, dev=dev)
        kb.p.finalize(final_waits=[kb.p.fence(e) for e in Prog.ENGINES])
    return kb


def build_fused_program(dev=False):
    kb = KB({"fused": True, "io": dev})
    with kb.ctx:
        build_layer0(kb, dev=False)
        kb.cfg["phase_dep_l2"] = [kb.p.fence(s_) for s_ in Prog.STREAMS]
        build_l1attn(kb, dev=False)
        kb.cfg["phase_dep_l3"] = [kb.p.fence(s_) for s_ in Prog.STREAMS]
        build_moe(kb, dev=False)
        kb.p.finalize(final_waits=[kb.p.fence(e) for e in Prog.ENGINES])
    return kb


def make_fused_inputs(inputs, core):
    b, h = core // 2, core % 2
    x = np.asarray(inputs["L0_attn_u"], np.float32)[b]
    z16 = np.zeros((HALO, D), np.float32)
    if h == 0:
        xin = np.concatenate([z16, x], axis=0)
        xhB = np.ascontiguousarray(x[NTOK - HALO:NTOK])
        mA, mB = 0.0, 1.0
    else:
        xin = np.concatenate([x[NTOK - HALO:NTOK], x[NTOK:], x[:NTOK]], axis=0)
        xhB = z16
        mA, mB = 1.0, 0.0
    m = make_l0_inputs(inputs, core)
    m["x"] = np.ascontiguousarray(xin)
    m["hmask"] = np.full((128, 1), mA, np.float32)
    m["xhB"] = xhB
    m["hmaskB"] = np.full((128, 1), mB, np.float32)
    m.update(make_l2_inputs(inputs, core, [None] * 8, [None] * 8))
    m.update(make_l3_inputs(inputs, core, [None] * 8, [None] * 8, [None] * 8, [None] * 8))
    return m


def kernel(**inputs):
    cores = list(range(NCORE))
    kb = build_fused_program()
    res = run_bass_kernel_spmd(kb.nc, [make_fused_inputs(inputs, c) for c in cores],
                               core_ids=cores)
    out = np.zeros((B, L, D), np.float32)
    for c in cores:
        b, h = c // 2, c % 2
        out[b, h * NTOK:(h + 1) * NTOK] = np.asarray(res.results[c]["x3"])
    return out


def kernel_unfused(**inputs):
    cores = list(range(NCORE))
    kb0 = build_l0_program(io=True)
    res0 = run_bass_kernel_spmd(kb0.nc, [make_l0_inputs(inputs, c) for c in cores],
                                core_ids=cores)
    x2_all = [np.asarray(res0.results[c]["x2_d"]) for c in cores]
    x32_all = [np.asarray(res0.results[c]["xn1t32_d"]) for c in cores]
    x16_all = [np.asarray(res0.results[c]["xn1t16_d"]) for c in cores]
    kb2 = build_l2_program(io=True)
    res2 = run_bass_kernel_spmd(kb2.nc, [make_l2_inputs(inputs, c, x16_all, x32_all) for c in cores],
                                core_ids=cores)
    s_all = [np.asarray(res2.results[c]["s_d"]) for c in cores]
    kb3 = build_l3_program()
    res3 = run_bass_kernel_spmd(kb3.nc, [make_l3_inputs(inputs, c, x16_all, x2_all, s_all, x32_all)
                                         for c in cores], core_ids=cores)
    out = np.zeros((B, L, D), np.float32)
    for c in cores:
        b, h = c // 2, c % 2
        out[b, h * NTOK:(h + 1) * NTOK] = np.asarray(res3.results[c]["x3"])
    return out


# version 05
# speedup vs baseline: 1.1940x; 1.1940x over previous
import contextlib
import numpy as np
import ml_dtypes
import concourse.bass as bass
import concourse.mybir as mybir
from concourse.bass_utils import run_bass_kernel_spmd

F32 = mybir.dt.float32
BF16 = mybir.dt.bfloat16
I32 = mybir.dt.int32
I16 = mybir.dt.int16
AF = mybir.ActivationFunctionType
ALU = mybir.AluOpType
AX = mybir.AxisListType

B, L, D = 4, 8192, 1024
NCORE = 8
NTOK = 4096
HALO = 16
T0 = 256
NT0 = NTOK // T0
TC0 = T0 // 128
D3 = 3072
LN_EPS = 1e-5


def bf(x):
    return np.ascontiguousarray(x).astype(ml_dtypes.bfloat16)


class Prog:
    STREAMS = {"tensor": "tensor", "vector": "vector", "scalar": "scalar",
               "gpsimd": "gpsimd", "sdma": "sync", "qdma": "scalar",
               "gdma": "gpsimd"}
    COMPUTE = ("vector", "scalar", "gpsimd")
    ENGINES = tuple(STREAMS)

    def __init__(self, nc, ctx):
        self.nc = nc
        self.steps = []
        self.cnt = {s: 0 for s in self.STREAMS}
        self.sems = {s: ctx.enter_context(nc.semaphore("sem_" + s)) for s in self.STREAMS}
        self.last_wait = {}

    def emit(self, stream, fn, deps=(), dk=1):
        dl = []
        is_comp = stream in self.COMPUTE
        for t in deps:
            if t is None:
                continue
            s2, v = t
            if v <= 0:
                continue
            if s2 == stream and (is_comp or stream == "tensor"):
                continue
            k = (stream, s2)
            if self.last_wait.get(k, 0) >= v:
                continue
            self.last_wait[k] = v
            dl.append((s2, v))
        selfw = self.cnt[stream] if is_comp else None
        self.cnt[stream] += dk
        tok = (stream, self.cnt[stream])
        sems = self.sems

        def step(e, fn=fn, dl=dl, stream=stream, dk=dk, selfw=selfw):
            if selfw:
                e.wait_ge(sems[stream], selfw)
            for s2, v in dl:
                e.wait_ge(sems[s2], v)
            r = fn(e)
            r.then_inc(sems[stream], dk)

        self.steps.append((self.STREAMS[stream], step))
        return tok

    def fence(self, stream):
        return (stream, self.cnt[stream])

    def finalize(self, final_waits=()):
        sems = self.sems

        def fin(e):
            for s2, v in final_waits:
                e.wait_ge(sems[s2], v)
        self.steps.append(("sync", fin))
        with self.nc.Block() as block:
            for en in ("sync", "tensor", "vector", "scalar", "gpsimd"):
                def mk(en):
                    def _(e):
                        for e2, st in self.steps:
                            if e2 == en:
                                st(e)
                    return _
                getattr(block, en)(mk(en))


def ap3(t, off, dims):
    a = t.ap()
    return bass.AP(a.tensor, a.offset + off, [a.ap[0]] + [list(d) for d in dims])


class KB:
    def __init__(self, cfg):
        self.cfg = cfg
        self.nc = bass.Bass("TRN2", target_bir_lowering=False)
        self.ctx = contextlib.ExitStack()
        self.p = Prog(self.nc, self.ctx)
        self.dram = {}
        self.taps = {}

    def dt(self, name, shape, dtype, kind=None):
        if name in self.dram:
            return self.dram[name]
        t = self.nc.dram_tensor(name, list(shape), dtype, **({"kind": kind} if kind else {}))
        self.dram[name] = t
        return t

    def sb(self, name, shape, dtype):
        return self.ctx.enter_context(self.nc.sbuf_tensor(name, list(shape), dtype))

    def ps(self, name, shape, dtype):
        return self.ctx.enter_context(self.nc.psum_tensor(name, list(shape), dtype))

    def mm(self, psum_ap, pairs, deps=(), tp=False):
        def fn(e):
            r = None
            n = len(pairs)
            for i, (l, rh) in enumerate(pairs):
                r = e.matmul(psum_ap, l, rh, start=(i == 0), stop=(i == n - 1),
                             is_transpose=tp or None)
            return r
        return self.p.emit("tensor", fn, deps=deps)

    def trans(self, psum_t, items, eye, deps=()):
        def fn(e):
            r = None
            for src, off, ncols in items:
                dst = ap3(psum_t, off, [[1, ncols]])
                r = e.matmul(dst, src, eye, start=True, stop=True, is_transpose=True)
            return r
        return self.p.emit("tensor", fn, deps=deps)

    def act(self, dst, src, func, bias=None, scale=1.0, deps=(), accum=None):
        def fn(e):
            kw = {}
            if bias is not None:
                kw["bias"] = bias
            if accum is not None:
                kw["accum_out"] = accum
            return e.activation(dst, src, func, scale=scale, **kw)
        return self.p.emit("scalar", fn, deps=deps)

    def v(self, fn, deps=()):
        return self.p.emit("vector", fn, deps=deps)

    DMAQ = {"sync": "sdma", "scalar": "qdma", "gpsimd": "gdma"}

    def dma(self, q, dst, src, deps=(), accum=False):
        if accum:
            return self.p.emit(self.DMAQ.get(q, q),
                               lambda e: e.dma_start(dst, src, accum_op=ALU.add),
                               deps=deps, dk=16)
        return self.p.emit(self.DMAQ.get(q, q), lambda e: e.dma_start(dst, src),
                           deps=deps, dk=16)

    def tap(self, name, cols_per_tile, ntiles, dtype=F32):
        if self.cfg.get("dev") and name not in self.taps:
            self.taps[name] = self.dt("tap_" + name, [128, cols_per_tile * ntiles], dtype, "ExternalOutput")
        return self.taps.get(name)

    def tap_dma(self, name, k, sbuf_ap, cols, deps):
        t = self.taps.get(name)
        if t is not None:
            self.dma("scalar", t[:, k * cols:(k + 1) * cols], sbuf_ap, deps=deps)

    def ln_tok(self, x_ap3, nsub, width, sqbuf, stats, scol, out3, eps, deps):
        p = self.p
        ssum = stats[:, scol:scol + nsub]
        s2 = stats[:, scol + nsub:scol + 2 * nsub]
        mean = stats[:, scol + 2 * nsub:scol + 3 * nsub]
        var = stats[:, scol + 3 * nsub:scol + 4 * nsub]
        rstd = stats[:, scol + 4 * nsub:scol + 5 * nsub]
        nmean = stats[:, scol + 5 * nsub:scol + 6 * nsub]
        sq3 = sqbuf.rearrange("p (c d) -> p c d", d=width)
        t_sq = self.act(sqbuf, x_ap3, AF.Square, deps=deps)
        t1 = self.v(lambda e: e.tensor_reduce(ssum, x_ap3, axis=AX.X, op=ALU.add), deps=deps)
        t2 = self.v(lambda e: e.tensor_reduce(s2, sq3, axis=AX.X, op=ALU.add), deps=[t_sq])
        t3 = self.v(lambda e: e.tensor_scalar(mean, ssum, 1.0 / width, None, op0=ALU.mult), deps=[t1])
        t4 = self.v(lambda e: e.tensor_scalar(var, s2, 1.0 / width, None, op0=ALU.mult), deps=[t2])
        t5 = self.v(lambda e: e.tensor_tensor(nmean, mean, mean, op=ALU.mult), deps=[t3])
        t6 = self.v(lambda e: e.tensor_tensor(var, var, nmean, op=ALU.subtract), deps=[t4, t5])
        t7 = self.v(lambda e: e.tensor_scalar(var, var, 1.0, eps, op0=ALU.mult, op1=ALU.add), deps=[t6])
        t8 = self.v(lambda e: e.reciprocal(var, var), deps=[t7])
        t9 = self.act(rstd, var, AF.Sqrt, deps=[t8])
        t10 = self.v(lambda e: e.tensor_scalar(nmean, mean, -1.0, None, op0=ALU.mult), deps=[t5])
        toks = []
        for c in range(nsub):
            tk = self.v(lambda e, c=c: e.tensor_scalar(
                out3[:, c], x_ap3[:, c], nmean[:, c:c + 1], rstd[:, c:c + 1],
                op0=ALU.add, op1=ALU.mult), deps=[t9, t10])
            toks.append(tk)
        return toks[-1], t_sq

    def quant(self, xn3, nblk, xs, axs, ti, q_g2, q_g4, q_m, q_s, qsc, xd, deps):
        n = nblk * 32
        amax = qsc[:, 0:nblk]
        scale = qsc[:, nblk:2 * nblk]
        rscale = qsc[:, 2 * nblk:3 * nblk]
        xsv = xs[:, 0:n].rearrange("p (b k) -> p b k", k=32)
        t1 = self.v(lambda e: e.tensor_reduce(amax, xn3, axis=AX.X, op=ALU.max,
                                              apply_absolute_value=True), deps=deps)
        t2 = self.v(lambda e: e.tensor_scalar(scale, amax, 1.0 / 7.5, 1e-30,
                                              op0=ALU.mult, op1=ALU.max), deps=[t1])
        t3 = self.v(lambda e: e.reciprocal(rscale, scale), deps=[t2])
        rsb = ap3(qsc, 2 * nblk, [[1, nblk], [0, 32]])
        t4 = self.v(lambda e: e.tensor_tensor(xsv, xn3, rsb, op=ALU.mult), deps=[t3])
        t5 = self.v(lambda e: e.scalar_tensor_tensor(axs[:, 0:n], xs[:, 0:n], -1.0, xs[:, 0:n],
                                                     op0=ALU.mult, op1=ALU.max), deps=[t4])
        tg2 = self.v(lambda e: e.tensor_scalar(q_g2[:, 0:n], axs[:, 0:n], 2.0, None, op0=ALU.is_ge), deps=[t5])
        tg4 = self.v(lambda e: e.tensor_scalar(q_g4[:, 0:n], axs[:, 0:n], 4.0, None, op0=ALU.is_ge), deps=[t5])
        tm1 = self.v(lambda e: e.tensor_scalar(q_m[:, 0:n], q_g2[:, 0:n], 1.0, None, op0=ALU.add), deps=[tg2])
        tm2 = self.v(lambda e: e.tensor_scalar(q_s[:, 0:n], q_g4[:, 0:n], 1.0, None, op0=ALU.add), deps=[tg4])
        tst = self.v(lambda e: e.scalar_tensor_tensor(q_m[:, 0:n], q_m[:, 0:n], 0.125, q_s[:, 0:n],
                                                      op0=ALU.mult, op1=ALU.mult), deps=[tm1, tm2])
        tu = self.v(lambda e: e.tensor_scalar(q_g2[:, 0:n], q_g2[:, 0:n], -0.5, 1.0,
                                              op0=ALU.mult, op1=ALU.add), deps=[tm1])
        tw = self.v(lambda e: e.tensor_scalar(q_g4[:, 0:n], q_g4[:, 0:n], -0.5, 1.0,
                                              op0=ALU.mult, op1=ALU.add), deps=[tm2])
        trs = self.v(lambda e: e.tensor_tensor(q_g4[:, 0:n], q_g2[:, 0:n], q_g4[:, 0:n], op=ALU.mult), deps=[tu, tw])
        tt_ = self.v(lambda e: e.scalar_tensor_tensor(axs[:, 0:n], xs[:, 0:n], 8.0, q_g4[:, 0:n],
                                                      op0=ALU.mult, op1=ALU.mult), deps=[trs, t5])
        tni = self.act(ti[:, 0:n], axs[:, 0:n], AF.Copy, bias=-0.5, deps=[tt_])
        tnf = self.v(lambda e: e.tensor_copy(xs[:, 0:n], ti[:, 0:n]), deps=[tni, tt_])
        tmk = self.v(lambda e: e.scalar_tensor_tensor(q_s[:, 0:n], xs[:, 0:n], 0.5, axs[:, 0:n],
                                                      op0=ALU.add, op1=ALU.is_lt), deps=[tnf])
        tnq = self.v(lambda e: e.tensor_tensor(xs[:, 0:n], xs[:, 0:n], q_s[:, 0:n], op=ALU.add), deps=[tmk])
        scb = ap3(qsc, nblk, [[1, nblk], [0, 32]])
        axv = axs[:, 0:n].rearrange("p (b k) -> p b k", k=32)
        qmv = q_m[:, 0:n].rearrange("p (b k) -> p b k", k=32)
        tF = self.v(lambda e: e.tensor_tensor(axv, qmv, scb, op=ALU.mult), deps=[tst, tnf])
        txd = self.v(lambda e: e.tensor_tensor(xd[:, 0:n], xs[:, 0:n], axs[:, 0:n], op=ALU.mult), deps=[tnq, tF])
        return txd


def build_layer0(kb, dev=False):
    nc, p = kb.nc, kb.p
    cfg = kb.cfg
    dup = cfg.get("dup", False)
    NTOK0 = L if dup else NTOK
    NTL = NTOK0 // T0
    es0 = contextlib.ExitStack()
    sb0 = lambda n_, sh, dt_: es0.enter_context(nc.sbuf_tensor(n_, list(sh), dt_))
    ps0 = lambda n_, sh, dt_: es0.enter_context(nc.psum_tensor(n_, list(sh), dt_))
    x_in = kb.dt("x", [HALO + NTOK0, D], F32, "ExternalInput")
    w_in_d = kb.dt("w_in", [D, D3], F32, "ExternalInput")
    w_out_d = kb.dt("w_out", [D, D], F32, "ExternalInput")
    fw_d = kb.dt("fw", [D, D], F32, "ExternalInput")
    b_in_d = kb.dt("b_in_c", [128, 24], F32, "ExternalInput")
    sw_d = kb.dt("short_w_c", [128, 72], F32, "ExternalInput")
    sbc_d = kb.dt("short_b_c", [128, 24], F32, "ExternalInput")
    iw_d = kb.dt("inner_w_c", [128, 32], F32, "ExternalInput")
    ib_d = kb.dt("inner_b_c", [128, 8], F32, "ExternalInput")
    bo_d = kb.dt("b_out_c", [128, 8], F32, "ExternalInput")
    fb_d = kb.dt("fb_c", [128, 8], F32, "ExternalInput")
    eye32_d = kb.dt("eye32", [128, 128], F32, "ExternalInput")
    eye16_d = kb.dt("eye16", [128, 128], BF16, "ExternalInput")
    hm_d = kb.dt("hmask", [128, 1], F32, "ExternalInput")
    okind = "ExternalOutput" if (dev or kb.cfg.get("io")) else None
    x2_d = kb.dt("x2_d", [NTOK0, D], F32, okind)
    xn1t32_d = kb.dt("xn1t32_d", [D, NTOK0], F32, okind)
    xn1t16_d = kb.dt("xn1t16_d", [D, NTOK0], BF16, okind)
    if dup:
        xhB_d = kb.dt("xhB", [HALO, D], F32, "ExternalInput")
        kb.dt("hmaskB", [128, 1], F32, "ExternalInput")

    w_in_sb = sb0("w_in_sb", [128, 8 * D3], BF16)
    w_out_sb = sb0("w_out_sb", [128, 8 * D], BF16)
    fwdT = sb0("fwdT", [128, 8 * D], BF16)
    b_in_c = sb0("b_in_cs", [128, 24], F32)
    sw_c = sb0("sw_cs", [128, 72], F32)
    sb_c = sb0("sb_cs", [128, 24], F32)
    iw_c = sb0("iw_cs", [128, 32], F32)
    ib_c = sb0("ib_cs", [128, 8], F32)
    bo_c = sb0("bo_cs", [128, 8], F32)
    fb_c = sb0("fb_cs", [128, 8], F32)
    eye32 = sb0("eye32_s", [128, 128], F32)
    eye16 = sb0("eye16_s", [128, 128], BF16)
    hmask = sb0("hmask_s", [128, 1], F32)
    hmaskB = sb0("hmaskB_s", [128, 1], F32) if dup else None

    t_win = kb.dma("gpsimd", w_in_sb[:, :], w_in_d[:, :].rearrange("(kc p) e -> p kc e", p=128))
    t_wout = kb.dma("gpsimd", w_out_sb[:, :], w_out_d[:, :].rearrange("(kc p) e -> p kc e", p=128))
    for nm, dst in [("b_in_c", b_in_c), ("short_w_c", sw_c), ("short_b_c", sb_c),
                    ("inner_w_c", iw_c), ("inner_b_c", ib_c), ("b_out_c", bo_c),
                    ("fb_c", fb_c), ("eye32", eye32), ("eye16", eye16), ("hmask", hmask)] + \
                   ([("hmaskB", hmaskB)] if dup else []):
        kb.dma("sync", dst[:, :], kb.dram[nm][:, :])
    t_consts = p.fence("sdma")

    pT16 = ps0("pT16", [128, 1024], BF16)
    pT32 = ps0("pT32", [128, 512], F32)
    pM = [ps0("pM0", [128, 512], F32), ps0("pM1", [128, 512], F32)]
    pC = [ps0("pC0", [128, 512], F32), ps0("pC1", [128, 512], F32)]
    dgS = sb0("dgS", [128, 72 * 128], BF16)
    dgI = sb0("dgI", [128, 32 * 128], BF16)
    t_dg = []
    for j in range(72):
        t_dg.append(kb.v(lambda e, j=j: e.tensor_scalar(
            dgS[:, j * 128:(j + 1) * 128], eye16[:, :], sw_c[:, j:j + 1], None,
            op0=ALU.mult), deps=[t_consts]))
    for j in range(32):
        t_dg.append(kb.v(lambda e, j=j: e.tensor_scalar(
            dgI[:, j * 128:(j + 1) * 128], eye16[:, :], iw_c[:, j:j + 1], None,
            op0=ALU.mult), deps=[t_consts]))
    t_dg = t_dg[-1]

    x_t = [sb0("x_t0", [128, TC0 * D], F32), sb0("x_t1", [128, TC0 * D], F32)]
    xn = sb0("xn", [128, TC0 * D], F32)
    stats = sb0("stats", [128, 64], F32)
    qsc = sb0("qsc", [128, 3 * TC0 * 32], F32)
    xd = sb0("xd", [128, TC0 * D], BF16)
    xnT = sb0("xnT", [128, 8 * T0], BF16)
    xdT = sb0("xdT", [128, 8 * T0], BF16)
    x1e = sb0("x1e", [128, 24 * (T0 + 2)], BF16)
    qb_ = sb0("qb_", [128, 8 * T0], BF16)
    kbf = sb0("kbf", [128, 8 * T0], BF16)
    vbf = sb0("vbf", [128, 8 * T0], BF16)
    vve = sb0("vve", [128, 8 * (T0 + 3)], BF16)
    qs1 = vve
    s_b = sb0("s_b", [128, 8 * T0], F32)
    g_b = sb0("g_b", [128, 8 * T0], F32)
    ti_b = sb0("ti_b", [128, TC0 * D], I16)
    x1sl = sb0("x1sl", [128, 24 * 2], BF16)
    vvsl = sb0("vvsl", [128, 8 * 3], BF16)
    x1slB = sb0("x1slB", [128, 24 * 2], BF16) if dup else None
    vvslB = sb0("vvslB", [128, 8 * 3], BF16) if dup else None
    xn1t32 = sb0("xn1t32", [128, 8 * T0], F32)
    W1E = T0 + 2
    VVE = T0 + 3

    dbg = sb0("dbg", [128, 2048], F32) if dev else None
    tapstate = {"last": None}

    def dtap(name, src_ap, ncols, deps):
        if not dev:
            return
        d = kb.dt("tap_" + name, [128, ncols], F32, "ExternalOutput")
        tcp = kb.v(lambda e: e.tensor_copy(dbg[:, 0:ncols], src_ap),
                   deps=list(deps) + [tapstate["last"]])
        tapstate["last"] = kb.dma("scalar", d[:, :], dbg[:, 0:ncols], deps=[tcp])

    NBW = 32
    with nc.sbuf_tensor("fw_pt", [128, D], F32) as fw_pt, \
         nc.sbuf_tensor("fwd_pt", [128, D], BF16) as fwd_pt:
        ev_tok = None
        tq = None
        fw_r = fw_d[:, :].rearrange("(ec p) d -> p ec d", p=128)
        for ps_ in range(8):
            t_fw = kb.dma("sync", fw_pt[:, :].rearrange("p (c d) -> p c d", d=D),
                          fw_r[:, ps_:ps_ + 1, :], deps=[tq])
            fw3 = fw_pt[:, :].rearrange("p (b k) -> p b k", k=32)
            pe_fq = p.fence("tensor")
            tq = kb.quant(fw3, NBW, s_b, g_b, ti_b, qb_, kbf, vbf, qs1, qsc, fwd_pt,
                          deps=[t_fw, pe_fq])
            ec = ps_
            for dq in range(2):
                items = [(fwd_pt[:, (dq * 4 + j) * 128: (dq * 4 + j) * 128 + 128],
                          j * 128, 128) for j in range(4)]

                def fnq(e, items=items):
                    r = None
                    for src, off, ncols in items:
                        r = e.matmul(ap3(pT16, off, [[1, ncols]]), src, eye16[:, :],
                                     start=True, stop=True, is_transpose=True)
                    return r
                ttr = p.emit("tensor", fnq, deps=[tq, t_consts, ev_tok])
                dst = ap3(fwdT, (dq * 4) * D + ec * 128, [[D, 4], [1, 128]])
                src = ap3(pT16, 0, [[128, 4], [1, 128]])
                ev_tok = kb.v(lambda e, dst=dst, src=src: e.tensor_copy(dst, src), deps=[ttr])
    t_fwdT = ev_tok
    pre_f = [p.fence(e) for e in Prog.ENGINES]

    xh = x_t[0]
    xnh = x_t[1]
    sqh = s_b
    stats_h = sb0("stats_h", [16, 8], F32)
    xnTh = sb0("xnTh", [128, 8 * 16], BF16)
    x1h = sb0("x1h", [128, 24 * 16], BF16)
    uch = sb0("uch", [128, 24 * 14], BF16)
    vvh = sb0("vvh", [128, 8 * 14], BF16)
    tmpc = sb0("tmpc", [128, T0], BF16)

    def emit_halo(x_src, mask_col, x1sl_t, vvsl_t, pre):
     t_xh = kb.dma("sync", xh[0:16, 0:D], x_src, deps=pre)
     xh3 = xh[0:16, 0:D].rearrange("p (c d) -> p c d", d=D)
     xnh3 = xnh[0:16, 0:D].rearrange("p (c d) -> p c d", d=D)
     t_lnh, _ = kb.ln_tok(xh3, 1, D, sqh[0:16, 0:D], stats_h, 0, xnh3, LN_EPS,
                          deps=pre_f + [t_xh])
     tok_tr = None
     for dq in range(2):
        items = [(xnh[0:16, (dq * 4 + j) * 128:(dq * 4 + j) * 128 + 128], j * 16, 16) for j in range(4)]

        def fnh(e, items=items):
            r = None
            for src, off, ncols in items:
                r = e.matmul(ap3(pT32, off, [[1, ncols]]), src, eye32[0:16, 0:16], start=True,
                             stop=True, is_transpose=True)
            return r
        th = p.emit("tensor", fnh, deps=[t_lnh, t_consts, tok_tr])
        dstp = ap3(xnTh, dq * 4 * 16, [[16, 4], [1, 16]])
        srcp = ap3(pT32, 0, [[16, 4], [1, 16]])
        tok_tr = kb.v(lambda e, d=dstp, s=srcp: e.tensor_copy(d, s), deps=[th])
     ev_h = []
     prev_ev = None
     for mc in range(24):
        bank = pM[mc % 2]
        pairs = [(w_in_sb[:, kc * D3 + mc * 128: kc * D3 + mc * 128 + 128],
                  xnTh[:, kc * 16:(kc + 1) * 16]) for kc in range(8)]
        dep_bank = ev_h[mc - 2] if mc >= 2 else None
        tmm = kb.mm(ap3(bank, 0, [[1, 16]]), pairs, deps=[tok_tr, t_win, dep_bank])
        tev = kb.act(x1h[:, mc * 16:(mc + 1) * 16], ap3(bank, 0, [[1, 16]]), AF.Identity,
                     bias=b_in_c[:, mc:mc + 1], deps=[tmm])
        ev_h.append(tev)
     t_x1h = kb.v(lambda e: e.tensor_scalar(x1h[:, :], x1h[:, :], mask_col, None, op0=ALU.mult),
                 deps=ev_h + [t_consts])
     tc3 = None
     for cc in range(24):
        dst = uch[:, cc * 14:(cc + 1) * 14]
        ta = kb.v(lambda e, cc=cc, dst=dst: e.tensor_scalar(
            dst, x1h[:, cc * 16:cc * 16 + 14], sw_c[:, cc * 3:cc * 3 + 1],
            sb_c[:, cc:cc + 1], op0=ALU.mult, op1=ALU.add), deps=[t_x1h])
        for j in (1, 2):
            kb.v(lambda e, cc=cc, j=j: e.tensor_scalar(
                tmpc[:, 0:14], x1h[:, cc * 16 + j:cc * 16 + j + 14],
                sw_c[:, cc * 3 + j:cc * 3 + j + 1], None, op0=ALU.mult), deps=[])
            tc3 = kb.v(lambda e, dst=dst: e.tensor_tensor(dst, dst, tmpc[:, 0:14], op=ALU.add),
                       deps=[])
     vvh3 = vvh[:, :].rearrange("p (c t) -> p c t", t=14)
     tv1 = kb.v(lambda e: e.tensor_tensor(vvh3, ap3(uch, 8 * 14, [[14, 8], [1, 14]]),
                                         ap3(uch, 16 * 14, [[14, 8], [1, 14]]), op=ALU.mult), deps=[tc3])
     tv2 = kb.v(lambda e: e.tensor_scalar(vvh[:, :], vvh[:, :], mask_col, None, op0=ALU.mult),
               deps=[tv1])
     t_sl1 = kb.v(lambda e: e.tensor_copy(ap3(x1sl_t, 0, [[2, 24], [1, 2]]),
                                         ap3(x1h, 14, [[16, 24], [1, 2]])), deps=[t_x1h])
     t_sl2 = kb.v(lambda e: e.tensor_copy(ap3(vvsl_t, 0, [[3, 8], [1, 3]]),
                                         ap3(vvh, 11, [[14, 8], [1, 3]])), deps=[tv2])

     return tok_tr, t_sl1, t_sl2

    tok_tr, t_sl1, t_sl2 = emit_halo(x_in[0:HALO, :], hmask[:, 0:1], x1sl, vvsl,
                                     pre_f + [p.fence("sdma")])
    if dup:
        tokB, tB_sl1, tB_sl2 = emit_halo(xhB_d[:, :], hmaskB[:, 0:1], x1slB, vvslB,
                                         [p.fence("vector"), p.fence("tensor"), p.fence("scalar")])
        tok_tr = tokB
    halo_f = [p.fence("vector"), p.fence("tensor"), p.fence("scalar")]
    if dev:
        dtap("x1h", x1h[:, :], 384, [t_x1h])
        dtap("uch", uch[:, :], 336, [tc3])
        dtap("vvh", vvh[:, :], 112, [tv2])
        dtap("x1sl_h", x1sl[:, :], 48, [t_sl1])
        dtap("vvsl_h", vvsl[:, :], 24, [t_sl2])
    NB = TC0 * 32
    prev = {"pT32_free": tok_tr}
    last_store = []
    for k in range(NTL):
        xb = x_t[k % 2]
        r0 = HALO + k * T0
        t_x = kb.dma("sync", xb[:, :].rearrange("p (c d) -> p c d", d=D),
                     x_in[r0:r0 + T0, :].rearrange("(c p) d -> p c d", p=128),
                     deps=(pre_f + halo_f if k < 2 else prev.get(("x_free", k % 2), [])))
        x3 = xb[:, :].rearrange("p (c d) -> p c d", d=D)
        xn3 = xn[:, :].rearrange("p (c d) -> p c d", d=D)
        t_ln, _ = kb.ln_tok(x3, TC0, D, s_b[:, 0:TC0 * D], stats, 0, xn3, LN_EPS,
                            deps=[t_x, prev.get("pe")])
        if k == 0:
            dtap("xn0", xn[:, :], 2048, [t_ln])
        xnq = xn[:, :].rearrange("p (b k) -> p b k", k=32)
        t_qd = kb.quant(xnq, NB, s_b, g_b, ti_b, qb_, kbf, vbf, qs1, qsc, xd,
                        deps=[t_ln, prev.get("pe")])
        if k == 0:
            dtap("xd0", xd[:, :], 2048, [t_qd])
        t_xnT = []
        tr_dep = prev.get("pT32_free")
        first_ev = prev.get("pe")
        for tc in range(TC0):
            for dq in range(2):
                items = [(xn[:, tc * D + (dq * 4 + j) * 128: tc * D + (dq * 4 + j) * 128 + 128],
                          j * 128, 128) for j in range(4)]

                def ftr(e, items=items):
                    r = None
                    for src, off, ncols in items:
                        r = e.matmul(ap3(pT32, off, [[1, ncols]]), src, eye32[:, :],
                                     start=True, stop=True, is_transpose=True)
                    return r
                ttr = p.emit("tensor", ftr, deps=[t_ln, tr_dep])
                dstp = ap3(xnT, dq * 4 * T0 + tc * 128, [[T0, 4], [1, 128]])
                tr_dep = kb.act(dstp, ap3(pT32, 0, [[128, 4], [1, 128]]), AF.Copy,
                                deps=[ttr, first_ev])
                first_ev = None
                t_xnT.append(tr_dep)
        t_xdT = []
        trd_dep = prev.get("pT16_free")
        for tc in range(TC0):
            items = [(xd[:, tc * D + j * 128: tc * D + j * 128 + 128], j * 128, 128) for j in range(8)]

            def ftd(e, items=items):
                r = None
                for src, off, ncols in items:
                    r = e.matmul(ap3(pT16, off, [[1, ncols]]), src, eye16[:, :],
                                 start=True, stop=True, is_transpose=True)
                return r
            ttr = p.emit("tensor", ftd, deps=[t_qd, trd_dep, t_fwdT])
            dstp = ap3(xdT, tc * 128, [[T0, 8], [1, 128]])
            trd_dep = kb.v(lambda e, d=dstp: e.tensor_copy(d, ap3(pT16, 0, [[128, 8], [1, 128]])),
                           deps=[ttr])
            t_xdT.append(trd_dep)
        prev["pT16_free"] = trd_dep

        seam = dup and (k == NTOK // T0)
        rs_x1 = x1slB if seam else x1sl
        rs_dep = tB_sl1 if seam else t_sl1
        t_rs = kb.v(lambda e, rs_x1=rs_x1: e.tensor_copy(ap3(x1e, 0, [[W1E, 24], [1, 2]]),
                                                         ap3(rs_x1, 0, [[2, 24], [1, 2]])),
                    deps=[rs_dep, prev.get("conv_done")])
        ev_toks = []
        mm_prev = [None, None]
        for mc in range(24):
            bank = pM[mc % 2]
            pairs = [(w_in_sb[:, kc * D3 + mc * 128: kc * D3 + mc * 128 + 128],
                      xnT[:, kc * T0:(kc + 1) * T0]) for kc in range(8)]
            tmm = kb.mm(ap3(bank, 0, [[1, T0]]), pairs,
                        deps=[t_xnT[-1], t_win, mm_prev[mc % 2], prev.get("conv_done") if mc < 2 else None])
            tev = kb.act(ap3(x1e, mc * W1E + 2, [[1, T0]]), ap3(bank, 0, [[1, T0]]),
                         AF.Identity, bias=b_in_c[:, mc:mc + 1],
                         deps=[tmm, t_rs if mc == 0 else None, prev.get("x1sl_done")])
            mm_prev[mc % 2] = tev
            ev_toks.append(tev)
        grp = {0: qb_, 1: kbf, 2: vbf}
        t_conv = []
        cdep = {0: prev.get("pC0f"), 1: prev.get("pC1f")}
        for g in range(3):
            gb = grp[g]
            last = None
            for l in range(8):
                cc = g * 8 + l
                bank = pC[cc % 2]

                def fcv(e, cc=cc, bank=bank):
                    r = None
                    for j in range(3):
                        r = e.matmul(ap3(bank, 0, [[1, T0]]),
                                     dgS[:, (cc * 3 + j) * 128:(cc * 3 + j) * 128 + 128],
                                     ap3(x1e, cc * W1E + j, [[1, T0]]),
                                     start=(j == 0), stop=(j == 2))
                    return r
                tmm = p.emit("tensor", fcv, deps=[ev_toks[cc], t_rs, t_dg, cdep[cc % 2]])
                last = kb.v(lambda e, cc=cc, l=l, gb=gb, bank=bank: e.tensor_scalar(
                    gb[:, l * T0:(l + 1) * T0], ap3(bank, 0, [[1, T0]]),
                    1.0, sb_c[:, cc:cc + 1], op0=ALU.mult, op1=ALU.add),
                    deps=[tmm, t_qd])
                cdep[cc % 2] = last
            t_conv.append(last)
        if k == 0:
            dtap("x1e0", ap3(x1e, 2, [[W1E, 8], [1, T0]]), 2048, ev_toks)
            dtap("q0", qb_[:, :], 2048, [t_conv[0]])
            dtap("k0", kbf[:, :], 2048, [t_conv[1]])
            dtap("v0c", vbf[:, :], 2048, [t_conv[2]])
        t_sl1 = kb.v(lambda e: e.tensor_copy(ap3(x1sl, 0, [[2, 24], [1, 2]]),
                                             ap3(x1e, T0, [[W1E, 24], [1, 2]])),
                     deps=ev_toks[-1:])
        prev["x1sl_done"] = t_sl1
        prev["conv_done"] = t_conv[2]
        t_vv = kb.v(lambda e: e.tensor_tensor(ap3(vve, 3, [[VVE, 8], [1, T0]]),
                                              kbf[:, :].rearrange("p (c t) -> p c t", t=T0),
                                              vbf[:, :].rearrange("p (c t) -> p c t", t=T0),
                                              op=ALU.mult), deps=[t_conv[1], t_conv[2], prev.get("v2_done")])
        rs_vv = vvslB if seam else vvsl
        rs_dep2 = tB_sl2 if seam else t_sl2
        t_vr = kb.v(lambda e, rs_vv=rs_vv: e.tensor_copy(ap3(vve, 0, [[VVE, 8], [1, 3]]),
                                                         ap3(rs_vv, 0, [[3, 8], [1, 3]])),
                    deps=[rs_dep2, t_vv])
        t_sl2 = kb.v(lambda e: e.tensor_copy(ap3(vvsl, 0, [[3, 8], [1, 3]]),
                                             ap3(vve, T0, [[VVE, 8], [1, 3]])), deps=[t_vv])
        b1 = None
        for l in range(8):
            bank = pC[l % 2]

            def fci(e, l=l, bank=bank):
                r = None
                for j in range(4):
                    r = e.matmul(ap3(bank, 0, [[1, T0]]),
                                 dgI[:, (l * 4 + j) * 128:(l * 4 + j) * 128 + 128],
                                 ap3(vve, l * VVE + j, [[1, T0]]),
                                 start=(j == 0), stop=(j == 3))
                return r
            tmm = p.emit("tensor", fci, deps=[t_vr, t_sl2, cdep[l % 2]])
            b1 = kb.v(lambda e, l=l, bank=bank: e.tensor_scalar(
                kbf[:, l * T0:(l + 1) * T0], ap3(bank, 0, [[1, T0]]),
                1.0, ib_c[:, l:l + 1], op0=ALU.mult, op1=ALU.add), deps=[tmm])
            cdep[l % 2] = b1
        prev["pC0f"], prev["pC1f"] = cdep[0], cdep[1]
        t_y = kb.v(lambda e: e.tensor_tensor(vbf[:, :], qb_[:, :], kbf[:, :], op=ALU.mult),
                   deps=[b1, t_conv[0]])
        prev["v2_done"] = t_y
        if k == 0:
            dtap("vve0", ap3(vve, 0, [[VVE, 8], [1, T0]]), 2048, [t_vr, t_vv])
            dtap("y0", vbf[:, :], 2048, [t_y])

        mm_prev2 = [None, None]
        sg_toks = []
        for which, (wsb, rhs, dstb, biasc, func, wdep) in enumerate([
                (w_out_sb, vbf, s_b, bo_c, AF.Identity, t_wout),
                (fwdT, xdT, g_b, fb_c, AF.Silu, t_fwdT)]):
            for mc in range(8):
                bank = pM[mc % 2]
                pairs = [(wsb[:, kc * D + mc * 128: kc * D + mc * 128 + 128],
                          rhs[:, kc * T0:(kc + 1) * T0]) for kc in range(8)]
                dep_in = t_y if which == 0 else t_xdT[-1]
                tmm = kb.mm(ap3(bank, 0, [[1, T0]]), pairs,
                            deps=[dep_in, wdep, mm_prev2[mc % 2]])
                tev = kb.act(dstb[:, mc * T0:(mc + 1) * T0], ap3(bank, 0, [[1, T0]]),
                             func, bias=biasc[:, mc:mc + 1], deps=[tmm, t_qd])
                mm_prev2[mc % 2] = tev
                sg_toks.append(tev)
        if k == 0:
            dtap("s0", s_b[:, :], 2048, sg_toks[:8])
            dtap("g0", g_b[:, :], 2048, sg_toks[8:])
        t_p = kb.v(lambda e: e.tensor_tensor(s_b[:, :], s_b[:, :], g_b[:, :], op=ALU.mult),
                   deps=sg_toks)
        t_pT = []
        trp_dep = prev.get("pT32_free")
        for tc in range(TC0):
            for dq in range(2):
                items = [(s_b[:, (dq * 4 + j) * T0 + tc * 128: (dq * 4 + j) * T0 + tc * 128 + 128],
                          j * 128, 128) for j in range(4)]

                def ftp(e, items=items):
                    r = None
                    for src, off, ncols in items:
                        r = e.matmul(ap3(pT32, off, [[1, ncols]]), src, eye32[:, :],
                                     start=True, stop=True, is_transpose=True)
                    return r
                ttr = p.emit("tensor", ftp, deps=[t_p, trp_dep])
                dstp = ap3(xn, tc * D + dq * 4 * 128, [[128, 4], [1, 128]])
                trp_dep = kb.v(lambda e, d=dstp: e.tensor_copy(d, ap3(pT32, 0, [[128, 4], [1, 128]])),
                               deps=[ttr, t_qd])
                t_pT.append(trp_dep)
        prev["pT32_free"] = trp_dep
        if k == 0:
            dtap("p0", s_b[:, :], 2048, [t_p])
            dtap("ptok0", xn[:, :], 2048, [t_pT[-1]])
        t_x2 = kb.v(lambda e, xb=xb: e.tensor_tensor(xb[:, :], xb[:, :], xn[:, :], op=ALU.add),
                    deps=[t_pT[-1]])
        if k == 0:
            dtap("x2t0", xb[:, :], 2048, [t_x2])
        t_st_x2 = kb.dma("scalar", x2_d[k * T0:(k + 1) * T0, :].rearrange("(c p) d -> p c d", p=128),
                         xb[:, :].rearrange("p (c d) -> p c d", d=D), deps=[t_x2])
        xn1v = xn[:, :].rearrange("p (c d) -> p c d", d=D)
        t_ln1, t_sq1 = kb.ln_tok(x3, TC0, D, g_b[:, 0:TC0 * D], stats, 20, xn1v, LN_EPS,
                                 deps=[t_x2, t_p, prev.get("pe")])
        t_tr1 = []
        tr1_dep = prev.get("pT32_free")
        ev_extra = prev.get("xn1t_free", [])
        for tc in range(TC0):
            for dq in range(2):
                items = [(xn[:, tc * D + (dq * 4 + j) * 128: tc * D + (dq * 4 + j) * 128 + 128],
                          j * 128, 128) for j in range(4)]

                def ft1(e, items=items):
                    r = None
                    for src, off, ncols in items:
                        r = e.matmul(ap3(pT32, off, [[1, ncols]]), src, eye32[:, :],
                                     start=True, stop=True, is_transpose=True)
                    return r
                ttr = p.emit("tensor", ft1, deps=[t_ln1, tr1_dep])
                dstp = ap3(xn1t32, dq * 4 * T0 + tc * 128, [[T0, 4], [1, 128]])
                tr1_dep = kb.v(lambda e, d=dstp: e.tensor_copy(d, ap3(pT32, 0, [[128, 4], [1, 128]])),
                               deps=[ttr] + list(ev_extra))
                ev_extra = []
                t_tr1.append(tr1_dep)
        prev["pT32_free"] = tr1_dep
        st1 = kb.dma("scalar", xn1t32_d[:, k * T0:(k + 1) * T0].rearrange("(dc p) t -> p dc t", p=128),
                     xn1t32[:, :].rearrange("p (dc t) -> p dc t", t=T0), deps=[tr1_dep])
        st2 = kb.dma("gpsimd", xn1t16_d[:, k * T0:(k + 1) * T0].rearrange("(dc p) t -> p dc t", p=128),
                     xn1t32[:, :].rearrange("p (dc t) -> p dc t", t=T0), deps=[tr1_dep])
        prev[("x_free", k % 2)] = [t_st_x2, t_ln1, st1]
        prev["xn1t_free"] = [st1, st2]
        prev["pe"] = p.fence("tensor")
        last_store = [t_st_x2, st1, st2]
    es0.close()
    return last_store

T1 = 512
NT1 = L // T1
SC = float(64.0 ** -0.25)
LN16 = float(np.log(16.0))
EPS1 = 1e-6


def build_l1attn(kb, dev=False, na=NT1, nb=8):
    nc, p = kb.nc, kb.p
    skip = kb.cfg.get("skip", set())
    fused = kb.cfg.get("fused", False)
    es2 = contextlib.ExitStack()
    if fused:
        xnf_d = kb.dram["xn1t16_d"]
        xn32_d = kb.dram["xn1t32_d"]
    else:
        xnf_d = kb.dt("xnf", [D, L], BF16, "ExternalInput")
        xn32_d = kb.dt("xn32", [D, NTOK], F32, "ExternalInput")
    for nm in ["wq", "wk", "wv", "wo"]:
        kb.dt(nm, [D, D], F32, "ExternalInput")
    omT_d = kb.dt("omT", [128, 256], F32, "ExternalInput")
    e16_d = kb.dt("e16f", [128, 128], F32, "ExternalInput")
    for nm in ["ln1g_c", "ln1b_c", "ln2g_c", "ln2b_c", "b2_c"]:
        kb.dt(nm, [128, 8], F32, "ExternalInput")
    kb.dt("b1_c", [128, 32], F32, "ExternalInput")
    w1_d = kb.dt("w1", [D, 4 * D], F32, "ExternalInput")
    w2_d = kb.dt("w2", [4 * D, D], F32, "ExternalInput")
    kb.dt("eye16", [128, 128], BF16, "ExternalInput")
    kb.dt("eye32", [128, 128], F32, "ExternalInput")
    kb.dt("onesb", [128, 1], BF16, "ExternalInput")
    kb.dt("onesf", [1, 128], F32, "ExternalInput")
    okind = "ExternalOutput" if (dev or kb.cfg.get("io")) else None
    s_d = kb.dt("s_d", [D, NTOK], F32, okind)
    phase_dep = kb.cfg.get("phase_dep_l2", [])
    qT_dram = kb.dt("qT_sc", [D, NTOK], BF16)
    sqq_dram = kb.dt("sqq_sc", [128, 8 * 64], F32)

    es2sb = lambda n_, sh, dt_: es2.enter_context(kb.nc.sbuf_tensor('l2_' + n_, list(sh), dt_))
    omT = es2sb("omT_s", [128, 256], BF16)
    e16f = es2sb("e16f_s", [128, 128], F32)
    eye16 = es2sb("eye16_s", [128, 128], BF16)
    eye32 = es2sb("eye32_s", [128, 128], F32)
    onesb = es2sb("onesb_s", [128, 1], BF16)
    onesf = es2sb("onesf_s", [1, 128], F32)
    lnv = {nm: es2sb(nm + "_s", [128, 8], F32)
           for nm in ["ln1g_c", "ln1b_c", "ln2g_c", "ln2b_c", "b2_c"]}
    b1c = es2sb("b1c_s", [128, 32], F32)
    kb.dma("gpsimd", omT[:, :], omT_d[:, :], deps=phase_dep)
    first_sd = list(phase_dep)
    for nm, dst in ([("e16f", e16f), ("eye16", eye16), ("eye32", eye32),
                     ("onesb", onesb), ("onesf", onesf), ("b1_c", b1c)] +
                    [(nm, lnv[nm]) for nm in lnv]):
        kb.dma("sync", dst[:, :], kb.dram[nm][:, :], deps=first_sd)
        first_sd = []
    t_w = [p.fence("gdma"), p.fence("sdma")] + list(phase_dep)

    qTt = es2sb("qTt", [128, 8 * T1], BF16)
    fscr = es2sb("fscr", [128, 8 * T1], F32)
    bscr = es2sb("bscr", [128, 8 * T1], BF16)
    bsc2 = es2sb("bsc2", [128, 8 * T1], BF16)
    sqc = es2sb("sqc", [128, 64], F32)
    sqcq = es2sb("sqcq", [128, 64], F32)
    stb = es2sb("stb", [128, 16], F32)
    cng = es2sb("cng", [128, 16], F32)
    kv_bf = es2sb("kv_bfuf", [128, 5 * 455], BF16)

    def dtap(name, src_ap, ncols, deps, dt_=F32):
        if not dev:
            return
        d = kb.dt("tap_" + name, [128, ncols], dt_, "ExternalOutput")
        kb.dma("qdma", d[:, :], src_ap, deps=deps)

    es_a = contextlib.ExitStack()
    with es_a:
        _e = es_a.enter_context
        pk0 = _e(nc.psum_tensor("pkv0", [128, 512], F32))
        pk1 = _e(nc.psum_tensor("pkv1", [128, 512], F32))
        pk2 = _e(nc.psum_tensor("pkv2", [128, 512], F32))
        pk3 = _e(nc.psum_tensor("pkv3", [128, 512], F32))
        pk4 = _e(nc.psum_tensor("pkv4", [128, 512], F32))
        pA = _e(nc.psum_tensor("pA", [128, 512], F32))
        pB = _e(nc.psum_tensor("pB", [128, 512], F32))
        pC = _e(nc.psum_tensor("pC", [128, 512], F32))
        wqs = _e(nc.sbuf_tensor("wqs", [128, 8 * D], BF16))
        wks = _e(nc.sbuf_tensor("wks", [128, 8 * D], BF16))
        wvs = _e(nc.sbuf_tensor("wvs", [128, 8 * D], BF16))
        xnb0 = _e(nc.sbuf_tensor("xnb0", [128, 8 * T1], BF16))
        xnb1 = _e(nc.sbuf_tensor("xnb1", [128, 8 * T1], BF16))
        kT = _e(nc.sbuf_tensor("kT", [128, 8 * T1], BF16))
        vab = _e(nc.sbuf_tensor("vab", [128, 4 * 1040], BF16))
        sqrow = _e(nc.sbuf_tensor("sqrow", [16, T1], F32))
        kv_sb = _e(nc.sbuf_tensor("kv_sbuf", [128, 5 * 455], F32))
        xnb = [xnb0, xnb1]
        for nm, dst in [("wq", wqs), ("wk", wks), ("wv", wvs)]:
            kb.dma("gpsimd", dst[:, :], kb.dram[nm][:, :].rearrange("(kc p) e -> p kc e", p=128))
        t_w = t_w + [p.fence("gdma")]
        pkv = [pk0, pk1, pk2, pk3, pk4]
        prev = {}
        kv_last = None
        for ti in range(na):
            own = ti < nb
            xb = xnb[ti % 2]
            t_x = kb.dma("sync", xb[:, :].rearrange("p (kc t) -> p kc t", t=T1),
                         xnf_d[:, ti * T1:(ti + 1) * T1].rearrange("(kc p) t -> p kc t", p=128),
                         deps=[prev.get(("xfree", ti % 2))] + (t_w if ti == 0 else []))
            rot = [pA, pB]
            rotdep = {0: [prev.get("rA")], 1: [prev.get("rB")]}

            for which in (["q", "k"] if own else ["k"]):
                wsb = wqs if which == "q" else wks
                dstT = qTt if which == "q" else kT
                dst_sq = sqcq if which == "q" else sqc
                ev = []
                for mc in range(8):
                    bank = rot[mc % 2]
                    pairs = [(wsb[:, kc * D + mc * 128: kc * D + mc * 128 + 128],
                              xb[:, kc * T1:(kc + 1) * T1]) for kc in range(8)]
                    if "mm" in skip:
                        continue
                    tmm = kb.mm(ap3(bank, 0, [[1, T1]]), pairs,
                                deps=[t_x] + rotdep[mc % 2])
                    e1 = e2 = tmm
                    if "e1" not in skip:
                        e1 = kb.act(fscr[:, mc * T1:(mc + 1) * T1], ap3(bank, 0, [[1, T1]]),
                                    AF.Square, scale=SC, deps=[tmm])
                    if "e2" not in skip:
                        e2 = kb.v(lambda e, bank=bank, dstT=dstT, mc=mc: e.tensor_scalar(
                            dstT[:, mc * T1:(mc + 1) * T1], ap3(bank, 0, [[1, T1]]),
                            SC, None, op0=ALU.mult), deps=[tmm, e1, prev.get("kTfree") if which == "k" else prev.get("qTfree")])
                    rotdep[mc % 2] = [e1, e2]
                    ev.append((e1, e2))
                if "mm" in skip:
                    continue
                if "sq" in skip:
                    continue
                sq_mm = None
                for mc in range(8):
                    sq_mm = p.emit("tensor", lambda e, mc=mc: e.matmul(
                        pC[0:16, 0:T1], e16f[:, mc * 16:(mc + 1) * 16],
                        fscr[:, mc * T1:(mc + 1) * T1], start=(mc == 0), stop=(mc == 7)),
                        deps=[ev[mc][0], prev.get("pCfree")])
                e3 = kb.v(lambda e: e.tensor_copy(sqrow[:, :], pC[0:16, 0:T1]), deps=[sq_mm])
                tr = None
                for j in range(4):
                    tr = p.emit("tensor", lambda e, j=j: e.matmul(
                        pC[:, 256 + j * 16:256 + (j + 1) * 16],
                        sqrow[:, j * 128:(j + 1) * 128], eye32[0:16, 0:16],
                        start=True, stop=True, is_transpose=True), deps=[e3, tr])
                e4 = kb.v(lambda e, dst_sq=dst_sq: e.tensor_scalar(
                    dst_sq[:, :], pC[:, 256:320], 0.5, LN16, op0=ALU.mult, op1=ALU.add),
                    deps=[tr])
                prev["pCfree"] = e4
                if which == "q":
                    stq = kb.dma("qdma", qT_dram[:, ti * T1:(ti + 1) * T1].rearrange("(kc p) t -> p kc t", p=128),
                                 qTt[:, :].rearrange("p (kc t) -> p kc t", t=T1), deps=[ev[7][1]])
                    stsq = kb.dma("qdma", sqq_dram[:, ti * 64:(ti + 1) * 64], sqcq[:, :], deps=[e4])
                    prev["qTfree"] = stq
            if "v" in skip:
                prev[("xfree", ti % 2)] = p.fence("tensor")
                continue
            evv = []
            for tc in range(4):
                tm1 = kb.v(lambda e, tc=tc: e.memset(
                    ap3(vab, tc * 1040 + 64, [[65, 16], [1, 1]]), 1.0),
                    deps=[prev.get("vabfree")])
                for nh in range(2):
                    bank = rot[(tc * 2 + nh) % 2]
                    pairs = [(xb[:, kc * T1 + tc * 128: kc * T1 + tc * 128 + 128],
                              wvs[:, kc * D + nh * 512: kc * D + nh * 512 + 512]) for kc in range(8)]
                    tmm = kb.mm(ap3(bank, 0, [[1, 512]]), pairs,
                                deps=[t_x] + rotdep[(tc * 2 + nh) % 2])
                    dst = ap3(vab, tc * 1040 + nh * 8 * 65, [[65, 8], [1, 64]])
                    tev = kb.v(lambda e, dst=dst, bank=bank: e.tensor_copy(
                        dst, ap3(bank, 0, [[64, 8], [1, 64]])),
                        deps=[tmm, prev.get("vabfree")])
                    rotdep[(tc * 2 + nh) % 2] = [tev]
                    evv.append(tev)
            t_vaug = evv[-1]
            for tc in (() if "pk" in skip else range(4)):
                prj = []
                for h in range(16):
                    bank = rot[h % 2]
                    lh = kT[(h % 2) * 64:(h % 2) * 64 + 64,
                            (h // 2) * T1 + tc * 128:(h // 2) * T1 + tc * 128 + 128]
                    omx = omT[(h % 2) * 64:(h % 2) * 64 + 64, :]
                    tmm = p.emit("tensor", lambda e, lh=lh, bank=bank, omx=omx: e.matmul(
                        ap3(bank, 0, [[1, 256]]), lh, omx, start=True, stop=True),
                        deps=rotdep[h % 2] + [ev[7][1]])
                    tev = kb.v(lambda e, h=h, bank=bank: e.tensor_scalar(
                        fscr[:, h * 256:(h + 1) * 256], ap3(bank, 0, [[1, 256]]),
                        1.0, None, op0=ALU.mult, op1=ALU.max, accum_out=stb[:, h:h + 1]),
                        deps=[tmm, prev.get("fscrfree")])
                    rotdep[h % 2] = [tev]
                    prj.append(tev)
                tcn = kb.v(lambda e, tc=tc: e.scalar_tensor_tensor(
                    cng[:, :], stb[:, :], -1.0, sqc[:, tc * 16:(tc + 1) * 16],
                    op0=ALU.mult, op1=ALU.subtract), deps=prj + [prev.get("cngfree")])
                texp = None
                for h in range(16):
                    texp = kb.act(fscr[:, h * 256:(h + 1) * 256], fscr[:, h * 256:(h + 1) * 256],
                                  AF.Exp, bias=cng[:, h:h + 1], deps=[tcn])
                prev["cngfree"] = texp
                tpk = kb.v(lambda e: e.tensor_scalar(bscr[:, :16 * 256], fscr[:, :16 * 256],
                                                     1e-6, None, op0=ALU.add), deps=[texp])
                prev["fscrfree"] = tpk
                for h in ([] if "kv" in skip else range(16)):
                    for j in range(2):
                        s_ = 2 * h + j
                        bank = pkv[s_ // 7]
                        col = (s_ % 7) * 65
                        stflag = (ti == 0 and tc == 0 and (s_ % 7) == 0)
                        last = (ti == na - 1 and tc == 3)
                        kv_last = p.emit("tensor", lambda e, h=h, j=j, bank=bank, col=col,
                                         stflag=stflag, last=last, tc=tc: e.matmul(
                            ap3(bank, col, [[1, 65]]),
                            bscr[:, h * 256 + j * 128: h * 256 + j * 128 + 128],
                            vab[:, tc * 1040 + h * 65: tc * 1040 + h * 65 + 65],
                            start=stflag, stop=last, skip_group_check=True),
                            deps=[tpk, t_vaug])
            prev[("xfree", ti % 2)] = p.fence("tensor")
            prev["vabfree"] = kv_last
            prev["kTfree"] = kv_last
            prev["rA"], prev["rB"] = rotdep[0][0], rotdep[1][0]
        if "kv" in skip or "pk" in skip or "v" in skip:
            kv_last = p.fence("tensor")
        evk = []
        for b_ in range(5):
            ncols = 455 if b_ < 4 else 260
            evk.append(kb.v(lambda e, b_=b_, ncols=ncols: e.tensor_copy(
                kv_sb[:, b_ * 455:b_ * 455 + ncols],
                ap3(pkv[b_], 0, [[1, ncols]])), deps=[kv_last]))
        if fused:
            kvp_d = kb.dt("kvp_sc", [128, 2080], F32)
            kvo_d = kb.dt("kvo_sc", [128, 2080], F32)
            st_kv = kb.dma("scalar", kvp_d[:, :], kv_sb[:, 0:2080], deps=evk)
            RG = [[0, 1], [2, 3], [4, 5], [6, 7]]
            t_ar = p.emit("gdma", lambda e: e.collective_compute(
                "AllReduce", ALU.add, replica_groups=RG,
                ins=[kvp_d.ap().opt()], outs=[kvo_d.ap().opt()]),
                deps=[st_kv], dk=1)
            t_ld = kb.dma("sync", kv_sb[:, 0:2080], kvo_d[:, :], deps=[t_ar])
            evk = evk + [t_ld]
        t_kv = kb.v(lambda e: e.tensor_copy(kv_bf[:, 0:2080], kv_sb[:, 0:2080]), deps=evk)
        if dev:
            dtap("kv", kv_sb[:, 0:2048], 2048, [t_kv])

    af = [p.fence(s_) for s_ in Prog.STREAMS]

    def kvs(h, j):
        s_ = 2 * h + j
        return kv_bf[:, (s_ // 7) * 455 + (s_ % 7) * 65: (s_ // 7) * 455 + (s_ % 7) * 65 + 65]

    es_b = contextlib.ExitStack()
    with es_b:
        _e = es_b.enter_context
        qA = _e(nc.psum_tensor("qA", [128, 512], F32))
        qB = _e(nc.psum_tensor("qB", [128, 512], F32))
        qX = _e(nc.psum_tensor("qX", [128, 1024], BF16))
        pH0 = _e(nc.psum_tensor("pH0", [128, 512], F32))
        pH1 = _e(nc.psum_tensor("pH1", [128, 512], F32))
        pY0 = _e(nc.psum_tensor("pY0", [128, 512], F32))
        pY1 = _e(nc.psum_tensor("pY1", [128, 512], F32))
        wos = _e(nc.sbuf_tensor("wos", [128, 8 * D], BF16))
        xnT32 = _e(nc.sbuf_tensor("xnT32", [128, 8 * T1], F32))
        pqT = _e(nc.sbuf_tensor("pqT", [128, 16 * 256], BF16))
        attn = _e(nc.sbuf_tensor("attn", [128, 4 * 1024], BF16))
        y1 = _e(nc.sbuf_tensor("y1", [128, 8 * T1], F32))
        xa = _e(nc.sbuf_tensor("xa", [128, 8 * T1], F32))
        num_sb = _e(nc.sbuf_tensor("num_sb", [128, 16 * 65], F32))
        rd16 = _e(nc.sbuf_tensor("rd16", [128, 16], F32))
        w1g0 = _e(nc.sbuf_tensor("w1g0", [128, 8 * 512], BF16))
        w1g1 = _e(nc.sbuf_tensor("w1g1", [128, 8 * 512], BF16))
        w2m0 = _e(nc.sbuf_tensor("w2m0", [128, 32 * 128], BF16))
        w2m1 = _e(nc.sbuf_tensor("w2m1", [128, 32 * 128], BF16))
        h1_sb = _e(nc.sbuf_tensor("h1_sb", [128, 32 * T1], BF16))
        rsb = _e(nc.sbuf_tensor("rsb", [1, 5 * T1], F32))
        w1g = [w1g0, w1g1]
        w2m = [w2m0, w2m1]
        pH = [pH0, pH1]
        pY = [pY0, pY1]
        t_wo = kb.dma("gpsimd", wos[:, :], kb.dram["wo"][:, :].rearrange("(kc p) e -> p kc e", p=128),
                      deps=af)
        prev = {"rA": None, "rB": None}
        rows_mean = rsb[0:1, 0:T1]
        rows_sq = rsb[0:1, T1:2 * T1]
        rows_var = rsb[0:1, 2 * T1:3 * T1]
        rows_rstd = rsb[0:1, 3 * T1:4 * T1]
        rows_mr = rsb[0:1, 4 * T1:5 * T1]

        def ln_T(yf, ybf, sqbf, gname, bname, outf, outbf, deps_in, eps=EPS1):
            tb = kb.v(lambda e: e.tensor_copy(ybf[:, :], yf[:, :]), deps=deps_in)
            tq = kb.act(sqbf[:, :], yf[:, :], AF.Square, deps=deps_in)
            m1 = None
            for mc in range(8):
                m1 = p.emit("tensor", lambda e, mc=mc: e.matmul(
                    qA[0:1, 0:T1], onesb[:, :], ybf[:, mc * T1:(mc + 1) * T1],
                    start=(mc == 0), stop=(mc == 7)), deps=[tb, prev.get("rA")])
            m2 = None
            for mc in range(8):
                m2 = p.emit("tensor", lambda e, mc=mc: e.matmul(
                    qB[0:1, 0:T1], onesb[:, :], sqbf[:, mc * T1:(mc + 1) * T1],
                    start=(mc == 0), stop=(mc == 7)), deps=[tq, prev.get("rB")])
            r1 = kb.v(lambda e: e.tensor_scalar(rows_mean, qA[0:1, 0:T1], 1.0 / D, None,
                                                op0=ALU.mult), deps=[m1])
            r2 = kb.v(lambda e: e.tensor_scalar(rows_sq, qB[0:1, 0:T1], 1.0 / D, None,
                                                op0=ALU.mult), deps=[m2])
            r3 = kb.v(lambda e: e.tensor_tensor(rows_var, rows_mean, rows_mean, op=ALU.mult),
                      deps=[r1])
            r4 = kb.v(lambda e: e.tensor_tensor(rows_var, rows_sq, rows_var, op=ALU.subtract),
                      deps=[r2, r3])
            r5 = kb.v(lambda e, eps=eps: e.tensor_scalar(rows_var, rows_var, 1.0, eps,
                                                         op0=ALU.mult, op1=ALU.add), deps=[r4])
            r6 = kb.v(lambda e: e.reciprocal(rows_var, rows_var), deps=[r5])
            r7 = kb.act(rows_rstd, rows_var, AF.Sqrt, deps=[r6])
            r8 = kb.v(lambda e: e.tensor_tensor(rows_mr, rows_mean, rows_rstd, op=ALU.mult),
                      deps=[r7])
            b1_ = p.emit("tensor", lambda e: e.matmul(qA[:, 0:T1], onesf[:, :],
                                                      rows_rstd, start=True, stop=True),
                         deps=[r7, r1])
            b2_ = p.emit("tensor", lambda e: e.matmul(qB[:, 0:T1], onesf[:, :],
                                                      rows_mr, start=True, stop=True),
                         deps=[r8, r2])
            tf = None
            for mc in range(8):
                c1 = kb.v(lambda e, mc=mc: e.tensor_tensor(
                    fscr[:, mc * T1:(mc + 1) * T1], yf[:, mc * T1:(mc + 1) * T1],
                    qA[:, 0:T1], op=ALU.mult), deps=[b1_])
                c2 = kb.v(lambda e, mc=mc: e.tensor_tensor(
                    fscr[:, mc * T1:(mc + 1) * T1], fscr[:, mc * T1:(mc + 1) * T1],
                    qB[:, 0:T1], op=ALU.subtract), deps=[b2_])
                c3 = kb.v(lambda e, mc=mc: e.tensor_scalar(
                    outf[:, mc * T1:(mc + 1) * T1], fscr[:, mc * T1:(mc + 1) * T1],
                    lnv[gname][:, mc:mc + 1], lnv[bname][:, mc:mc + 1],
                    op0=ALU.mult, op1=ALU.add), deps=[c2])
                tf = kb.v(lambda e, mc=mc: e.tensor_copy(
                    outbf[:, mc * T1:(mc + 1) * T1], outf[:, mc * T1:(mc + 1) * T1]),
                    deps=[c3])
            prev["rA"] = tf
            prev["rB"] = tf
            return tf

        st_s = t_kv
        for ti in range(nb):
            t_x32 = kb.dma("sync", xnT32[:, :].rearrange("p (kc t) -> p kc t", t=T1),
                           xn32_d[:, ti * T1:(ti + 1) * T1].rearrange("(kc p) t -> p kc t", p=128),
                           deps=[prev.get("x32free")] + (af if ti == 0 else []))
            t_q = kb.dma("sync", qTt[:, :].rearrange("p (kc t) -> p kc t", t=T1),
                         qT_dram[:, ti * T1:(ti + 1) * T1].rearrange("(kc p) t -> p kc t", p=128),
                         deps=[prev.get("qfree")] + (af if ti == 0 else []))
            t_sq = kb.dma("sync", sqcq[:, :], sqq_dram[:, ti * 64:(ti + 1) * 64],
                          deps=[prev.get("sqfree")] + (af if ti == 0 else []))
            rot = [qA, qB]
            rotdep = {0: [prev.get("rA")], 1: [prev.get("rB")]}
            for tc in range(4):
                prj = []
                for h in range(16):
                    bank = rot[h % 2]
                    lh = qTt[(h % 2) * 64:(h % 2) * 64 + 64,
                             (h // 2) * T1 + tc * 128:(h // 2) * T1 + tc * 128 + 128]
                    omx = omT[(h % 2) * 64:(h % 2) * 64 + 64, :]
                    tmm = p.emit("tensor", lambda e, lh=lh, bank=bank, omx=omx: e.matmul(
                        ap3(bank, 0, [[1, 256]]), lh, omx, start=True, stop=True),
                        deps=rotdep[h % 2] + [t_q])
                    tev = kb.v(lambda e, h=h, bank=bank: e.tensor_scalar(
                        fscr[:, h * 256:(h + 1) * 256], ap3(bank, 0, [[1, 256]]),
                        1.0, None, op0=ALU.mult, op1=ALU.max, accum_out=stb[:, h:h + 1]),
                        deps=[tmm, prev.get("fscrfree"), t_sq])
                    rotdep[h % 2] = [tev]
                    prj.append(tev)
                tcn = kb.v(lambda e, tc=tc: e.scalar_tensor_tensor(
                    cng[:, :], stb[:, :], -1.0, sqcq[:, tc * 16:(tc + 1) * 16],
                    op0=ALU.mult, op1=ALU.subtract), deps=prj + [prev.get("cngfree")])
                texp = None
                for h in range(16):
                    texp = kb.act(fscr[:, h * 256:(h + 1) * 256], fscr[:, h * 256:(h + 1) * 256],
                                  AF.Exp, bias=cng[:, h:h + 1], deps=[tcn])
                prev["cngfree"] = texp
                tpq = kb.v(lambda e: e.tensor_scalar(bscr[:, :16 * 256], fscr[:, :16 * 256],
                                                     1e-6, None, op0=ALU.add), deps=[texp])
                prev["fscrfree"] = tpq
                tevp = None
                for bq in range(4):
                    ttr = None
                    for sl in range(8):
                        idx = bq * 8 + sl
                        h_, j_ = idx // 2, idx % 2
                        src = bscr[:, h_ * 256 + j_ * 128: h_ * 256 + j_ * 128 + 128]
                        ttr = p.emit("tensor", lambda e, src=src, sl=sl: e.matmul(
                            qX[:, sl * 128:(sl + 1) * 128], src, eye16[:, :],
                            start=True, stop=True, is_transpose=True),
                            deps=[tpq, prev.get("qXfree"), ttr])
                    tevp = kb.v(lambda e, bq=bq: e.tensor_copy(
                        pqT[:, bq * 1024:(bq + 1) * 1024], qX[:, 0:1024]), deps=[ttr])
                    prev["qXfree"] = tevp
                nev = []
                for h in range(16):
                    bank = rot[h % 2]
                    tmm = p.emit("tensor", lambda e, h=h, bank=bank: e.matmul(
                        ap3(bank, 0, [[1, 65]]), pqT[:, (2 * h) * 128:(2 * h) * 128 + 128],
                        kvs(h, 0), start=True, stop=False, skip_group_check=True),
                        deps=[tevp, t_kv] + rotdep[h % 2])
                    tmm = p.emit("tensor", lambda e, h=h, bank=bank: e.matmul(
                        ap3(bank, 0, [[1, 65]]), pqT[:, (2 * h + 1) * 128:(2 * h + 1) * 128 + 128],
                        kvs(h, 1), start=False, stop=True, skip_group_check=True),
                        deps=[])
                    tev = kb.v(lambda e, h=h, bank=bank: e.tensor_copy(
                        num_sb[:, h * 65:(h + 1) * 65], ap3(bank, 0, [[1, 65]])),
                        deps=[tmm])
                    rotdep[h % 2] = [tev]
                    nev.append(tev)
                trd = kb.v(lambda e: e.reciprocal(rd16[:, :], ap3(num_sb, 64, [[65, 16]])),
                           deps=nev)
                for h in range(16):
                    kb.v(lambda e, h=h, tc=tc: e.tensor_scalar(
                        attn[:, tc * 1024 + h * 64: tc * 1024 + h * 64 + 64],
                        num_sb[:, h * 65:h * 65 + 64], rd16[:, h:h + 1], None,
                        op0=ALU.mult), deps=[trd])
            t_attn = p.fence("vector")
            tat = None
            for tc in range(4):
                for dq in range(2):
                    ttr = None
                    for jj in range(4):
                        asrc = attn[:, tc * 1024 + (dq * 4 + jj) * 128: tc * 1024 + (dq * 4 + jj) * 128 + 128]
                        ttr = p.emit("tensor", lambda e, asrc=asrc, jj=jj: e.matmul(
                            qX[:, jj * 128:(jj + 1) * 128], asrc, eye16[:, :],
                            start=True, stop=True, is_transpose=True),
                            deps=[t_attn, prev.get("qXfree"), ttr])
                    dstp = ap3(bsc2, (dq * 4) * T1 + tc * 128, [[T1, 4], [1, 128]])
                    tat = kb.v(lambda e, dstp=dstp: e.tensor_copy(dstp, ap3(qX, 0, [[128, 4], [1, 128]])),
                               deps=[ttr])
                    prev["qXfree"] = tat
            wv_ev = []
            for mc in range(8):
                bank = rot[mc % 2]
                pairs = [(wos[:, kc * D + mc * 128: kc * D + mc * 128 + 128],
                          bsc2[:, kc * T1:(kc + 1) * T1]) for kc in range(8)]
                tmm = kb.mm(ap3(bank, 0, [[1, T1]]), pairs, deps=[tat, t_wo] + rotdep[mc % 2])
                tev = kb.v(lambda e, mc=mc, bank=bank: e.tensor_tensor(
                    y1[:, mc * T1:(mc + 1) * T1], ap3(bank, 0, [[1, T1]]),
                    xnT32[:, mc * T1:(mc + 1) * T1], op=ALU.add), deps=[tmm, t_x32])
                rotdep[mc % 2] = [tev]
                wv_ev.append(tev)
            prev["rA"], prev["rB"] = rotdep[0][0], rotdep[1][0]
            prev["x32free"] = wv_ev[-1]
            t_xa = ln_T(y1, attn[:, 0:8 * T1] if False else bscr, pqT[:, 0:8 * T1] if False else bsc2,
                        "ln1g_c", "ln1b_c", xa, bscr, wv_ev)
            h_ev = []
            for m in range(32):
                g_ = m // 4
                if m % 4 == 0:
                    w1t = kb.dma("gpsimd", w1g[g_ % 2][:, :].rearrange("p (kc c) -> p kc c", c=512),
                                 w1_d[:, g_ * 512:(g_ + 1) * 512].rearrange("(kc p) c -> p kc c", p=128),
                                 deps=[prev.get(("w1gfree", g_ % 2))])
                    prev[("w1gld", g_ % 2)] = w1t
                mloc = m % 4
                wb1 = w1g[g_ % 2]
                ph = pH[m % 2]
                pairs = [(wb1[:, kc * 512 + mloc * 128: kc * 512 + mloc * 128 + 128],
                          bscr[:, kc * T1:(kc + 1) * T1]) for kc in range(8)]
                tmm = kb.mm(ap3(ph, 0, [[1, T1]]), pairs,
                            deps=[t_xa, prev.get(("w1gld", g_ % 2)), prev.get(("hfree", m % 2))])
                the = kb.act(h1_sb[:, m * T1:(m + 1) * T1], ap3(ph, 0, [[1, T1]]), AF.Gelu,
                             bias=b1c[:, m:m + 1], deps=[tmm, prev.get("h1sbfree")])
                prev[("hfree", m % 2)] = the
                h_ev.append(the)
                if m % 4 == 3:
                    prev[("w1gfree", g_ % 2)] = p.fence("tensor")
            y2ev = []
            for mp in range(8):
                w2t = kb.dma("gpsimd", w2m[mp % 2][:, :].rearrange("p (mh c) -> p mh c", c=128),
                             w2_d[:, mp * 128:(mp + 1) * 128].rearrange("(mh p) c -> p mh c", p=128),
                             deps=[prev.get(("w2mfree", mp % 2))])
                py = pY[mp % 2]
                w2b = w2m[mp % 2]

                def f2(e, py=py, w2b=w2b):
                    r = None
                    for m_ in range(32):
                        r = e.matmul(ap3(py, 0, [[1, T1]]),
                                     w2b[:, m_ * 128:(m_ + 1) * 128],
                                     h1_sb[:, m_ * T1:(m_ + 1) * T1],
                                     start=(m_ == 0), stop=(m_ == 31))
                    return r
                tmm = p.emit("tensor", f2, deps=[h_ev[-1], w2t, prev.get(("pYfree", mp % 2))])
                tev = kb.v(lambda e, mp=mp, py=py: e.scalar_tensor_tensor(
                    y1[:, mp * T1:(mp + 1) * T1], ap3(py, 0, [[1, T1]]),
                    lnv["b2_c"][:, mp:mp + 1], xa[:, mp * T1:(mp + 1) * T1],
                    op0=ALU.add, op1=ALU.add), deps=[tmm])
                prev[("pYfree", mp % 2)] = tev
                prev[("w2mfree", mp % 2)] = tev
                y2ev.append(tev)
            prev["h1sbfree"] = p.fence("tensor")
            t_s = ln_T(y1, bscr, bsc2, "ln2g_c", "ln2b_c", xa, bscr, y2ev)
            st_s = kb.dma("qdma", s_d[:, ti * T1:(ti + 1) * T1].rearrange("(kc p) t -> p kc t", p=128),
                          xa[:, :].rearrange("p (kc t) -> p kc t", t=T1), deps=[t_s])
            if dev and ti == 0:
                dtap("attn0", attn[:, :], 4096, [t_attn], BF16)
                dtap("y2_0", y1[:, :], 4096, y2ev + [st_s])
            prev["qfree"] = st_s
            prev["sqfree"] = st_s
            prev["sfree"] = st_s
    es2.close()
    return st_s


def _conv_w_c(w, nch):
    K = w.shape[1]
    return np.ascontiguousarray(w.reshape(nch // 128, 128, K).transpose(1, 0, 2).reshape(128, -1)).astype(np.float32)


def _vec_c(b):
    return np.ascontiguousarray(b.reshape(-1, 128).T).astype(np.float32)


def make_l0_inputs(inputs, core):
    b, h = core // 2, core % 2
    x = np.asarray(inputs["L0_attn_u"], np.float32)
    t0 = h * NTOK
    if h == 0:
        xin = np.concatenate([np.zeros((HALO, D), np.float32), x[b, 0:NTOK]], axis=0)
    else:
        xin = x[b, t0 - HALO:t0 + NTOK]
    m = {
        "x": np.ascontiguousarray(xin),
        "w_in": np.asarray(inputs["L0_attn_W_in"], np.float32),
        "w_out": np.asarray(inputs["L0_attn_W_out"], np.float32),
        "fw": np.asarray(inputs["L0_ffn_weight"], np.float32),
        "b_in_c": _vec_c(np.asarray(inputs["L0_attn_b_in"], np.float32)),
        "short_w_c": _conv_w_c(np.asarray(inputs["L0_attn_short_w"], np.float32), D3),
        "short_b_c": _vec_c(np.asarray(inputs["L0_attn_short_b"], np.float32)),
        "inner_w_c": _conv_w_c(np.asarray(inputs["L0_attn_inner_w"], np.float32), D),
        "inner_b_c": _vec_c(np.asarray(inputs["L0_attn_inner_b"], np.float32)),
        "b_out_c": _vec_c(np.asarray(inputs["L0_attn_b_out"], np.float32)),
        "fb_c": _vec_c(np.asarray(inputs["L0_ffn_bias"], np.float32)),
        "eye32": np.eye(128, dtype=np.float32),
        "eye16": np.eye(128).astype(ml_dtypes.bfloat16),
        "hmask": np.full((128, 1), 0.0 if h == 0 else 1.0, np.float32),
    }
    return m


def build_moe(kb, dev=False):
    nc, p = kb.nc, kb.p
    fused = kb.cfg.get("fused", False)
    es3 = contextlib.ExitStack()
    es3sb = lambda n_, sh, dt_: es3.enter_context(nc.sbuf_tensor('l3_' + n_, list(sh), dt_))
    if fused:
        xn_d = kb.dram["xn1t16_d"]
        xn32_d = kb.dram["xn1t32_d"]
        x2_d = kb.dram["x2_d"]
        s_d = kb.dram["s_d"]
    else:
        xn_d = kb.dt("xn16", [D, NTOK], BF16, "ExternalInput")
        xn32_d = kb.dt("xn32m", [D, NTOK], F32, "ExternalInput")
        x2_d = kb.dt("x2", [NTOK, D], F32, "ExternalInput")
        s_d = kb.dt("s_in", [D, NTOK], F32, "ExternalInput")
    wgT_d = kb.dt("wgT", [D, 8], F32, "ExternalInput")
    w1_d = kb.dt("mw1", [8 * D, 4 * D], F32, "ExternalInput")
    w2_d = kb.dt("mw2", [8 * 4 * D, D], F32, "ExternalInput")
    b1_d = kb.dt("mb1_c", [128, 8 * 32], F32, "ExternalInput")
    b2_d = kb.dt("mb2e", [8, D], F32, "ExternalInput")
    eye32_d = kb.dt("eye32", [128, 128], F32, "ExternalInput")
    x3_d = kb.dt("x3", [NTOK, D], F32, "ExternalOutput")
    acc_d = kb.dt("acc_sc", [NTOK, D], F32)

    wgs = es3sb("wgs", [128, 8 * 8], F32)
    b2sb = es3sb("b2sb", [8, D], F32)
    b1c = es3sb("mb1c_s", [128, 8 * 32], F32)
    eye32 = es3sb("eye32_s", [128, 128], F32)
    cw = es3sb("cw", [128, 32 * 8], F32)
    gts = es3sb("gts", [128, 32], F32)
    e8 = es3sb("e8", [128, 8], F32)
    pm8 = es3sb("pm8", [128, 8], F32)
    m81 = es3sb("m81", [128, 8], F32)
    m82 = es3sb("m82", [128, 8], F32)
    gcol = es3sb("gcol", [128, 4], F32)
    ystage = es3sb("ystage", [128, 4 * D], F32)
    pdep3 = list(kb.cfg.get("phase_dep_l3", []))
    kb.dma("sync", wgs[:, :], wgT_d[:, :].rearrange("(kc p) e -> p kc e", p=128), deps=pdep3)
    kb.dma("sync", b2sb[:, :], b2_d[:, :])
    kb.dma("sync", b1c[:, :], b1_d[:, :])
    kb.dma("sync", eye32[:, :], eye32_d[:, :])
    t_w = [p.fence("gdma"), p.fence("sdma")] + pdep3
    accdma = {}

    with nc.psum_tensor("mH0", [128, 512], F32) as mH0, \
         nc.psum_tensor("mH1", [128, 512], F32) as mH1, \
         nc.psum_tensor("mY0", [128, 512], F32) as mY0, \
         nc.psum_tensor("mY1", [128, 512], F32) as mY1, \
         nc.psum_tensor("mY2", [128, 512], F32) as mY2, \
         nc.psum_tensor("mY3", [128, 512], F32) as mY3, \
         nc.sbuf_tensor("xg0", [128, 8 * 512], BF16) as xg0, \
         nc.sbuf_tensor("xg1", [128, 8 * 512], BF16) as xg1:
        pH = [mH0, mH1]
        pY = [mY0, mY1, mY2, mY3]
        xg = [xg0, xg1]
        prev = {}
        es_g = contextlib.ExitStack()
        cwT = es_g.enter_context(nc.sbuf_tensor("cwT", [8, 32 * 128], F32))
        grow = es_g.enter_context(nc.sbuf_tensor("grow", [8, 512], F32))
        xg32 = es_g.enter_context(nc.sbuf_tensor("xg32", [128, 8 * 512], F32))
        for tg in range(8):
            t_x = kb.dma("sync", xg32[:, :].rearrange("p (kc t) -> p kc t", t=512),
                         xn32_d[:, tg * 512:(tg + 1) * 512].rearrange("(kc p) t -> p kc t", p=128),
                         deps=t_w + [prev.get("xg32free")])
            gmm = None
            for kc in range(8):
                gmm = p.emit("tensor", lambda e, kc=kc: e.matmul(
                    mH0[0:8, 0:512], wgs[:, kc * 8:(kc + 1) * 8],
                    xg32[:, kc * 512:(kc + 1) * 512], start=(kc == 0), stop=(kc == 7)),
                    deps=[t_x, prev.get("growfree")])
            tg1 = kb.v(lambda e: e.tensor_copy(grow[:, :], mH0[0:8, 0:512]), deps=[gmm])
            prev["growfree"] = tg1
            ttr = None
            for j in range(4):
                ttr = p.emit("tensor", lambda e, j=j: e.matmul(
                    mH1[:, j * 8:(j + 1) * 8], grow[:, j * 128:(j + 1) * 128],
                    eye32[0:8, 0:8], start=True, stop=True, is_transpose=True),
                    deps=[tg1, prev.get("mh1free"), ttr])
            tg2 = kb.v(lambda e: e.tensor_copy(gts[:, :], mH1[:, 0:32]), deps=[ttr])
            prev["mh1free"] = tg2
            for c in range(4):
                gch = tg * 4 + c
                sl = gts[:, c * 8:(c + 1) * 8]
                cws = cw[:, gch * 8:(gch + 1) * 8]
                t1 = kb.v(lambda e, sl=sl: e.tensor_reduce(gcol[:, 0:1], sl, axis=AX.X,
                                                           op=ALU.max, negate=True), deps=[tg2])
                t2 = kb.act(e8[:, :], sl, AF.Exp, bias=gcol[:, 0:1], deps=[t1])
                t3 = kb.v(lambda e: e.tensor_reduce(gcol[:, 1:2], e8[:, :], axis=AX.X,
                                                    op=ALU.add), deps=[t2])
                t4 = kb.v(lambda e: e.reciprocal(gcol[:, 1:2], gcol[:, 1:2]), deps=[t3])
                t5 = kb.v(lambda e, cws=cws: e.tensor_scalar(cws, e8[:, :], gcol[:, 1:2],
                                                             None, op0=ALU.mult), deps=[t4])
                t6 = kb.v(lambda e, cws=cws: e.tensor_reduce(gcol[:, 2:3], cws, axis=AX.X,
                                                             op=ALU.max), deps=[t5])
                t7 = kb.v(lambda e, cws=cws: e.tensor_scalar(m81[:, :], cws, gcol[:, 2:3],
                                                             None, op0=ALU.is_ge), deps=[t6])
                t8 = kb.v(lambda e: e.tensor_scalar(m82[:, :], m81[:, :], -1.0, 1.0,
                                                    op0=ALU.mult, op1=ALU.add), deps=[t7])
                t9 = kb.v(lambda e, cws=cws: e.tensor_tensor(pm8[:, :], cws, m82[:, :],
                                                             op=ALU.mult), deps=[t8])
                t10 = kb.v(lambda e: e.tensor_reduce(gcol[:, 3:4], pm8[:, :], axis=AX.X,
                                                     op=ALU.max), deps=[t9])
                t11 = kb.v(lambda e: e.tensor_scalar(m82[:, :], pm8[:, :], gcol[:, 3:4],
                                                     None, op0=ALU.is_ge), deps=[t10])
                t12 = kb.v(lambda e: e.tensor_tensor(m81[:, :], m81[:, :], m82[:, :],
                                                     op=ALU.add), deps=[t11])
                t13 = kb.v(lambda e, cws=cws: e.tensor_tensor(cws, cws, m81[:, :],
                                                              op=ALU.mult), deps=[t12])
                ttr2 = p.emit("tensor", lambda e, cws=cws: e.matmul(
                    mH1[0:8, 40:168], cws, eye32[:, :], start=True, stop=True,
                    is_transpose=True), deps=[t13, prev.get("cwtfree")])
                tcv = kb.v(lambda e, gch=gch: e.tensor_copy(
                    cwT[:, gch * 128:(gch + 1) * 128], mH1[0:8, 40:168]), deps=[ttr2])
                prev["cwtfree"] = tcv
                bmm = p.emit("tensor", lambda e, gch=gch, c=c: e.matmul(
                    pY[c % 2][:, 0:512], cwT[0:8, gch * 128:(gch + 1) * 128],
                    b2sb[:, 0:512], start=True, stop=True),
                    deps=[tcv, prev.get(("pyfree", c % 2))])
                bmm2 = p.emit("tensor", lambda e, gch=gch, c=c: e.matmul(
                    pY[2 + c % 2][:, 0:512], cwT[0:8, gch * 128:(gch + 1) * 128],
                    b2sb[:, 512:1024], start=True, stop=True),
                    deps=[])
                ev1 = kb.v(lambda e, c=c: e.tensor_copy(ystage[:, c * D:c * D + 512],
                                                        pY[c % 2][:, 0:512]), deps=[bmm, prev.get("ystagefree")])
                ev2 = kb.v(lambda e, c=c: e.tensor_copy(ystage[:, c * D + 512:(c + 1) * D],
                                                        pY[2 + c % 2][:, 0:512]), deps=[bmm2])
                prev[("pyfree", c % 2)] = ev1
                prev[("pyfree", 2 + c % 2)] = ev2
                prev["mh1free"] = tcv
            tini = kb.dma("gpsimd", acc_d[tg * 512:(tg + 1) * 512, :].rearrange("(c p) d -> p c d", p=128),
                          ystage[:, :].rearrange("p (c d) -> p c d", d=D),
                          deps=[prev[("pyfree", 0)], prev[("pyfree", 1)],
                                prev[("pyfree", 2)], prev[("pyfree", 3)]])
            accdma[tg] = tini
            prev["ystagefree"] = tini
            prev["xg32free"] = p.fence("tensor")
        prev[("hfree", 0)] = prev[("hfree", 1)] = p.fence("vector")
        if dev:
            acc0_d = kb.dt("acc0_dev", [NTOK, D], F32, "ExternalOutput")
            cw_d = kb.dt("cw_dev", [128, 256], F32, "ExternalOutput")
            kb.dma("qdma", cw_d[:, :], cw[:, :], deps=[p.fence("vector")])
            for tg in range(8):
                snap = kb.dma("qdma", acc0_d[tg * 512:(tg + 1) * 512, :],
                              acc_d[tg * 512:(tg + 1) * 512, :], deps=[accdma[tg]])
                accdma[tg] = snap
        es_g.close()
        with nc.sbuf_tensor("w1e", [128, 8 * 4 * D], BF16) as w1e, \
             nc.sbuf_tensor("w2e", [128, 32 * D], BF16) as w2e, \
             nc.sbuf_tensor("h1m", [128, 32 * 512], BF16) as h1m:
            for ex in range(8):
                pef = prev.get("expf")
                tw1 = [kb.dma("gpsimd",
                              ap3(w1e, half * 4 * 4 * D, [[4 * D, 4], [1, 4 * D]]),
                              w1_d[ex * D + half * 512:ex * D + (half + 1) * 512, :]
                              .rearrange("(kc p) m -> p kc m", p=128),
                              deps=[pef]) for half in range(2)]
                tw2 = [kb.dma("gpsimd",
                              ap3(w2e, half * 16 * D, [[D, 16], [1, D]]),
                              w2_d[ex * 4 * D + half * 2048:ex * 4 * D + (half + 1) * 2048, :]
                              .rearrange("(kc p) d -> p kc d", p=128),
                              deps=[pef]) for half in range(2)]
                for t_ in range(8):
                    xb = xg[t_ % 2]
                    t_x = kb.dma("sync", xb[:, :].rearrange("p (kc t) -> p kc t", t=512),
                                 xn_d[:, t_ * 512:(t_ + 1) * 512].rearrange("(kc p) t -> p kc t", p=128),
                                 deps=[prev.get(("xfree", t_ % 2))])
                    hev = []
                    for m in range(32):
                        ph = pH[m % 2]
                        pairs = [(w1e[:, kc * 4 * D + m * 128: kc * 4 * D + m * 128 + 128],
                                  xb[:, kc * 512:(kc + 1) * 512]) for kc in range(8)]
                        tmm = kb.mm(ap3(ph, 0, [[1, 512]]), pairs,
                                    deps=[t_x, tw1[0], tw1[1], prev.get(("hfree", m % 2))])
                        the = kb.act(h1m[:, m * 512:(m + 1) * 512], ap3(ph, 0, [[1, 512]]),
                                     AF.Gelu, bias=b1c[:, ex * 32 + m:ex * 32 + m + 1],
                                     deps=[tmm, prev.get("h1mfree")])
                        prev[("hfree", m % 2)] = the
                        hev.append(the)
                    for tc in range(4):
                        for dh in range(2):
                            py = pY[2 * (tc % 2) + dh]

                            def f2(e, tc=tc, dh=dh, py=py):
                                r = None
                                for kc in range(32):
                                    r = e.matmul(ap3(py, 0, [[1, 512]]),
                                                 h1m[:, kc * 512 + tc * 128: kc * 512 + tc * 128 + 128],
                                                 w2e[:, kc * D + dh * 512: kc * D + dh * 512 + 512],
                                                 start=(kc == 0), stop=(kc == 31))
                                return r
                            tmm = p.emit("tensor", f2,
                                         deps=[hev[-1], tw2[0], tw2[1], prev.get(("pyfree", 2 * (tc % 2) + dh))])
                            tev = kb.v(lambda e, tc=tc, dh=dh, py=py, t_=t_, ex=ex: e.tensor_scalar(
                                ystage[:, tc * D + dh * 512: tc * D + (dh + 1) * 512],
                                ap3(py, 0, [[1, 512]]),
                                cw[:, (t_ * 4 + tc) * 8 + ex:(t_ * 4 + tc) * 8 + ex + 1],
                                None, op0=ALU.mult),
                                deps=[tmm, prev.get("ystagefree")])
                            prev[("pyfree", 2 * (tc % 2) + dh)] = tev
                    if dev and ex == 0 and t_ == 0:
                        ys_d = kb.dt("ys00_dev", [128, 4 * D], F32, "ExternalOutput")
                        h1_d = kb.dt("h1m_dev", [128, 4 * 512], F32, "ExternalOutput")
                        kb.dma("qdma", ys_d[:, :], ystage[:, :],
                               deps=[prev[("pyfree", 0)], prev[("pyfree", 1)],
                                     prev[("pyfree", 2)], prev[("pyfree", 3)]])
                        tqh = kb.v(lambda e: e.tensor_copy(ystage[:, 0:0 + 2048] if False else ystage[:, 0:2048], h1m[:, 0:2048]), deps=[hev[-1]]) if False else None
                        h1tap = kb.dt("h1tap", [128, 2048], BF16, "ExternalOutput")
                        kb.dma("qdma", h1tap[:, :], h1m[:, 0:2048], deps=[hev[-1]])
                    tacc = kb.dma("gpsimd", acc_d[t_ * 512:(t_ + 1) * 512, :].rearrange("(c p) d -> p c d", p=128),
                                  ystage[:, :].rearrange("p (c d) -> p c d", d=D),
                                  deps=[accdma.get(t_), prev[("pyfree", 0)], prev[("pyfree", 1)],
                                        prev[("pyfree", 2)], prev[("pyfree", 3)]] +
                                       ([p.fence("qdma")] if (dev and ex == 0 and t_ == 0) else []), accum=True)
                    accdma[t_] = tacc
                    prev["ystagefree"] = tacc
                    prev["h1mfree"] = p.fence("tensor")
                    prev[("xfree", t_ % 2)] = prev["h1mfree"]
                prev["expf"] = p.fence("tensor")
        with nc.sbuf_tensor("sT", [128, 8 * 512], F32) as sT, \
             nc.sbuf_tensor("stok", [128, 4 * D], F32) as stok, \
             nc.sbuf_tensor("x2t", [128, 4 * D], F32) as x2t:
            last = None
            for t_ in range(8):
                t_acc = kb.dma("sync", ystage[:, :].rearrange("p (c d) -> p c d", d=D),
                               acc_d[t_ * 512:(t_ + 1) * 512, :].rearrange("(c p) d -> p c d", p=128),
                               deps=[accdma[t_], prev.get("ystagefree"), prev.get("ystagefree2"), last])
                t_s = kb.dma("sync", sT[:, :].rearrange("p (kc t) -> p kc t", t=512),
                             s_d[:, t_ * 512:(t_ + 1) * 512].rearrange("(kc p) t -> p kc t", p=128),
                             deps=[prev.get("sTfree")])
                t_x2 = kb.dma("sync", x2t[:, :].rearrange("p (c d) -> p c d", d=D),
                              x2_d[t_ * 512:(t_ + 1) * 512, :].rearrange("(c p) d -> p c d", p=128),
                              deps=[prev.get("x2tfree")])
                tg_ = kb.act(ystage[:, :], ystage[:, :], AF.Silu, deps=[t_acc])
                trd = prev.get("ph0free")
                hguard = [prev.get(("hfree", 0)), prev.get(("hfree", 1))]
                for tc in range(4):
                    for dq in range(2):
                        ttr = None
                        for j in range(4):
                            ssrc = sT[:, (dq * 4 + j) * 512 + tc * 128: (dq * 4 + j) * 512 + tc * 128 + 128]
                            ttr = p.emit("tensor", lambda e, ssrc=ssrc, j=j: e.matmul(
                                mH0[:, j * 128:(j + 1) * 128], ssrc, eye32[:, :],
                                start=True, stop=True, is_transpose=True),
                                deps=[t_s, trd, ttr] + hguard)
                        dstp = ap3(stok, tc * D + dq * 4 * 128, [[128, 4], [1, 128]])
                        trd = kb.v(lambda e, dstp=dstp: e.tensor_copy(
                            dstp, ap3(mH0, 0, [[128, 4], [1, 128]])), deps=[ttr])
                prev["ph0free"] = trd
                tm1 = kb.v(lambda e: e.tensor_tensor(ystage[:, :], ystage[:, :], stok[:, :],
                                                     op=ALU.mult), deps=[tg_, trd])
                tm2 = kb.v(lambda e: e.tensor_tensor(x2t[:, :], x2t[:, :], ystage[:, :],
                                                     op=ALU.add), deps=[tm1, t_x2])
                last = kb.dma("qdma", x3_d[t_ * 512:(t_ + 1) * 512, :].rearrange("(c p) d -> p c d", p=128),
                              x2t[:, :].rearrange("p (c d) -> p c d", d=D), deps=[tm2])
                prev["ystagefree2"] = tm2
                prev["sTfree"] = last
                prev["x2tfree"] = last
    es3.close()
    return last



def make_e16f():
    e = np.zeros((128, 128), np.float32)
    for mc in range(8):
        for p_ in range(128):
            e[p_, mc * 16 + 2 * mc + p_ // 64] = 1.0
    return e


def make_l2_inputs(inputs, core, xn1t16_all, xn1t32_all):
    b, h = core // 2, core % 2
    m = {}
    if xn1t16_all[core] is not None:
        own = np.asarray(xn1t16_all[core])
        partner = np.asarray(xn1t16_all[2 * b + (1 - h)])
        m["xnf"] = np.ascontiguousarray(np.concatenate([own, partner], axis=1))
        m["xn32"] = np.ascontiguousarray(np.asarray(xn1t32_all[core]))
    m.update({
        "wq": np.ascontiguousarray(np.asarray(inputs["L1_attn_wq"], np.float32).reshape(D, D)),
        "wk": np.ascontiguousarray(np.asarray(inputs["L1_attn_wk"], np.float32).reshape(D, D)),
        "wv": np.ascontiguousarray(np.asarray(inputs["L1_attn_wv"], np.float32).reshape(D, D)),
        "wo": np.asarray(inputs["L1_attn_wo"], np.float32),
        "omT": np.ascontiguousarray(np.concatenate([np.asarray(inputs["L1_attn_omega"], np.float32).T] * 2, axis=0)),
        "e16f": make_e16f(),
        "ln1g_c": _vec_c(np.asarray(inputs["L1_attn_ln1_g"], np.float32)),
        "ln1b_c": _vec_c(np.asarray(inputs["L1_attn_ln1_b"], np.float32)),
        "ln2g_c": _vec_c(np.asarray(inputs["L1_attn_ln2_g"], np.float32)),
        "ln2b_c": _vec_c(np.asarray(inputs["L1_attn_ln2_b"], np.float32)),
        "b1_c": _vec_c(np.asarray(inputs["L1_attn_b1"], np.float32)),
        "b2_c": _vec_c(np.asarray(inputs["L1_attn_b2"], np.float32)),
        "w1": np.asarray(inputs["L1_attn_w1"], np.float32),
        "w2": np.asarray(inputs["L1_attn_w2"], np.float32),
        "eye16": np.eye(128).astype(ml_dtypes.bfloat16),
        "eye32": np.eye(128, dtype=np.float32),
        "onesb": np.ones((128, 1), ml_dtypes.bfloat16),
        "onesf": np.ones((1, 128), np.float32),
    })
    return m


def build_l2_program(dev=False, na=NT1, nb=8, skip=(), io=False):
    kb = KB({"dev": dev, "skip": set(skip), "io": io})
    with kb.ctx:
        build_l1attn(kb, dev=dev, na=na, nb=nb)
        kb.p.finalize(final_waits=[kb.p.fence(e) for e in Prog.ENGINES])
    return kb


def make_l3_inputs(inputs, core, xn1t16_all, x2_all, s_all, x32m_all):
    b1 = np.asarray(inputs["L1_ffn_b1"], np.float32)
    m = {}
    if xn1t16_all[core] is not None:
        m["xn16"] = np.ascontiguousarray(np.asarray(xn1t16_all[core]))
        m["xn32m"] = np.ascontiguousarray(np.asarray(x32m_all[core], np.float32))
        m["x2"] = np.ascontiguousarray(np.asarray(x2_all[core], np.float32))
        m["s_in"] = np.ascontiguousarray(np.asarray(s_all[core], np.float32))
    m.update({
        "wgT": np.ascontiguousarray(np.asarray(inputs["L1_ffn_w_gate"], np.float32).T),
        "mw1": np.ascontiguousarray(np.asarray(inputs["L1_ffn_W1"], np.float32).reshape(8 * D, 4 * D)),
        "mw2": np.ascontiguousarray(np.asarray(inputs["L1_ffn_W2"], np.float32).reshape(8 * 4 * D, D)),
        "mb1_c": np.concatenate([_vec_c(b1[e]) for e in range(8)], axis=1),
        "mb2e": np.asarray(inputs["L1_ffn_b2"], np.float32),
        "eye32": np.eye(128, dtype=np.float32),
    })
    return m


def build_l3_program(dev=False):
    kb = KB({"dev": dev})
    with kb.ctx:
        build_moe(kb, dev=dev)
        kb.p.finalize(final_waits=[kb.p.fence(e) for e in Prog.ENGINES])
    return kb


def build_l0_program(dev=False, io=False):
    kb = KB({"dev": dev, "io": io})
    with kb.ctx:
        last = build_layer0(kb, dev=dev)
        kb.p.finalize(final_waits=[kb.p.fence(e) for e in Prog.ENGINES])
    return kb


def build_fused_program(dev=False):
    kb = KB({"fused": True, "io": dev})
    with kb.ctx:
        build_layer0(kb, dev=False)
        kb.cfg["phase_dep_l2"] = [kb.p.fence(s_) for s_ in Prog.STREAMS]
        build_l1attn(kb, dev=False, na=8)
        kb.cfg["phase_dep_l3"] = [kb.p.fence(s_) for s_ in Prog.STREAMS]
        build_moe(kb, dev=False)
        kb.p.finalize(final_waits=[kb.p.fence(e) for e in Prog.ENGINES])
    return kb


def make_fused_inputs(inputs, core):
    m = make_l0_inputs(inputs, core)
    m.update(make_l2_inputs(inputs, core, [None] * 8, [None] * 8))
    m.update(make_l3_inputs(inputs, core, [None] * 8, [None] * 8, [None] * 8, [None] * 8))
    return m


def kernel(**inputs):
    cores = list(range(NCORE))
    kb = build_fused_program()
    res = run_bass_kernel_spmd(kb.nc, [make_fused_inputs(inputs, c) for c in cores],
                               core_ids=cores)
    out = np.zeros((B, L, D), np.float32)
    for c in cores:
        b, h = c // 2, c % 2
        out[b, h * NTOK:(h + 1) * NTOK] = np.asarray(res.results[c]["x3"])
    return out


def kernel_unfused(**inputs):
    cores = list(range(NCORE))
    kb0 = build_l0_program(io=True)
    res0 = run_bass_kernel_spmd(kb0.nc, [make_l0_inputs(inputs, c) for c in cores],
                                core_ids=cores)
    x2_all = [np.asarray(res0.results[c]["x2_d"]) for c in cores]
    x32_all = [np.asarray(res0.results[c]["xn1t32_d"]) for c in cores]
    x16_all = [np.asarray(res0.results[c]["xn1t16_d"]) for c in cores]
    kb2 = build_l2_program(io=True)
    res2 = run_bass_kernel_spmd(kb2.nc, [make_l2_inputs(inputs, c, x16_all, x32_all) for c in cores],
                                core_ids=cores)
    s_all = [np.asarray(res2.results[c]["s_d"]) for c in cores]
    kb3 = build_l3_program()
    res3 = run_bass_kernel_spmd(kb3.nc, [make_l3_inputs(inputs, c, x16_all, x2_all, s_all, x32_all)
                                         for c in cores], core_ids=cores)
    out = np.zeros((B, L, D), np.float32)
    for c in cores:
        b, h = c // 2, c % 2
        out[b, h * NTOK:(h + 1) * NTOK] = np.asarray(res3.results[c]["x3"])
    return out


# version 06
# speedup vs baseline: 1.3561x; 1.1357x over previous
import contextlib
import numpy as np
import ml_dtypes
import concourse.bass as bass
import concourse.mybir as mybir
from concourse.bass_utils import run_bass_kernel_spmd

F32 = mybir.dt.float32
BF16 = mybir.dt.bfloat16
I32 = mybir.dt.int32
I16 = mybir.dt.int16
AF = mybir.ActivationFunctionType
ALU = mybir.AluOpType
AX = mybir.AxisListType

B, L, D = 4, 8192, 1024
NCORE = 8
NTOK = 4096
HALO = 16
T0 = 256
NT0 = NTOK // T0
TC0 = T0 // 128
D3 = 3072
LN_EPS = 1e-5


def bf(x):
    return np.ascontiguousarray(x).astype(ml_dtypes.bfloat16)


class Prog:
    STREAMS = {"tensor": "tensor", "vector": "vector", "scalar": "scalar",
               "gpsimd": "gpsimd", "sdma": "sync", "qdma": "scalar",
               "gdma": "gpsimd"}
    COMPUTE = ("vector", "scalar", "gpsimd")
    ENGINES = tuple(STREAMS)

    def __init__(self, nc, ctx):
        self.nc = nc
        self.steps = []
        self.cnt = {s: 0 for s in self.STREAMS}
        self.sems = {s: ctx.enter_context(nc.semaphore("sem_" + s)) for s in self.STREAMS}
        self.last_wait = {}

    def emit(self, stream, fn, deps=(), dk=1):
        dl = []
        is_comp = stream in self.COMPUTE
        for t in deps:
            if t is None:
                continue
            s2, v = t
            if v <= 0:
                continue
            if s2 == stream and (is_comp or stream == "tensor"):
                continue
            k = (stream, s2)
            if self.last_wait.get(k, 0) >= v:
                continue
            self.last_wait[k] = v
            dl.append((s2, v))
        selfw = self.cnt[stream] if is_comp else None
        self.cnt[stream] += dk
        tok = (stream, self.cnt[stream])
        sems = self.sems

        def step(e, fn=fn, dl=dl, stream=stream, dk=dk, selfw=selfw):
            if selfw:
                e.wait_ge(sems[stream], selfw)
            for s2, v in dl:
                e.wait_ge(sems[s2], v)
            r = fn(e)
            r.then_inc(sems[stream], dk)

        self.steps.append((self.STREAMS[stream], step))
        return tok

    def fence(self, stream):
        return (stream, self.cnt[stream])

    def finalize(self, final_waits=()):
        sems = self.sems

        def fin(e):
            for s2, v in final_waits:
                e.wait_ge(sems[s2], v)
        self.steps.append(("sync", fin))
        with self.nc.Block() as block:
            for en in ("sync", "tensor", "vector", "scalar", "gpsimd"):
                def mk(en):
                    def _(e):
                        for e2, st in self.steps:
                            if e2 == en:
                                st(e)
                    return _
                getattr(block, en)(mk(en))


def ap3(t, off, dims):
    a = t.ap()
    return bass.AP(a.tensor, a.offset + off, [a.ap[0]] + [list(d) for d in dims])


class KB:
    def __init__(self, cfg):
        self.cfg = cfg
        self.nc = bass.Bass("TRN2", target_bir_lowering=False)
        self.ctx = contextlib.ExitStack()
        self.p = Prog(self.nc, self.ctx)
        self.dram = {}
        self.taps = {}

    def dt(self, name, shape, dtype, kind=None):
        if name in self.dram:
            return self.dram[name]
        t = self.nc.dram_tensor(name, list(shape), dtype, **({"kind": kind} if kind else {}))
        self.dram[name] = t
        return t

    def sb(self, name, shape, dtype):
        return self.ctx.enter_context(self.nc.sbuf_tensor(name, list(shape), dtype))

    def ps(self, name, shape, dtype):
        return self.ctx.enter_context(self.nc.psum_tensor(name, list(shape), dtype))

    def mm(self, psum_ap, pairs, deps=(), tp=False):
        def fn(e):
            r = None
            n = len(pairs)
            for i, (l, rh) in enumerate(pairs):
                r = e.matmul(psum_ap, l, rh, start=(i == 0), stop=(i == n - 1),
                             is_transpose=tp or None)
            return r
        return self.p.emit("tensor", fn, deps=deps)

    def trans(self, psum_t, items, eye, deps=()):
        def fn(e):
            r = None
            for src, off, ncols in items:
                dst = ap3(psum_t, off, [[1, ncols]])
                r = e.matmul(dst, src, eye, start=True, stop=True, is_transpose=True)
            return r
        return self.p.emit("tensor", fn, deps=deps)

    def act(self, dst, src, func, bias=None, scale=1.0, deps=(), accum=None):
        def fn(e):
            kw = {}
            if bias is not None:
                kw["bias"] = bias
            if accum is not None:
                kw["accum_out"] = accum
            return e.activation(dst, src, func, scale=scale, **kw)
        return self.p.emit("scalar", fn, deps=deps)

    def v(self, fn, deps=()):
        return self.p.emit("vector", fn, deps=deps)

    DMAQ = {"sync": "sdma", "scalar": "qdma", "gpsimd": "gdma"}

    def dma(self, q, dst, src, deps=(), accum=False):
        if accum:
            return self.p.emit(self.DMAQ.get(q, q),
                               lambda e: e.dma_start(dst, src, accum_op=ALU.add),
                               deps=deps, dk=16)
        return self.p.emit(self.DMAQ.get(q, q), lambda e: e.dma_start(dst, src),
                           deps=deps, dk=16)

    def tap(self, name, cols_per_tile, ntiles, dtype=F32):
        if self.cfg.get("dev") and name not in self.taps:
            self.taps[name] = self.dt("tap_" + name, [128, cols_per_tile * ntiles], dtype, "ExternalOutput")
        return self.taps.get(name)

    def tap_dma(self, name, k, sbuf_ap, cols, deps):
        t = self.taps.get(name)
        if t is not None:
            self.dma("scalar", t[:, k * cols:(k + 1) * cols], sbuf_ap, deps=deps)

    def ln_tok(self, x_ap3, nsub, width, sqbuf, stats, scol, out3, eps, deps):
        p = self.p
        ssum = stats[:, scol:scol + nsub]
        s2 = stats[:, scol + nsub:scol + 2 * nsub]
        mean = stats[:, scol + 2 * nsub:scol + 3 * nsub]
        var = stats[:, scol + 3 * nsub:scol + 4 * nsub]
        rstd = stats[:, scol + 4 * nsub:scol + 5 * nsub]
        nmean = stats[:, scol + 5 * nsub:scol + 6 * nsub]
        sq3 = sqbuf.rearrange("p (c d) -> p c d", d=width)
        t_sq = self.act(sqbuf, x_ap3, AF.Square, deps=deps)
        t1 = self.v(lambda e: e.tensor_reduce(ssum, x_ap3, axis=AX.X, op=ALU.add), deps=deps)
        t2 = self.v(lambda e: e.tensor_reduce(s2, sq3, axis=AX.X, op=ALU.add), deps=[t_sq])
        t3 = self.v(lambda e: e.tensor_scalar(mean, ssum, 1.0 / width, None, op0=ALU.mult), deps=[t1])
        t4 = self.v(lambda e: e.tensor_scalar(var, s2, 1.0 / width, None, op0=ALU.mult), deps=[t2])
        t5 = self.v(lambda e: e.tensor_tensor(nmean, mean, mean, op=ALU.mult), deps=[t3])
        t6 = self.v(lambda e: e.tensor_tensor(var, var, nmean, op=ALU.subtract), deps=[t4, t5])
        t7 = self.v(lambda e: e.tensor_scalar(var, var, 1.0, eps, op0=ALU.mult, op1=ALU.add), deps=[t6])
        t8 = self.v(lambda e: e.reciprocal(var, var), deps=[t7])
        t9 = self.act(rstd, var, AF.Sqrt, deps=[t8])
        t10 = self.v(lambda e: e.tensor_scalar(nmean, mean, -1.0, None, op0=ALU.mult), deps=[t5])
        toks = []
        for c in range(nsub):
            tk = self.v(lambda e, c=c: e.tensor_scalar(
                out3[:, c], x_ap3[:, c], nmean[:, c:c + 1], rstd[:, c:c + 1],
                op0=ALU.add, op1=ALU.mult), deps=[t9, t10])
            toks.append(tk)
        return toks[-1], t_sq

    def quant(self, xn3, nblk, xs, axs, ti, q_g2, q_g4, q_m, q_s, qsc, xd, deps):
        n = nblk * 32
        amax = qsc[:, 0:nblk]
        scale = qsc[:, nblk:2 * nblk]
        rscale = qsc[:, 2 * nblk:3 * nblk]
        xsv = xs[:, 0:n].rearrange("p (b k) -> p b k", k=32)
        t1 = self.v(lambda e: e.tensor_reduce(amax, xn3, axis=AX.X, op=ALU.max,
                                              apply_absolute_value=True), deps=deps)
        t2 = self.v(lambda e: e.tensor_scalar(scale, amax, 1.0 / 7.5, 1e-30,
                                              op0=ALU.mult, op1=ALU.max), deps=[t1])
        t3 = self.v(lambda e: e.reciprocal(rscale, scale), deps=[t2])
        rsb = ap3(qsc, 2 * nblk, [[1, nblk], [0, 32]])
        t4 = self.v(lambda e: e.tensor_tensor(xsv, xn3, rsb, op=ALU.mult), deps=[t3])
        t5 = self.v(lambda e: e.scalar_tensor_tensor(axs[:, 0:n], xs[:, 0:n], -1.0, xs[:, 0:n],
                                                     op0=ALU.mult, op1=ALU.max), deps=[t4])
        tg2 = self.v(lambda e: e.tensor_scalar(q_g2[:, 0:n], axs[:, 0:n], 2.0, None, op0=ALU.is_ge), deps=[t5])
        tg4 = self.v(lambda e: e.tensor_scalar(q_g4[:, 0:n], axs[:, 0:n], 4.0, None, op0=ALU.is_ge), deps=[t5])
        tm1 = self.v(lambda e: e.tensor_scalar(q_m[:, 0:n], q_g2[:, 0:n], 1.0, None, op0=ALU.add), deps=[tg2])
        tm2 = self.v(lambda e: e.tensor_scalar(q_s[:, 0:n], q_g4[:, 0:n], 1.0, None, op0=ALU.add), deps=[tg4])
        tst = self.v(lambda e: e.scalar_tensor_tensor(q_m[:, 0:n], q_m[:, 0:n], 0.125, q_s[:, 0:n],
                                                      op0=ALU.mult, op1=ALU.mult), deps=[tm1, tm2])
        tu = self.v(lambda e: e.tensor_scalar(q_g2[:, 0:n], q_g2[:, 0:n], -0.5, 1.0,
                                              op0=ALU.mult, op1=ALU.add), deps=[tm1])
        tw = self.v(lambda e: e.tensor_scalar(q_g4[:, 0:n], q_g4[:, 0:n], -0.5, 1.0,
                                              op0=ALU.mult, op1=ALU.add), deps=[tm2])
        trs = self.v(lambda e: e.tensor_tensor(q_g4[:, 0:n], q_g2[:, 0:n], q_g4[:, 0:n], op=ALU.mult), deps=[tu, tw])
        tt_ = self.v(lambda e: e.scalar_tensor_tensor(axs[:, 0:n], xs[:, 0:n], 8.0, q_g4[:, 0:n],
                                                      op0=ALU.mult, op1=ALU.mult), deps=[trs, t5])
        tni = self.act(ti[:, 0:n], axs[:, 0:n], AF.Copy, bias=-0.5, deps=[tt_])
        tnf = self.v(lambda e: e.tensor_copy(xs[:, 0:n], ti[:, 0:n]), deps=[tni, tt_])
        tmk = self.v(lambda e: e.scalar_tensor_tensor(q_s[:, 0:n], xs[:, 0:n], 0.5, axs[:, 0:n],
                                                      op0=ALU.add, op1=ALU.is_lt), deps=[tnf])
        tnq = self.v(lambda e: e.tensor_tensor(xs[:, 0:n], xs[:, 0:n], q_s[:, 0:n], op=ALU.add), deps=[tmk])
        scb = ap3(qsc, nblk, [[1, nblk], [0, 32]])
        axv = axs[:, 0:n].rearrange("p (b k) -> p b k", k=32)
        qmv = q_m[:, 0:n].rearrange("p (b k) -> p b k", k=32)
        tF = self.v(lambda e: e.tensor_tensor(axv, qmv, scb, op=ALU.mult), deps=[tst, tnf])
        txd = self.v(lambda e: e.tensor_tensor(xd[:, 0:n], xs[:, 0:n], axs[:, 0:n], op=ALU.mult), deps=[tnq, tF])
        return txd


def build_layer0(kb, dev=False):
    nc, p = kb.nc, kb.p
    cfg = kb.cfg
    dup = cfg.get("dup", False)
    NTOK0 = L if dup else NTOK
    NTL = NTOK0 // T0
    es0 = contextlib.ExitStack()
    sb0 = lambda n_, sh, dt_: es0.enter_context(nc.sbuf_tensor(n_, list(sh), dt_))
    ps0 = lambda n_, sh, dt_: es0.enter_context(nc.psum_tensor(n_, list(sh), dt_))
    x_in = kb.dt("x", [HALO + NTOK0, D], F32, "ExternalInput")
    w_in_d = kb.dt("w_in", [D, D3], F32, "ExternalInput")
    w_out_d = kb.dt("w_out", [D, D], F32, "ExternalInput")
    fw_d = kb.dt("fw", [D, D], F32, "ExternalInput")
    b_in_d = kb.dt("b_in_c", [128, 24], F32, "ExternalInput")
    sw_d = kb.dt("short_w_c", [128, 72], F32, "ExternalInput")
    sbc_d = kb.dt("short_b_c", [128, 24], F32, "ExternalInput")
    iw_d = kb.dt("inner_w_c", [128, 32], F32, "ExternalInput")
    ib_d = kb.dt("inner_b_c", [128, 8], F32, "ExternalInput")
    bo_d = kb.dt("b_out_c", [128, 8], F32, "ExternalInput")
    fb_d = kb.dt("fb_c", [128, 8], F32, "ExternalInput")
    eye32_d = kb.dt("eye32", [128, 128], F32, "ExternalInput")
    eye16_d = kb.dt("eye16", [128, 128], BF16, "ExternalInput")
    hm_d = kb.dt("hmask", [128, 1], F32, "ExternalInput")
    okind = "ExternalOutput" if (dev or kb.cfg.get("io")) else None
    x2_d = kb.dt("x2_d", [NTOK0, D], F32, okind)
    xn1t32_d = kb.dt("xn1t32_d", [D, NTOK0], F32, okind)
    xn1t16_d = kb.dt("xn1t16_d", [D, NTOK0], BF16, okind)
    if dup:
        xhB_d = kb.dt("xhB", [HALO, D], F32, "ExternalInput")
        kb.dt("hmaskB", [128, 1], F32, "ExternalInput")

    w_in_sb = sb0("w_in_sb", [128, 8 * D3], BF16)
    w_out_sb = sb0("w_out_sb", [128, 8 * D], BF16)
    fwdT = sb0("fwdT", [128, 8 * D], BF16)
    b_in_c = sb0("b_in_cs", [128, 24], F32)
    sw_c = sb0("sw_cs", [128, 72], F32)
    sb_c = sb0("sb_cs", [128, 24], F32)
    iw_c = sb0("iw_cs", [128, 32], F32)
    ib_c = sb0("ib_cs", [128, 8], F32)
    bo_c = sb0("bo_cs", [128, 8], F32)
    fb_c = sb0("fb_cs", [128, 8], F32)
    eye32 = sb0("eye32_s", [128, 128], F32)
    eye16 = sb0("eye16_s", [128, 128], BF16)
    hmask = sb0("hmask_s", [128, 1], F32)
    hmaskB = sb0("hmaskB_s", [128, 1], F32) if dup else None

    t_win = kb.dma("gpsimd", w_in_sb[:, :], w_in_d[:, :].rearrange("(kc p) e -> p kc e", p=128))
    t_wout = kb.dma("gpsimd", w_out_sb[:, :], w_out_d[:, :].rearrange("(kc p) e -> p kc e", p=128))
    for nm, dst in [("b_in_c", b_in_c), ("short_w_c", sw_c), ("short_b_c", sb_c),
                    ("inner_w_c", iw_c), ("inner_b_c", ib_c), ("b_out_c", bo_c),
                    ("fb_c", fb_c), ("eye32", eye32), ("eye16", eye16), ("hmask", hmask)] + \
                   ([("hmaskB", hmaskB)] if dup else []):
        kb.dma("sync", dst[:, :], kb.dram[nm][:, :])
    t_consts = p.fence("sdma")

    pT16 = ps0("pT16", [128, 1024], BF16)
    pT32 = ps0("pT32", [128, 512], F32)
    pM = [ps0("pM0", [128, 512], F32), ps0("pM1", [128, 512], F32)]
    pC = [ps0("pC0", [128, 512], F32), ps0("pC1", [128, 512], F32)]
    dgS = sb0("dgS", [128, 72 * 128], BF16)
    dgI = sb0("dgI", [128, 32 * 128], BF16)
    t_dg = []
    for j in range(72):
        t_dg.append(kb.v(lambda e, j=j: e.tensor_scalar(
            dgS[:, j * 128:(j + 1) * 128], eye16[:, :], sw_c[:, j:j + 1], None,
            op0=ALU.mult), deps=[t_consts]))
    for j in range(32):
        t_dg.append(kb.v(lambda e, j=j: e.tensor_scalar(
            dgI[:, j * 128:(j + 1) * 128], eye16[:, :], iw_c[:, j:j + 1], None,
            op0=ALU.mult), deps=[t_consts]))
    t_dg = t_dg[-1]

    x_t = [sb0("x_t0", [128, TC0 * D], F32), sb0("x_t1", [128, TC0 * D], F32)]
    xn = sb0("xn", [128, TC0 * D], F32)
    stats = sb0("stats", [128, 64], F32)
    qsc = sb0("qsc", [128, 3 * TC0 * 32], F32)
    xd = sb0("xd", [128, TC0 * D], BF16)
    xnT = sb0("xnT", [128, 8 * T0], BF16)
    xdT = sb0("xdT", [128, 8 * T0], BF16)
    x1e = sb0("x1e", [128, 24 * (T0 + 2)], BF16)
    qb_ = sb0("qb_", [128, 8 * T0], BF16)
    kbf = sb0("kbf", [128, 8 * T0], BF16)
    vbf = sb0("vbf", [128, 8 * T0], BF16)
    vve = sb0("vve", [128, 8 * (T0 + 3)], BF16)
    qs1 = vve
    s_b = sb0("s_b", [128, 8 * T0], F32)
    g_b = sb0("g_b", [128, 8 * T0], F32)
    ti_b = sb0("ti_b", [128, TC0 * D], I16)
    x1sl = sb0("x1sl", [128, 24 * 2], BF16)
    vvsl = sb0("vvsl", [128, 8 * 3], BF16)
    x1slB = sb0("x1slB", [128, 24 * 2], BF16) if dup else None
    vvslB = sb0("vvslB", [128, 8 * 3], BF16) if dup else None
    xn1t32 = sb0("xn1t32", [128, 8 * T0], F32)
    W1E = T0 + 2
    VVE = T0 + 3

    dbg = sb0("dbg", [128, 2048], F32) if dev else None
    tapstate = {"last": None}

    def dtap(name, src_ap, ncols, deps):
        if not dev:
            return
        d = kb.dt("tap_" + name, [128, ncols], F32, "ExternalOutput")
        tcp = kb.v(lambda e: e.tensor_copy(dbg[:, 0:ncols], src_ap),
                   deps=list(deps) + [tapstate["last"]])
        tapstate["last"] = kb.dma("scalar", d[:, :], dbg[:, 0:ncols], deps=[tcp])

    NBW = 32
    with nc.sbuf_tensor("fw_pt", [128, D], F32) as fw_pt, \
         nc.sbuf_tensor("fwd_pt", [128, D], BF16) as fwd_pt:
        ev_tok = None
        tq = None
        fw_r = fw_d[:, :].rearrange("(ec p) d -> p ec d", p=128)
        for ps_ in range(8):
            t_fw = kb.dma("sync", fw_pt[:, :].rearrange("p (c d) -> p c d", d=D),
                          fw_r[:, ps_:ps_ + 1, :], deps=[tq])
            fw3 = fw_pt[:, :].rearrange("p (b k) -> p b k", k=32)
            pe_fq = p.fence("tensor")
            tq = kb.quant(fw3, NBW, s_b, g_b, ti_b, qb_, kbf, vbf, qs1, qsc, fwd_pt,
                          deps=[t_fw, pe_fq])
            ec = ps_
            for dq in range(2):
                items = [(fwd_pt[:, (dq * 4 + j) * 128: (dq * 4 + j) * 128 + 128],
                          j * 128, 128) for j in range(4)]

                def fnq(e, items=items):
                    r = None
                    for src, off, ncols in items:
                        r = e.matmul(ap3(pT16, off, [[1, ncols]]), src, eye16[:, :],
                                     start=True, stop=True, is_transpose=True)
                    return r
                ttr = p.emit("tensor", fnq, deps=[tq, t_consts, ev_tok])
                dst = ap3(fwdT, (dq * 4) * D + ec * 128, [[D, 4], [1, 128]])
                src = ap3(pT16, 0, [[128, 4], [1, 128]])
                ev_tok = kb.v(lambda e, dst=dst, src=src: e.tensor_copy(dst, src), deps=[ttr])
    t_fwdT = ev_tok
    pre_f = [p.fence(e) for e in Prog.ENGINES]

    xh = x_t[0]
    xnh = x_t[1]
    sqh = s_b
    stats_h = sb0("stats_h", [16, 8], F32)
    xnTh = sb0("xnTh", [128, 8 * 16], BF16)
    x1h = sb0("x1h", [128, 24 * 16], BF16)
    uch = sb0("uch", [128, 24 * 14], BF16)
    vvh = sb0("vvh", [128, 8 * 14], BF16)
    tmpc = sb0("tmpc", [128, T0], BF16)

    def emit_halo(x_src, mask_col, x1sl_t, vvsl_t, pre):
     t_xh = kb.dma("sync", xh[0:16, 0:D], x_src, deps=pre)
     xh3 = xh[0:16, 0:D].rearrange("p (c d) -> p c d", d=D)
     xnh3 = xnh[0:16, 0:D].rearrange("p (c d) -> p c d", d=D)
     t_lnh, _ = kb.ln_tok(xh3, 1, D, sqh[0:16, 0:D], stats_h, 0, xnh3, LN_EPS,
                          deps=pre_f + [t_xh])
     tok_tr = None
     for dq in range(2):
        items = [(xnh[0:16, (dq * 4 + j) * 128:(dq * 4 + j) * 128 + 128], j * 16, 16) for j in range(4)]

        def fnh(e, items=items):
            r = None
            for src, off, ncols in items:
                r = e.matmul(ap3(pT32, off, [[1, ncols]]), src, eye32[0:16, 0:16], start=True,
                             stop=True, is_transpose=True)
            return r
        th = p.emit("tensor", fnh, deps=[t_lnh, t_consts, tok_tr])
        dstp = ap3(xnTh, dq * 4 * 16, [[16, 4], [1, 16]])
        srcp = ap3(pT32, 0, [[16, 4], [1, 16]])
        tok_tr = kb.v(lambda e, d=dstp, s=srcp: e.tensor_copy(d, s), deps=[th])
     ev_h = []
     prev_ev = None
     for mc in range(24):
        bank = pM[mc % 2]
        pairs = [(w_in_sb[:, kc * D3 + mc * 128: kc * D3 + mc * 128 + 128],
                  xnTh[:, kc * 16:(kc + 1) * 16]) for kc in range(8)]
        dep_bank = ev_h[mc - 2] if mc >= 2 else None
        tmm = kb.mm(ap3(bank, 0, [[1, 16]]), pairs, deps=[tok_tr, t_win, dep_bank])
        tev = kb.act(x1h[:, mc * 16:(mc + 1) * 16], ap3(bank, 0, [[1, 16]]), AF.Identity,
                     bias=b_in_c[:, mc:mc + 1], deps=[tmm])
        ev_h.append(tev)
     t_x1h = kb.v(lambda e: e.tensor_scalar(x1h[:, :], x1h[:, :], mask_col, None, op0=ALU.mult),
                 deps=ev_h + [t_consts])
     tc3 = None
     for cc in range(24):
        dst = uch[:, cc * 14:(cc + 1) * 14]
        ta = kb.v(lambda e, cc=cc, dst=dst: e.tensor_scalar(
            dst, x1h[:, cc * 16:cc * 16 + 14], sw_c[:, cc * 3:cc * 3 + 1],
            sb_c[:, cc:cc + 1], op0=ALU.mult, op1=ALU.add), deps=[t_x1h])
        for j in (1, 2):
            kb.v(lambda e, cc=cc, j=j: e.tensor_scalar(
                tmpc[:, 0:14], x1h[:, cc * 16 + j:cc * 16 + j + 14],
                sw_c[:, cc * 3 + j:cc * 3 + j + 1], None, op0=ALU.mult), deps=[])
            tc3 = kb.v(lambda e, dst=dst: e.tensor_tensor(dst, dst, tmpc[:, 0:14], op=ALU.add),
                       deps=[])
     vvh3 = vvh[:, :].rearrange("p (c t) -> p c t", t=14)
     tv1 = kb.v(lambda e: e.tensor_tensor(vvh3, ap3(uch, 8 * 14, [[14, 8], [1, 14]]),
                                         ap3(uch, 16 * 14, [[14, 8], [1, 14]]), op=ALU.mult), deps=[tc3])
     tv2 = kb.v(lambda e: e.tensor_scalar(vvh[:, :], vvh[:, :], mask_col, None, op0=ALU.mult),
               deps=[tv1])
     t_sl1 = kb.v(lambda e: e.tensor_copy(ap3(x1sl_t, 0, [[2, 24], [1, 2]]),
                                         ap3(x1h, 14, [[16, 24], [1, 2]])), deps=[t_x1h])
     t_sl2 = kb.v(lambda e: e.tensor_copy(ap3(vvsl_t, 0, [[3, 8], [1, 3]]),
                                         ap3(vvh, 11, [[14, 8], [1, 3]])), deps=[tv2])

     return tok_tr, t_sl1, t_sl2

    tok_tr, t_sl1, t_sl2 = emit_halo(x_in[0:HALO, :], hmask[:, 0:1], x1sl, vvsl,
                                     pre_f + [p.fence("sdma")])
    if dup:
        tokB, tB_sl1, tB_sl2 = emit_halo(xhB_d[:, :], hmaskB[:, 0:1], x1slB, vvslB,
                                         [p.fence("vector"), p.fence("tensor"), p.fence("scalar")])
        tok_tr = tokB
    halo_f = [p.fence("vector"), p.fence("tensor"), p.fence("scalar")]
    if dev:
        dtap("x1h", x1h[:, :], 384, [t_x1h])
        dtap("uch", uch[:, :], 336, [tc3])
        dtap("vvh", vvh[:, :], 112, [tv2])
        dtap("x1sl_h", x1sl[:, :], 48, [t_sl1])
        dtap("vvsl_h", vvsl[:, :], 24, [t_sl2])
    NB = TC0 * 32
    prev = {"pT32_free": tok_tr}
    last_store = []
    for k in range(NTL):
        xb = x_t[k % 2]
        r0 = HALO + k * T0
        t_x = kb.dma("sync", xb[:, :].rearrange("p (c d) -> p c d", d=D),
                     x_in[r0:r0 + T0, :].rearrange("(c p) d -> p c d", p=128),
                     deps=(pre_f + halo_f if k < 2 else prev.get(("x_free", k % 2), [])))
        x3 = xb[:, :].rearrange("p (c d) -> p c d", d=D)
        xn3 = xn[:, :].rearrange("p (c d) -> p c d", d=D)
        t_ln, _ = kb.ln_tok(x3, TC0, D, s_b[:, 0:TC0 * D], stats, 0, xn3, LN_EPS,
                            deps=[t_x, prev.get("pe")])
        if k == 0:
            dtap("xn0", xn[:, :], 2048, [t_ln])
        xnq = xn[:, :].rearrange("p (b k) -> p b k", k=32)
        t_qd = kb.quant(xnq, NB, s_b, g_b, ti_b, qb_, kbf, vbf, qs1, qsc, xd,
                        deps=[t_ln, prev.get("pe")])
        if k == 0:
            dtap("xd0", xd[:, :], 2048, [t_qd])
        t_xnT = []
        tr_dep = prev.get("pT32_free")
        first_ev = prev.get("pe")
        for tc in range(TC0):
            for dq in range(2):
                items = [(xn[:, tc * D + (dq * 4 + j) * 128: tc * D + (dq * 4 + j) * 128 + 128],
                          j * 128, 128) for j in range(4)]

                def ftr(e, items=items):
                    r = None
                    for src, off, ncols in items:
                        r = e.matmul(ap3(pT32, off, [[1, ncols]]), src, eye32[:, :],
                                     start=True, stop=True, is_transpose=True)
                    return r
                ttr = p.emit("tensor", ftr, deps=[t_ln, tr_dep])
                dstp = ap3(xnT, dq * 4 * T0 + tc * 128, [[T0, 4], [1, 128]])
                tr_dep = kb.act(dstp, ap3(pT32, 0, [[128, 4], [1, 128]]), AF.Copy,
                                deps=[ttr, first_ev])
                first_ev = None
                t_xnT.append(tr_dep)
        t_xdT = []
        trd_dep = prev.get("pT16_free")
        for tc in range(TC0):
            items = [(xd[:, tc * D + j * 128: tc * D + j * 128 + 128], j * 128, 128) for j in range(8)]

            def ftd(e, items=items):
                r = None
                for src, off, ncols in items:
                    r = e.matmul(ap3(pT16, off, [[1, ncols]]), src, eye16[:, :],
                                 start=True, stop=True, is_transpose=True)
                return r
            ttr = p.emit("tensor", ftd, deps=[t_qd, trd_dep, t_fwdT])
            dstp = ap3(xdT, tc * 128, [[T0, 8], [1, 128]])
            trd_dep = kb.v(lambda e, d=dstp: e.tensor_copy(d, ap3(pT16, 0, [[128, 8], [1, 128]])),
                           deps=[ttr])
            t_xdT.append(trd_dep)
        prev["pT16_free"] = trd_dep

        seam = dup and (k == NTOK // T0)
        rs_x1 = x1slB if seam else x1sl
        rs_dep = tB_sl1 if seam else t_sl1
        t_rs = kb.v(lambda e, rs_x1=rs_x1: e.tensor_copy(ap3(x1e, 0, [[W1E, 24], [1, 2]]),
                                                         ap3(rs_x1, 0, [[2, 24], [1, 2]])),
                    deps=[rs_dep, prev.get("conv_done")])
        ev_toks = []
        mm_prev = [None, None]
        for mc in range(24):
            bank = pM[mc % 2]
            pairs = [(w_in_sb[:, kc * D3 + mc * 128: kc * D3 + mc * 128 + 128],
                      xnT[:, kc * T0:(kc + 1) * T0]) for kc in range(8)]
            tmm = kb.mm(ap3(bank, 0, [[1, T0]]), pairs,
                        deps=[t_xnT[-1], t_win, mm_prev[mc % 2], prev.get("conv_done") if mc < 2 else None])
            tev = kb.act(ap3(x1e, mc * W1E + 2, [[1, T0]]), ap3(bank, 0, [[1, T0]]),
                         AF.Identity, bias=b_in_c[:, mc:mc + 1],
                         deps=[tmm, t_rs if mc == 0 else None, prev.get("x1sl_done")])
            mm_prev[mc % 2] = tev
            ev_toks.append(tev)
        grp = {0: qb_, 1: kbf, 2: vbf}
        t_conv = []
        cdep = {0: prev.get("pC0f"), 1: prev.get("pC1f")}
        for g in range(3):
            gb = grp[g]
            last = None
            for l in range(8):
                cc = g * 8 + l
                bank = pC[cc % 2]

                def fcv(e, cc=cc, bank=bank):
                    r = None
                    for j in range(3):
                        r = e.matmul(ap3(bank, 0, [[1, T0]]),
                                     dgS[:, (cc * 3 + j) * 128:(cc * 3 + j) * 128 + 128],
                                     ap3(x1e, cc * W1E + j, [[1, T0]]),
                                     start=(j == 0), stop=(j == 2))
                    return r
                tmm = p.emit("tensor", fcv, deps=[ev_toks[cc], t_rs, t_dg, cdep[cc % 2]])
                last = kb.v(lambda e, cc=cc, l=l, gb=gb, bank=bank: e.tensor_scalar(
                    gb[:, l * T0:(l + 1) * T0], ap3(bank, 0, [[1, T0]]),
                    1.0, sb_c[:, cc:cc + 1], op0=ALU.mult, op1=ALU.add),
                    deps=[tmm, t_qd])
                cdep[cc % 2] = last
            t_conv.append(last)
        if k == 0:
            dtap("x1e0", ap3(x1e, 2, [[W1E, 8], [1, T0]]), 2048, ev_toks)
            dtap("q0", qb_[:, :], 2048, [t_conv[0]])
            dtap("k0", kbf[:, :], 2048, [t_conv[1]])
            dtap("v0c", vbf[:, :], 2048, [t_conv[2]])
        t_sl1 = kb.v(lambda e: e.tensor_copy(ap3(x1sl, 0, [[2, 24], [1, 2]]),
                                             ap3(x1e, T0, [[W1E, 24], [1, 2]])),
                     deps=ev_toks[-1:])
        prev["x1sl_done"] = t_sl1
        prev["conv_done"] = t_conv[2]
        t_vv = kb.v(lambda e: e.tensor_tensor(ap3(vve, 3, [[VVE, 8], [1, T0]]),
                                              kbf[:, :].rearrange("p (c t) -> p c t", t=T0),
                                              vbf[:, :].rearrange("p (c t) -> p c t", t=T0),
                                              op=ALU.mult), deps=[t_conv[1], t_conv[2], prev.get("v2_done")])
        rs_vv = vvslB if seam else vvsl
        rs_dep2 = tB_sl2 if seam else t_sl2
        t_vr = kb.v(lambda e, rs_vv=rs_vv: e.tensor_copy(ap3(vve, 0, [[VVE, 8], [1, 3]]),
                                                         ap3(rs_vv, 0, [[3, 8], [1, 3]])),
                    deps=[rs_dep2, t_vv])
        t_sl2 = kb.v(lambda e: e.tensor_copy(ap3(vvsl, 0, [[3, 8], [1, 3]]),
                                             ap3(vve, T0, [[VVE, 8], [1, 3]])), deps=[t_vv])
        b1 = None
        for l in range(8):
            bank = pC[l % 2]

            def fci(e, l=l, bank=bank):
                r = None
                for j in range(4):
                    r = e.matmul(ap3(bank, 0, [[1, T0]]),
                                 dgI[:, (l * 4 + j) * 128:(l * 4 + j) * 128 + 128],
                                 ap3(vve, l * VVE + j, [[1, T0]]),
                                 start=(j == 0), stop=(j == 3))
                return r
            tmm = p.emit("tensor", fci, deps=[t_vr, t_sl2, cdep[l % 2]])
            b1 = kb.v(lambda e, l=l, bank=bank: e.tensor_scalar(
                kbf[:, l * T0:(l + 1) * T0], ap3(bank, 0, [[1, T0]]),
                1.0, ib_c[:, l:l + 1], op0=ALU.mult, op1=ALU.add), deps=[tmm])
            cdep[l % 2] = b1
        prev["pC0f"], prev["pC1f"] = cdep[0], cdep[1]
        t_y = kb.v(lambda e: e.tensor_tensor(vbf[:, :], qb_[:, :], kbf[:, :], op=ALU.mult),
                   deps=[b1, t_conv[0]])
        prev["v2_done"] = t_y
        if k == 0:
            dtap("vve0", ap3(vve, 0, [[VVE, 8], [1, T0]]), 2048, [t_vr, t_vv])
            dtap("y0", vbf[:, :], 2048, [t_y])

        mm_prev2 = [None, None]
        sg_toks = []
        for which, (wsb, rhs, dstb, biasc, func, wdep) in enumerate([
                (w_out_sb, vbf, s_b, bo_c, AF.Identity, t_wout),
                (fwdT, xdT, g_b, fb_c, AF.Silu, t_fwdT)]):
            for mc in range(8):
                bank = pM[mc % 2]
                pairs = [(wsb[:, kc * D + mc * 128: kc * D + mc * 128 + 128],
                          rhs[:, kc * T0:(kc + 1) * T0]) for kc in range(8)]
                dep_in = t_y if which == 0 else t_xdT[-1]
                tmm = kb.mm(ap3(bank, 0, [[1, T0]]), pairs,
                            deps=[dep_in, wdep, mm_prev2[mc % 2]])
                tev = kb.act(dstb[:, mc * T0:(mc + 1) * T0], ap3(bank, 0, [[1, T0]]),
                             func, bias=biasc[:, mc:mc + 1], deps=[tmm, t_qd])
                mm_prev2[mc % 2] = tev
                sg_toks.append(tev)
        if k == 0:
            dtap("s0", s_b[:, :], 2048, sg_toks[:8])
            dtap("g0", g_b[:, :], 2048, sg_toks[8:])
        t_p = kb.v(lambda e: e.tensor_tensor(s_b[:, :], s_b[:, :], g_b[:, :], op=ALU.mult),
                   deps=sg_toks)
        t_pT = []
        trp_dep = prev.get("pT32_free")
        for tc in range(TC0):
            for dq in range(2):
                items = [(s_b[:, (dq * 4 + j) * T0 + tc * 128: (dq * 4 + j) * T0 + tc * 128 + 128],
                          j * 128, 128) for j in range(4)]

                def ftp(e, items=items):
                    r = None
                    for src, off, ncols in items:
                        r = e.matmul(ap3(pT32, off, [[1, ncols]]), src, eye32[:, :],
                                     start=True, stop=True, is_transpose=True)
                    return r
                ttr = p.emit("tensor", ftp, deps=[t_p, trp_dep])
                dstp = ap3(xn, tc * D + dq * 4 * 128, [[128, 4], [1, 128]])
                trp_dep = kb.v(lambda e, d=dstp: e.tensor_copy(d, ap3(pT32, 0, [[128, 4], [1, 128]])),
                               deps=[ttr, t_qd])
                t_pT.append(trp_dep)
        prev["pT32_free"] = trp_dep
        if k == 0:
            dtap("p0", s_b[:, :], 2048, [t_p])
            dtap("ptok0", xn[:, :], 2048, [t_pT[-1]])
        t_x2 = kb.v(lambda e, xb=xb: e.tensor_tensor(xb[:, :], xb[:, :], xn[:, :], op=ALU.add),
                    deps=[t_pT[-1]])
        if k == 0:
            dtap("x2t0", xb[:, :], 2048, [t_x2])
        t_st_x2 = kb.dma("scalar", x2_d[k * T0:(k + 1) * T0, :].rearrange("(c p) d -> p c d", p=128),
                         xb[:, :].rearrange("p (c d) -> p c d", d=D), deps=[t_x2])
        xn1v = xn[:, :].rearrange("p (c d) -> p c d", d=D)
        t_ln1, t_sq1 = kb.ln_tok(x3, TC0, D, g_b[:, 0:TC0 * D], stats, 20, xn1v, LN_EPS,
                                 deps=[t_x2, t_p, prev.get("pe")])
        t_tr1 = []
        tr1_dep = prev.get("pT32_free")
        ev_extra = prev.get("xn1t_free", [])
        for tc in range(TC0):
            for dq in range(2):
                items = [(xn[:, tc * D + (dq * 4 + j) * 128: tc * D + (dq * 4 + j) * 128 + 128],
                          j * 128, 128) for j in range(4)]

                def ft1(e, items=items):
                    r = None
                    for src, off, ncols in items:
                        r = e.matmul(ap3(pT32, off, [[1, ncols]]), src, eye32[:, :],
                                     start=True, stop=True, is_transpose=True)
                    return r
                ttr = p.emit("tensor", ft1, deps=[t_ln1, tr1_dep])
                dstp = ap3(xn1t32, dq * 4 * T0 + tc * 128, [[T0, 4], [1, 128]])
                tr1_dep = kb.v(lambda e, d=dstp: e.tensor_copy(d, ap3(pT32, 0, [[128, 4], [1, 128]])),
                               deps=[ttr] + list(ev_extra))
                ev_extra = []
                t_tr1.append(tr1_dep)
        prev["pT32_free"] = tr1_dep
        st1 = kb.dma("scalar", xn1t32_d[:, k * T0:(k + 1) * T0].rearrange("(dc p) t -> p dc t", p=128),
                     xn1t32[:, :].rearrange("p (dc t) -> p dc t", t=T0), deps=[tr1_dep])
        st2 = kb.dma("gpsimd", xn1t16_d[:, k * T0:(k + 1) * T0].rearrange("(dc p) t -> p dc t", p=128),
                     xn1t32[:, :].rearrange("p (dc t) -> p dc t", t=T0), deps=[tr1_dep])
        prev[("x_free", k % 2)] = [t_st_x2, t_ln1, st1]
        prev["xn1t_free"] = [st1, st2]
        prev["pe"] = p.fence("tensor")
        last_store = [t_st_x2, st1, st2]
    es0.close()
    return last_store

T1 = 512
NT1 = L // T1
SC = float(64.0 ** -0.25)
LN16 = float(np.log(16.0))
EPS1 = 1e-6


def build_l1attn(kb, dev=False, na=NT1, nb=8):
    nc, p = kb.nc, kb.p
    skip = kb.cfg.get("skip", set())
    fused = kb.cfg.get("fused", False)
    es2 = contextlib.ExitStack()
    if fused:
        xnf_d = kb.dram["xn1t16_d"]
        xn32_d = kb.dram["xn1t32_d"]
    else:
        xnf_d = kb.dt("xnf", [D, L], BF16, "ExternalInput")
        xn32_d = kb.dt("xn32", [D, NTOK], F32, "ExternalInput")
    for nm in ["wq", "wk", "wv", "wo"]:
        kb.dt(nm, [D, D], F32, "ExternalInput")
    omT_d = kb.dt("omT", [128, 256], F32, "ExternalInput")
    e16_d = kb.dt("e16f", [128, 128], F32, "ExternalInput")
    for nm in ["ln1g_c", "ln1b_c", "ln2g_c", "ln2b_c", "b2_c"]:
        kb.dt(nm, [128, 8], F32, "ExternalInput")
    kb.dt("b1_c", [128, 32], F32, "ExternalInput")
    w1_d = kb.dt("w1", [D, 4 * D], F32, "ExternalInput")
    w2_d = kb.dt("w2", [4 * D, D], F32, "ExternalInput")
    kb.dt("eye16", [128, 128], BF16, "ExternalInput")
    kb.dt("eye32", [128, 128], F32, "ExternalInput")
    kb.dt("onesb", [128, 1], BF16, "ExternalInput")
    kb.dt("onesf", [1, 128], F32, "ExternalInput")
    okind = "ExternalOutput" if (dev or kb.cfg.get("io")) else None
    s_d = kb.dt("s_d", [D, NTOK], F32, okind)
    phase_dep = kb.cfg.get("phase_dep_l2", [])
    qT_dram = kb.dt("qT_sc", [D, NTOK], BF16)
    sqq_dram = kb.dt("sqq_sc", [128, 8 * 64], F32)

    es2sb = lambda n_, sh, dt_: es2.enter_context(kb.nc.sbuf_tensor('l2_' + n_, list(sh), dt_))
    omT = es2sb("omT_s", [128, 256], BF16)
    e16f = es2sb("e16f_s", [128, 128], F32)
    eye16 = es2sb("eye16_s", [128, 128], BF16)
    eye32 = es2sb("eye32_s", [128, 128], F32)
    onesb = es2sb("onesb_s", [128, 1], BF16)
    onesf = es2sb("onesf_s", [1, 128], F32)
    lnv = {nm: es2sb(nm + "_s", [128, 8], F32)
           for nm in ["ln1g_c", "ln1b_c", "ln2g_c", "ln2b_c", "b2_c"]}
    b1c = es2sb("b1c_s", [128, 32], F32)
    kb.dma("gpsimd", omT[:, :], omT_d[:, :], deps=phase_dep)
    first_sd = list(phase_dep)
    for nm, dst in ([("e16f", e16f), ("eye16", eye16), ("eye32", eye32),
                     ("onesb", onesb), ("onesf", onesf), ("b1_c", b1c)] +
                    [(nm, lnv[nm]) for nm in lnv]):
        kb.dma("sync", dst[:, :], kb.dram[nm][:, :], deps=first_sd)
        first_sd = []
    t_w = [p.fence("gdma"), p.fence("sdma")] + list(phase_dep)

    qTt = es2sb("qTt", [128, 8 * T1], BF16)
    bscr = es2sb("bscr", [128, 8 * T1], BF16)
    bsc2 = es2sb("bsc2", [128, 8 * T1], BF16)
    fscrP = [es2sb("fscrP0", [128, 16 * 256], F32), es2sb("fscrP1", [128, 16 * 256], F32)]
    bscrP = [es2sb("bscrP0", [128, 16 * 256], BF16), es2sb("bscrP1", [128, 16 * 256], BF16)]
    stbP = [es2sb("stbP0", [128, 16], F32), es2sb("stbP1", [128, 16], F32)]
    cngP = [es2sb("cngP0", [128, 16], F32), es2sb("cngP1", [128, 16], F32)]
    sqc = es2sb("sqc", [128, 64], F32)
    sqcq = es2sb("sqcq", [128, 64], F32)
    kv_bf = es2sb("kv_bfuf", [128, 5 * 455], BF16)

    def dtap(name, src_ap, ncols, deps, dt_=F32):
        if not dev:
            return
        d = kb.dt("tap_" + name, [128, ncols], dt_, "ExternalOutput")
        kb.dma("qdma", d[:, :], src_ap, deps=deps)

    es_a = contextlib.ExitStack()
    with es_a:
        _e = es_a.enter_context
        pk0 = _e(nc.psum_tensor("pkv0", [128, 512], F32))
        pk1 = _e(nc.psum_tensor("pkv1", [128, 512], F32))
        pk2 = _e(nc.psum_tensor("pkv2", [128, 512], F32))
        pk3 = _e(nc.psum_tensor("pkv3", [128, 512], F32))
        pk4 = _e(nc.psum_tensor("pkv4", [128, 512], F32))
        pA = _e(nc.psum_tensor("pA", [128, 512], F32))
        pB = _e(nc.psum_tensor("pB", [128, 512], F32))
        pC = _e(nc.psum_tensor("pC", [128, 512], F32))
        wqs = _e(nc.sbuf_tensor("wqs", [128, 8 * D], BF16))
        wks = _e(nc.sbuf_tensor("wks", [128, 8 * D], BF16))
        wvs = _e(nc.sbuf_tensor("wvs", [128, 8 * D], BF16))
        xnb0 = _e(nc.sbuf_tensor("xnb0", [128, 8 * T1], BF16))
        xnb1 = _e(nc.sbuf_tensor("xnb1", [128, 8 * T1], BF16))
        kT = _e(nc.sbuf_tensor("kT", [128, 8 * T1], BF16))
        vab = _e(nc.sbuf_tensor("vab", [128, 4 * 1040], BF16))
        sqrow = _e(nc.sbuf_tensor("sqrow", [16, T1], F32))
        kv_sb = _e(nc.sbuf_tensor("kv_sbuf", [128, 5 * 455], F32))
        fscr = _e(nc.sbuf_tensor("fscr_a", [128, 8 * T1], F32))
        xnb = [xnb0, xnb1]
        for nm, dst in [("wq", wqs), ("wk", wks), ("wv", wvs)]:
            kb.dma("gpsimd", dst[:, :], kb.dram[nm][:, :].rearrange("(kc p) e -> p kc e", p=128))
        t_w = t_w + [p.fence("gdma")]
        pkv = [pk0, pk1, pk2, pk3, pk4]
        prev = {}
        kv_last = None
        for ti in range(na):
            own = ti < nb
            xb = xnb[ti % 2]
            t_x = kb.dma("sync", xb[:, :].rearrange("p (kc t) -> p kc t", t=T1),
                         xnf_d[:, ti * T1:(ti + 1) * T1].rearrange("(kc p) t -> p kc t", p=128),
                         deps=[prev.get(("xfree", ti % 2))] + (t_w if ti == 0 else []))
            rot = [pA, pB]
            rotdep = {0: [prev.get("rA")], 1: [prev.get("rB")]}

            for which in (["q", "k"] if own else ["k"]):
                wsb = wqs if which == "q" else wks
                dstT = qTt if which == "q" else kT
                dst_sq = sqcq if which == "q" else sqc
                ev = []
                for mc in range(8):
                    bank = rot[mc % 2]
                    pairs = [(wsb[:, kc * D + mc * 128: kc * D + mc * 128 + 128],
                              xb[:, kc * T1:(kc + 1) * T1]) for kc in range(8)]
                    if "mm" in skip:
                        continue
                    tmm = kb.mm(ap3(bank, 0, [[1, T1]]), pairs,
                                deps=[t_x] + rotdep[mc % 2])
                    e1 = e2 = tmm
                    if "e1" not in skip:
                        e1 = kb.act(fscr[:, mc * T1:(mc + 1) * T1], ap3(bank, 0, [[1, T1]]),
                                    AF.Square, scale=SC, deps=[tmm])
                    if "e2" not in skip:
                        e2 = kb.v(lambda e, bank=bank, dstT=dstT, mc=mc: e.tensor_scalar(
                            dstT[:, mc * T1:(mc + 1) * T1], ap3(bank, 0, [[1, T1]]),
                            SC, None, op0=ALU.mult), deps=[tmm, e1, prev.get("kTfree") if which == "k" else prev.get("qTfree")])
                    rotdep[mc % 2] = [e1, e2]
                    ev.append((e1, e2))
                if "mm" in skip:
                    continue
                if "sq" in skip:
                    continue
                sq_mm = None
                for mc in range(8):
                    sq_mm = p.emit("tensor", lambda e, mc=mc: e.matmul(
                        pC[0:16, 0:T1], e16f[:, mc * 16:(mc + 1) * 16],
                        fscr[:, mc * T1:(mc + 1) * T1], start=(mc == 0), stop=(mc == 7)),
                        deps=[ev[mc][0], prev.get("pCfree")])
                e3 = kb.v(lambda e: e.tensor_copy(sqrow[:, :], pC[0:16, 0:T1]), deps=[sq_mm])
                tr = None
                for j in range(4):
                    tr = p.emit("tensor", lambda e, j=j: e.matmul(
                        pC[:, 256 + j * 16:256 + (j + 1) * 16],
                        sqrow[:, j * 128:(j + 1) * 128], eye32[0:16, 0:16],
                        start=True, stop=True, is_transpose=True), deps=[e3, tr])
                e4 = kb.v(lambda e, dst_sq=dst_sq: e.tensor_scalar(
                    dst_sq[:, :], pC[:, 256:320], 0.5, LN16, op0=ALU.mult, op1=ALU.add),
                    deps=[tr])
                prev["pCfree"] = e4
                if which == "q":
                    stq = kb.dma("qdma", qT_dram[:, ti * T1:(ti + 1) * T1].rearrange("(kc p) t -> p kc t", p=128),
                                 qTt[:, :].rearrange("p (kc t) -> p kc t", t=T1), deps=[ev[7][1]])
                    stsq = kb.dma("qdma", sqq_dram[:, ti * 64:(ti + 1) * 64], sqcq[:, :], deps=[e4])
                    prev["qTfree"] = stq
            if "v" in skip:
                prev[("xfree", ti % 2)] = p.fence("tensor")
                continue
            evv = []
            for tc in range(4):
                tm1 = kb.v(lambda e, tc=tc: e.memset(
                    ap3(vab, tc * 1040 + 64, [[65, 16], [1, 1]]), 1.0),
                    deps=[prev.get("vabfree")])
                for nh in range(2):
                    bank = rot[(tc * 2 + nh) % 2]
                    pairs = [(xb[:, kc * T1 + tc * 128: kc * T1 + tc * 128 + 128],
                              wvs[:, kc * D + nh * 512: kc * D + nh * 512 + 512]) for kc in range(8)]
                    tmm = kb.mm(ap3(bank, 0, [[1, 512]]), pairs,
                                deps=[t_x] + rotdep[(tc * 2 + nh) % 2])
                    dst = ap3(vab, tc * 1040 + nh * 8 * 65, [[65, 8], [1, 64]])
                    tev = kb.v(lambda e, dst=dst, bank=bank: e.tensor_copy(
                        dst, ap3(bank, 0, [[64, 8], [1, 64]])),
                        deps=[tmm, prev.get("vabfree")])
                    rotdep[(tc * 2 + nh) % 2] = [tev]
                    evv.append(tev)
            t_vaug = evv[-1]
            def flush_kv(st, t_vaug=t_vaug, ti=ti):
                par, tc = st["par"], st["tc"]
                bsP = bscrP[par]
                tpk = kb.v(lambda e, fsP=st["fs"], bsP=bsP: e.tensor_scalar(
                    bsP[:, :16 * 256], fsP[:, :16 * 256], 1e-6, None, op0=ALU.add),
                    deps=[st["texp"], prev.get(("bsfree", par))])
                prev[("fsfree", par)] = tpk
                kvl = None
                for h in ([] if "kv" in skip else range(16)):
                    for j in range(2):
                        s_ = 2 * h + j
                        bank = pkv[s_ // 7]
                        col = (s_ % 7) * 65
                        stflag = (ti == 0 and tc == 0 and (s_ % 7) == 0)
                        last = (ti == na - 1 and tc == 3)
                        kvl = p.emit("tensor", lambda e, h=h, j=j, bank=bank, col=col,
                                     stflag=stflag, last=last, tc=tc, bsP=bsP: e.matmul(
                            ap3(bank, col, [[1, 65]]),
                            bsP[:, h * 256 + j * 128: h * 256 + j * 128 + 128],
                            vab[:, tc * 1040 + h * 65: tc * 1040 + h * 65 + 65],
                            start=stflag, stop=last, skip_group_check=True),
                            deps=[tpk, t_vaug])
                prev[("bsfree", par)] = kvl
                return kvl if kvl is not None else tpk

            pend = None
            for tc in (() if "pk" in skip else range(4)):
                par = tc % 2
                fsP, stP, cnP = fscrP[par], stbP[par], cngP[par]
                prj = []
                for h in range(16):
                    bank = rot[h % 2]
                    lh = kT[(h % 2) * 64:(h % 2) * 64 + 64,
                            (h // 2) * T1 + tc * 128:(h // 2) * T1 + tc * 128 + 128]
                    omx = omT[(h % 2) * 64:(h % 2) * 64 + 64, :]
                    tmm = p.emit("tensor", lambda e, lh=lh, bank=bank, omx=omx: e.matmul(
                        ap3(bank, 0, [[1, 256]]), lh, omx, start=True, stop=True),
                        deps=rotdep[h % 2] + [ev[7][1]])
                    tev = kb.v(lambda e, h=h, bank=bank, fsP=fsP, stP=stP: e.tensor_scalar(
                        fsP[:, h * 256:(h + 1) * 256], ap3(bank, 0, [[1, 256]]),
                        1.0, None, op0=ALU.mult, op1=ALU.max, accum_out=stP[:, h:h + 1]),
                        deps=[tmm, prev.get(("fsfree", par))])
                    rotdep[h % 2] = [tev]
                    prj.append(tev)
                tcn = kb.v(lambda e, tc=tc, stP=stP, cnP=cnP: e.scalar_tensor_tensor(
                    cnP[:, :], stP[:, :], -1.0, sqc[:, tc * 16:(tc + 1) * 16],
                    op0=ALU.mult, op1=ALU.subtract), deps=prj + [prev.get(("cngfree", par))])
                texp = None
                for h in range(16):
                    texp = kb.act(fsP[:, h * 256:(h + 1) * 256], fsP[:, h * 256:(h + 1) * 256],
                                  AF.Exp, bias=cnP[:, h:h + 1], deps=[tcn])
                prev[("cngfree", par)] = texp
                st = {"par": par, "tc": tc, "fs": fsP, "texp": texp}
                if pend is not None:
                    kv_last = flush_kv(pend)
                pend = st
            if pend is not None:
                kv_last = flush_kv(pend)
            prev[("xfree", ti % 2)] = p.fence("tensor")
            prev["vabfree"] = kv_last
            prev["kTfree"] = kv_last
            prev["rA"], prev["rB"] = rotdep[0][0], rotdep[1][0]
        if "kv" in skip or "pk" in skip or "v" in skip:
            kv_last = p.fence("tensor")
        evk = []
        for b_ in range(5):
            ncols = 455 if b_ < 4 else 260
            evk.append(kb.v(lambda e, b_=b_, ncols=ncols: e.tensor_copy(
                kv_sb[:, b_ * 455:b_ * 455 + ncols],
                ap3(pkv[b_], 0, [[1, ncols]])), deps=[kv_last]))
        if fused:
            kvp_d = kb.dt("kvp_sc", [128, 2080], F32)
            kvo_d = kb.dt("kvo_sc", [128, 2080], F32)
            st_kv = kb.dma("scalar", kvp_d[:, :], kv_sb[:, 0:2080], deps=evk)
            RG = [[0, 1], [2, 3], [4, 5], [6, 7]]
            t_ar = p.emit("gdma", lambda e: e.collective_compute(
                "AllReduce", ALU.add, replica_groups=RG,
                ins=[kvp_d.ap().opt()], outs=[kvo_d.ap().opt()]),
                deps=[st_kv], dk=1)
            t_ld = kb.dma("sync", kv_sb[:, 0:2080], kvo_d[:, :], deps=[t_ar])
            evk = evk + [t_ld]
        t_kv = kb.v(lambda e: e.tensor_copy(kv_bf[:, 0:2080], kv_sb[:, 0:2080]), deps=evk)
        if dev:
            dtap("kv", kv_sb[:, 0:2048], 2048, [t_kv])

    af = [p.fence(s_) for s_ in Prog.STREAMS]

    def kvs(h, j):
        s_ = 2 * h + j
        return kv_bf[:, (s_ // 7) * 455 + (s_ % 7) * 65: (s_ // 7) * 455 + (s_ % 7) * 65 + 65]

    es_b = contextlib.ExitStack()
    with es_b:
        _e = es_b.enter_context
        qA = _e(nc.psum_tensor("qA", [128, 512], F32))
        qB = _e(nc.psum_tensor("qB", [128, 512], F32))
        qX = _e(nc.psum_tensor("qX", [128, 1024], BF16))
        pH0 = _e(nc.psum_tensor("pH0", [128, 512], F32))
        pH1 = _e(nc.psum_tensor("pH1", [128, 512], F32))
        pY0 = _e(nc.psum_tensor("pY0", [128, 512], F32))
        pY1 = _e(nc.psum_tensor("pY1", [128, 512], F32))
        wos = _e(nc.sbuf_tensor("wos", [128, 8 * D], BF16))
        pqT = _e(nc.sbuf_tensor("pqT", [128, 16 * 256], BF16))
        attn = pqT
        lnscr = _e(nc.sbuf_tensor("lnscr", [128, T1], F32))
        zc = _e(nc.sbuf_tensor("zc", [128, 1], F32))
        y1 = _e(nc.sbuf_tensor("y1", [128, 8 * T1], F32))
        xa = _e(nc.sbuf_tensor("xa", [128, 8 * T1], F32))
        num_sb = _e(nc.sbuf_tensor("num_sb", [128, 16 * 65], F32))
        rd16 = _e(nc.sbuf_tensor("rd16", [128, 16], F32))
        w1g0 = _e(nc.sbuf_tensor("w1g0", [128, 8 * 512], BF16))
        w1g1 = _e(nc.sbuf_tensor("w1g1", [128, 8 * 512], BF16))
        w2m0 = _e(nc.sbuf_tensor("w2m0", [128, 32 * 128], BF16))
        w2m1 = _e(nc.sbuf_tensor("w2m1", [128, 32 * 128], BF16))
        h1_sb = _e(nc.sbuf_tensor("h1_sb", [128, 32 * T1], BF16))
        w1g = [w1g0, w1g1]
        w2m = [w2m0, w2m1]
        pH = [pH0, pH1]
        pY = [pY0, pY1]
        t_wo = kb.dma("gpsimd", wos[:, :], kb.dram["wo"][:, :].rearrange("(kc p) e -> p kc e", p=128),
                      deps=af)
        prev = {"rA": None, "rB": None}
        t_zc = kb.v(lambda e: e.memset(zc[:, :], 0.0), deps=af)
        rows_mean = num_sb[0:1, T1:2 * T1]
        rows_sq = num_sb[0:1, 0:T1]
        rows_var = lnscr[0:1, 0:T1]
        rows_rstd = num_sb[0:1, 0:T1]
        rows_mr = num_sb[0:1, T1:2 * T1]

        def ln_T(yf, ybf, sqbf, gname, bname, outf, outbf, deps_in, eps=EPS1):
            tb = kb.v(lambda e: e.tensor_copy(ybf[:, :], yf[:, :]), deps=deps_in)
            tq = kb.act(sqbf[:, :], yf[:, :], AF.Square, deps=deps_in)
            m1 = None
            for mc in range(8):
                m1 = p.emit("tensor", lambda e, mc=mc: e.matmul(
                    qA[0:1, 0:T1], onesb[:, :], ybf[:, mc * T1:(mc + 1) * T1],
                    start=(mc == 0), stop=(mc == 7)), deps=[tb, prev.get("rA")])
            m2 = None
            for mc in range(8):
                m2 = p.emit("tensor", lambda e, mc=mc: e.matmul(
                    qB[0:1, 0:T1], onesb[:, :], sqbf[:, mc * T1:(mc + 1) * T1],
                    start=(mc == 0), stop=(mc == 7)), deps=[tq, prev.get("rB")])
            r1 = kb.v(lambda e: e.tensor_scalar(rows_mean, qA[0:1, 0:T1], 1.0 / D, None,
                                                op0=ALU.mult), deps=[m1])
            r2 = kb.v(lambda e: e.tensor_scalar(rows_sq, qB[0:1, 0:T1], 1.0 / D, None,
                                                op0=ALU.mult), deps=[m2])
            r3 = kb.v(lambda e: e.tensor_tensor(rows_var, rows_mean, rows_mean, op=ALU.mult),
                      deps=[r1])
            r4 = kb.v(lambda e: e.scalar_tensor_tensor(rows_var, rows_var, -1.0, rows_sq,
                                                       op0=ALU.mult, op1=ALU.add),
                      deps=[r2, r3])
            r5 = kb.v(lambda e, eps=eps: e.tensor_scalar(rows_var, rows_var, 1.0, eps,
                                                         op0=ALU.mult, op1=ALU.add), deps=[r4])
            r6 = kb.v(lambda e: e.reciprocal(rows_var, rows_var), deps=[r5])
            r7 = kb.act(rows_rstd, rows_var, AF.Sqrt, deps=[r6])
            r8 = kb.v(lambda e: e.tensor_tensor(rows_mr, rows_mean, rows_rstd, op=ALU.mult),
                      deps=[r7])
            b1_ = p.emit("tensor", lambda e: e.matmul(qA[:, 0:T1], onesf[:, :],
                                                      rows_rstd, start=True, stop=True),
                         deps=[r7, r1])
            b2_ = p.emit("tensor", lambda e: e.matmul(qB[:, 0:T1], onesf[:, :],
                                                      rows_mr, start=True, stop=True),
                         deps=[r8, r2])
            tf = None
            for mc in range(8):
                c1 = kb.v(lambda e, mc=mc: e.tensor_tensor(
                    lnscr[:, 0:T1], yf[:, mc * T1:(mc + 1) * T1],
                    qA[:, 0:T1], op=ALU.mult), deps=[b1_])
                c2 = kb.v(lambda e, mc=mc: e.tensor_tensor(
                    lnscr[:, 0:T1], lnscr[:, 0:T1],
                    qB[:, 0:T1], op=ALU.subtract), deps=[b2_])
                c3 = kb.v(lambda e, mc=mc: e.tensor_scalar(
                    outf[:, mc * T1:(mc + 1) * T1], lnscr[:, 0:T1],
                    lnv[gname][:, mc:mc + 1], lnv[bname][:, mc:mc + 1],
                    op0=ALU.mult, op1=ALU.add), deps=[c2])
                tf = kb.v(lambda e, mc=mc: e.tensor_copy(
                    outbf[:, mc * T1:(mc + 1) * T1], outf[:, mc * T1:(mc + 1) * T1]),
                    deps=[c3])
            prev["rA"] = tf
            prev["rB"] = tf
            return tf

        st_s = t_kv
        for ti in range(nb):
            t_x32 = kb.dma("sync", xa[:, :].rearrange("p (kc t) -> p kc t", t=T1),
                           xn32_d[:, ti * T1:(ti + 1) * T1].rearrange("(kc p) t -> p kc t", p=128),
                           deps=[prev.get("x32free")] + (af if ti == 0 else []))
            t_q = kb.dma("sync", qTt[:, :].rearrange("p (kc t) -> p kc t", t=T1),
                         qT_dram[:, ti * T1:(ti + 1) * T1].rearrange("(kc p) t -> p kc t", p=128),
                         deps=[prev.get("qfree")] + (af if ti == 0 else []))
            t_sq = kb.dma("sync", sqcq[:, :], sqq_dram[:, ti * 64:(ti + 1) * 64],
                          deps=[prev.get("sqfree")] + (af if ti == 0 else []))
            rot = [qA, qB]
            rotdep = {0: [prev.get("rA")], 1: [prev.get("rB")]}

            def flush_pq(st):
                par, tc = st["par"], st["tc"]
                bsP = bscrP[par]
                tpq = kb.v(lambda e, fsP=st["fs"], bsP=bsP: e.tensor_scalar(
                    bsP[:, :16 * 256], fsP[:, :16 * 256], 1e-6, None, op0=ALU.add),
                    deps=[st["texp"], prev.get(("bsfree", par))])
                prev[("fsfree", par)] = tpq
                tevp = None
                ttr = None
                for bq in range(4):
                    ttr = None
                    for sl in range(8):
                        idx = bq * 8 + sl
                        h_, j_ = idx // 2, idx % 2
                        src = bsP[:, h_ * 256 + j_ * 128: h_ * 256 + j_ * 128 + 128]
                        ttr = p.emit("tensor", lambda e, src=src, sl=sl: e.matmul(
                            qX[:, sl * 128:(sl + 1) * 128], src, eye16[:, :],
                            start=True, stop=True, is_transpose=True),
                            deps=[tpq, prev.get("qXfree"), ttr])
                    tevp = kb.v(lambda e, bq=bq: e.tensor_copy(
                        pqT[:, bq * 1024:(bq + 1) * 1024], qX[:, 0:1024]), deps=[ttr])
                    prev["qXfree"] = tevp
                prev[("bsfree", par)] = ttr
                nev = []
                for h in range(16):
                    bank = rot[h % 2]
                    tmm = p.emit("tensor", lambda e, h=h, bank=bank: e.matmul(
                        ap3(bank, 0, [[1, 65]]), pqT[:, (2 * h) * 128:(2 * h) * 128 + 128],
                        kvs(h, 0), start=True, stop=False, skip_group_check=True),
                        deps=[tevp, t_kv] + rotdep[h % 2])
                    tmm = p.emit("tensor", lambda e, h=h, bank=bank: e.matmul(
                        ap3(bank, 0, [[1, 65]]), pqT[:, (2 * h + 1) * 128:(2 * h + 1) * 128 + 128],
                        kvs(h, 1), start=False, stop=True, skip_group_check=True),
                        deps=[])
                    tev = kb.v(lambda e, h=h, bank=bank: e.tensor_copy(
                        num_sb[:, h * 65:(h + 1) * 65], ap3(bank, 0, [[1, 65]])),
                        deps=[tmm, prev.get("numfree")])
                    rotdep[h % 2] = [tev]
                    nev.append(tev)
                trd = kb.v(lambda e: e.reciprocal(rd16[:, :], ap3(num_sb, 64, [[65, 16]])),
                           deps=nev + [prev.get("numfree")])
                tsc = None
                for h in range(16):
                    tsc = kb.act(attn[:, h * 64:(h + 1) * 64],
                                 num_sb[:, h * 65:h * 65 + 64], AF.Identity,
                                 bias=zc[:, 0:1], scale=rd16[:, h:h + 1],
                                 deps=[trd, t_zc, prev.get("attnfree")])
                prev["numfree"] = tsc
                tat = None
                ttr = None
                for dq in range(2):
                    ttr = None
                    for jj in range(4):
                        asrc = attn[:, (dq * 4 + jj) * 128: (dq * 4 + jj) * 128 + 128]
                        ttr = p.emit("tensor", lambda e, asrc=asrc, jj=jj: e.matmul(
                            qX[:, jj * 128:(jj + 1) * 128], asrc, eye16[:, :],
                            start=True, stop=True, is_transpose=True),
                            deps=[tsc, prev.get("qXfree"), ttr])
                    dstp = ap3(bsc2, (dq * 4) * T1 + tc * 128, [[T1, 4], [1, 128]])
                    tat = kb.v(lambda e, dstp=dstp: e.tensor_copy(dstp, ap3(qX, 0, [[128, 4], [1, 128]])),
                               deps=[ttr])
                    prev["qXfree"] = tat
                prev["attnfree"] = ttr
                return tat

            pend = None
            tat = None
            for tc in range(4):
                par = tc % 2
                fsP, stP, cnP = fscrP[par], stbP[par], cngP[par]
                prj = []
                for h in range(16):
                    bank = rot[h % 2]
                    lh = qTt[(h % 2) * 64:(h % 2) * 64 + 64,
                             (h // 2) * T1 + tc * 128:(h // 2) * T1 + tc * 128 + 128]
                    omx = omT[(h % 2) * 64:(h % 2) * 64 + 64, :]
                    tmm = p.emit("tensor", lambda e, lh=lh, bank=bank, omx=omx: e.matmul(
                        ap3(bank, 0, [[1, 256]]), lh, omx, start=True, stop=True),
                        deps=rotdep[h % 2] + [t_q])
                    tev = kb.v(lambda e, h=h, bank=bank, fsP=fsP, stP=stP: e.tensor_scalar(
                        fsP[:, h * 256:(h + 1) * 256], ap3(bank, 0, [[1, 256]]),
                        1.0, None, op0=ALU.mult, op1=ALU.max, accum_out=stP[:, h:h + 1]),
                        deps=[tmm, prev.get(("fsfree", par)), t_sq])
                    rotdep[h % 2] = [tev]
                    prj.append(tev)
                tcn = kb.v(lambda e, tc=tc, stP=stP, cnP=cnP: e.scalar_tensor_tensor(
                    cnP[:, :], stP[:, :], -1.0, sqcq[:, tc * 16:(tc + 1) * 16],
                    op0=ALU.mult, op1=ALU.subtract), deps=prj + [prev.get(("cngfree", par))])
                texp = None
                for h in range(16):
                    texp = kb.act(fsP[:, h * 256:(h + 1) * 256], fsP[:, h * 256:(h + 1) * 256],
                                  AF.Exp, bias=cnP[:, h:h + 1], deps=[tcn])
                prev[("cngfree", par)] = texp
                st = {"par": par, "tc": tc, "fs": fsP, "texp": texp}
                if pend is not None:
                    tat = flush_pq(pend)
                pend = st
            tat = flush_pq(pend)
            wv_ev = []
            for mc in range(8):
                bank = rot[mc % 2]
                pairs = [(wos[:, kc * D + mc * 128: kc * D + mc * 128 + 128],
                          bsc2[:, kc * T1:(kc + 1) * T1]) for kc in range(8)]
                tmm = kb.mm(ap3(bank, 0, [[1, T1]]), pairs, deps=[tat, t_wo] + rotdep[mc % 2])
                tev = kb.v(lambda e, mc=mc, bank=bank: e.tensor_tensor(
                    y1[:, mc * T1:(mc + 1) * T1], ap3(bank, 0, [[1, T1]]),
                    xa[:, mc * T1:(mc + 1) * T1], op=ALU.add), deps=[tmm, t_x32])
                rotdep[mc % 2] = [tev]
                wv_ev.append(tev)
            prev["rA"], prev["rB"] = rotdep[0][0], rotdep[1][0]
            t_xa = ln_T(y1, attn[:, 0:8 * T1] if False else bscr, pqT[:, 0:8 * T1] if False else bsc2,
                        "ln1g_c", "ln1b_c", xa, bscr, wv_ev)
            h_ev = []
            for m in range(32):
                g_ = m // 4
                if m % 4 == 0:
                    w1t = kb.dma("gpsimd", w1g[g_ % 2][:, :].rearrange("p (kc c) -> p kc c", c=512),
                                 w1_d[:, g_ * 512:(g_ + 1) * 512].rearrange("(kc p) c -> p kc c", p=128),
                                 deps=[prev.get(("w1gfree", g_ % 2))])
                    prev[("w1gld", g_ % 2)] = w1t
                mloc = m % 4
                wb1 = w1g[g_ % 2]
                ph = pH[m % 2]
                pairs = [(wb1[:, kc * 512 + mloc * 128: kc * 512 + mloc * 128 + 128],
                          bscr[:, kc * T1:(kc + 1) * T1]) for kc in range(8)]
                tmm = kb.mm(ap3(ph, 0, [[1, T1]]), pairs,
                            deps=[t_xa, prev.get(("w1gld", g_ % 2)), prev.get(("hfree", m % 2))])
                the = kb.act(h1_sb[:, m * T1:(m + 1) * T1], ap3(ph, 0, [[1, T1]]), AF.Gelu,
                             bias=b1c[:, m:m + 1], deps=[tmm, prev.get("h1sbfree")])
                prev[("hfree", m % 2)] = the
                h_ev.append(the)
                if m % 4 == 3:
                    prev[("w1gfree", g_ % 2)] = p.fence("tensor")
            y2ev = []
            for mp in range(8):
                w2t = kb.dma("gpsimd", w2m[mp % 2][:, :].rearrange("p (mh c) -> p mh c", c=128),
                             w2_d[:, mp * 128:(mp + 1) * 128].rearrange("(mh p) c -> p mh c", p=128),
                             deps=[prev.get(("w2mfree", mp % 2))])
                py = pY[mp % 2]
                w2b = w2m[mp % 2]

                def f2(e, py=py, w2b=w2b):
                    r = None
                    for m_ in range(32):
                        r = e.matmul(ap3(py, 0, [[1, T1]]),
                                     w2b[:, m_ * 128:(m_ + 1) * 128],
                                     h1_sb[:, m_ * T1:(m_ + 1) * T1],
                                     start=(m_ == 0), stop=(m_ == 31))
                    return r
                tmm = p.emit("tensor", f2, deps=[h_ev[-1], w2t, prev.get(("pYfree", mp % 2))])
                tev = kb.v(lambda e, mp=mp, py=py: e.scalar_tensor_tensor(
                    y1[:, mp * T1:(mp + 1) * T1], ap3(py, 0, [[1, T1]]),
                    lnv["b2_c"][:, mp:mp + 1], xa[:, mp * T1:(mp + 1) * T1],
                    op0=ALU.add, op1=ALU.add), deps=[tmm])
                prev[("pYfree", mp % 2)] = tev
                prev[("w2mfree", mp % 2)] = tev
                y2ev.append(tev)
            prev["h1sbfree"] = p.fence("tensor")
            t_s = ln_T(y1, bscr, bsc2, "ln2g_c", "ln2b_c", xa, bscr, y2ev)
            st_s = kb.dma("qdma", s_d[:, ti * T1:(ti + 1) * T1].rearrange("(kc p) t -> p kc t", p=128),
                          xa[:, :].rearrange("p (kc t) -> p kc t", t=T1), deps=[t_s])
            if dev and ti == 0:
                dtap("attn0", attn[:, :], 1024, [tat], BF16)
                dtap("y2_0", y1[:, :], 4096, y2ev + [st_s])
            prev["qfree"] = st_s
            prev["sqfree"] = st_s
            prev["sfree"] = st_s
            prev["x32free"] = st_s
    es2.close()
    return st_s


def _conv_w_c(w, nch):
    K = w.shape[1]
    return np.ascontiguousarray(w.reshape(nch // 128, 128, K).transpose(1, 0, 2).reshape(128, -1)).astype(np.float32)


def _vec_c(b):
    return np.ascontiguousarray(b.reshape(-1, 128).T).astype(np.float32)


def make_l0_inputs(inputs, core):
    b, h = core // 2, core % 2
    x = np.asarray(inputs["L0_attn_u"], np.float32)
    t0 = h * NTOK
    if h == 0:
        xin = np.concatenate([np.zeros((HALO, D), np.float32), x[b, 0:NTOK]], axis=0)
    else:
        xin = x[b, t0 - HALO:t0 + NTOK]
    m = {
        "x": np.ascontiguousarray(xin),
        "w_in": np.asarray(inputs["L0_attn_W_in"], np.float32),
        "w_out": np.asarray(inputs["L0_attn_W_out"], np.float32),
        "fw": np.asarray(inputs["L0_ffn_weight"], np.float32),
        "b_in_c": _vec_c(np.asarray(inputs["L0_attn_b_in"], np.float32)),
        "short_w_c": _conv_w_c(np.asarray(inputs["L0_attn_short_w"], np.float32), D3),
        "short_b_c": _vec_c(np.asarray(inputs["L0_attn_short_b"], np.float32)),
        "inner_w_c": _conv_w_c(np.asarray(inputs["L0_attn_inner_w"], np.float32), D),
        "inner_b_c": _vec_c(np.asarray(inputs["L0_attn_inner_b"], np.float32)),
        "b_out_c": _vec_c(np.asarray(inputs["L0_attn_b_out"], np.float32)),
        "fb_c": _vec_c(np.asarray(inputs["L0_ffn_bias"], np.float32)),
        "eye32": np.eye(128, dtype=np.float32),
        "eye16": np.eye(128).astype(ml_dtypes.bfloat16),
        "hmask": np.full((128, 1), 0.0 if h == 0 else 1.0, np.float32),
    }
    return m


def build_moe(kb, dev=False):
    nc, p = kb.nc, kb.p
    fused = kb.cfg.get("fused", False)
    es3 = contextlib.ExitStack()
    es3sb = lambda n_, sh, dt_: es3.enter_context(nc.sbuf_tensor('l3_' + n_, list(sh), dt_))
    if fused:
        xn_d = kb.dram["xn1t16_d"]
        xn32_d = kb.dram["xn1t32_d"]
        x2_d = kb.dram["x2_d"]
        s_d = kb.dram["s_d"]
    else:
        xn_d = kb.dt("xn16", [D, NTOK], BF16, "ExternalInput")
        xn32_d = kb.dt("xn32m", [D, NTOK], F32, "ExternalInput")
        x2_d = kb.dt("x2", [NTOK, D], F32, "ExternalInput")
        s_d = kb.dt("s_in", [D, NTOK], F32, "ExternalInput")
    wgT_d = kb.dt("wgT", [D, 8], F32, "ExternalInput")
    w1_d = kb.dt("mw1", [8 * D, 4 * D], F32, "ExternalInput")
    w2_d = kb.dt("mw2", [8 * 4 * D, D], F32, "ExternalInput")
    b1_d = kb.dt("mb1_c", [128, 8 * 32], F32, "ExternalInput")
    b2_d = kb.dt("mb2e", [8, D], F32, "ExternalInput")
    eye32_d = kb.dt("eye32", [128, 128], F32, "ExternalInput")
    x3_d = kb.dt("x3", [NTOK, D], F32, "ExternalOutput")
    acc_d = kb.dt("acc_sc", [NTOK, D], F32)

    wgs = es3sb("wgs", [128, 8 * 8], F32)
    b2sb = es3sb("b2sb", [8, D], F32)
    b1c = es3sb("mb1c_s", [128, 8 * 32], F32)
    eye32 = es3sb("eye32_s", [128, 128], F32)
    cw = es3sb("cw", [128, 32 * 8], F32)
    gts = es3sb("gts", [128, 32], F32)
    e8 = es3sb("e8", [128, 8], F32)
    pm8 = es3sb("pm8", [128, 8], F32)
    m81 = es3sb("m81", [128, 8], F32)
    m82 = es3sb("m82", [128, 8], F32)
    gcol = es3sb("gcol", [128, 4], F32)
    ystage = es3sb("ystage", [128, 4 * D], F32)
    pdep3 = list(kb.cfg.get("phase_dep_l3", []))
    kb.dma("sync", wgs[:, :], wgT_d[:, :].rearrange("(kc p) e -> p kc e", p=128), deps=pdep3)
    kb.dma("sync", b2sb[:, :], b2_d[:, :])
    kb.dma("sync", b1c[:, :], b1_d[:, :])
    kb.dma("sync", eye32[:, :], eye32_d[:, :])
    t_w = [p.fence("gdma"), p.fence("sdma")] + pdep3
    accdma = {}

    with nc.psum_tensor("mH0", [128, 512], F32) as mH0, \
         nc.psum_tensor("mH1", [128, 512], F32) as mH1, \
         nc.psum_tensor("mY0", [128, 512], F32) as mY0, \
         nc.psum_tensor("mY1", [128, 512], F32) as mY1, \
         nc.psum_tensor("mY2", [128, 512], F32) as mY2, \
         nc.psum_tensor("mY3", [128, 512], F32) as mY3, \
         nc.sbuf_tensor("xg0", [128, 8 * 512], BF16) as xg0, \
         nc.sbuf_tensor("xg1", [128, 8 * 512], BF16) as xg1:
        pH = [mH0, mH1]
        pY = [mY0, mY1, mY2, mY3]
        xg = [xg0, xg1]
        prev = {}
        es_g = contextlib.ExitStack()
        cwT = es_g.enter_context(nc.sbuf_tensor("cwT", [8, 32 * 128], F32))
        grow = es_g.enter_context(nc.sbuf_tensor("grow", [8, 512], F32))
        xg32 = es_g.enter_context(nc.sbuf_tensor("xg32", [128, 8 * 512], F32))
        for tg in range(8):
            t_x = kb.dma("sync", xg32[:, :].rearrange("p (kc t) -> p kc t", t=512),
                         xn32_d[:, tg * 512:(tg + 1) * 512].rearrange("(kc p) t -> p kc t", p=128),
                         deps=t_w + [prev.get("xg32free")])
            gmm = None
            for kc in range(8):
                gmm = p.emit("tensor", lambda e, kc=kc: e.matmul(
                    mH0[0:8, 0:512], wgs[:, kc * 8:(kc + 1) * 8],
                    xg32[:, kc * 512:(kc + 1) * 512], start=(kc == 0), stop=(kc == 7)),
                    deps=[t_x, prev.get("growfree")])
            tg1 = kb.v(lambda e: e.tensor_copy(grow[:, :], mH0[0:8, 0:512]), deps=[gmm])
            prev["growfree"] = tg1
            ttr = None
            for j in range(4):
                ttr = p.emit("tensor", lambda e, j=j: e.matmul(
                    mH1[:, j * 8:(j + 1) * 8], grow[:, j * 128:(j + 1) * 128],
                    eye32[0:8, 0:8], start=True, stop=True, is_transpose=True),
                    deps=[tg1, prev.get("mh1free"), ttr])
            tg2 = kb.v(lambda e: e.tensor_copy(gts[:, :], mH1[:, 0:32]), deps=[ttr])
            prev["mh1free"] = tg2
            for c in range(4):
                gch = tg * 4 + c
                sl = gts[:, c * 8:(c + 1) * 8]
                cws = cw[:, gch * 8:(gch + 1) * 8]
                t1 = kb.v(lambda e, sl=sl: e.tensor_reduce(gcol[:, 0:1], sl, axis=AX.X,
                                                           op=ALU.max, negate=True), deps=[tg2])
                t2 = kb.act(e8[:, :], sl, AF.Exp, bias=gcol[:, 0:1], deps=[t1])
                t3 = kb.v(lambda e: e.tensor_reduce(gcol[:, 1:2], e8[:, :], axis=AX.X,
                                                    op=ALU.add), deps=[t2])
                t4 = kb.v(lambda e: e.reciprocal(gcol[:, 1:2], gcol[:, 1:2]), deps=[t3])
                t5 = kb.v(lambda e, cws=cws: e.tensor_scalar(cws, e8[:, :], gcol[:, 1:2],
                                                             None, op0=ALU.mult), deps=[t4])
                t6 = kb.v(lambda e, cws=cws: e.tensor_reduce(gcol[:, 2:3], cws, axis=AX.X,
                                                             op=ALU.max), deps=[t5])
                t7 = kb.v(lambda e, cws=cws: e.tensor_scalar(m81[:, :], cws, gcol[:, 2:3],
                                                             None, op0=ALU.is_ge), deps=[t6])
                t8 = kb.v(lambda e: e.tensor_scalar(m82[:, :], m81[:, :], -1.0, 1.0,
                                                    op0=ALU.mult, op1=ALU.add), deps=[t7])
                t9 = kb.v(lambda e, cws=cws: e.tensor_tensor(pm8[:, :], cws, m82[:, :],
                                                             op=ALU.mult), deps=[t8])
                t10 = kb.v(lambda e: e.tensor_reduce(gcol[:, 3:4], pm8[:, :], axis=AX.X,
                                                     op=ALU.max), deps=[t9])
                t11 = kb.v(lambda e: e.tensor_scalar(m82[:, :], pm8[:, :], gcol[:, 3:4],
                                                     None, op0=ALU.is_ge), deps=[t10])
                t12 = kb.v(lambda e: e.tensor_tensor(m81[:, :], m81[:, :], m82[:, :],
                                                     op=ALU.add), deps=[t11])
                t13 = kb.v(lambda e, cws=cws: e.tensor_tensor(cws, cws, m81[:, :],
                                                              op=ALU.mult), deps=[t12])
                ttr2 = p.emit("tensor", lambda e, cws=cws: e.matmul(
                    mH1[0:8, 40:168], cws, eye32[:, :], start=True, stop=True,
                    is_transpose=True), deps=[t13, prev.get("cwtfree")])
                tcv = kb.v(lambda e, gch=gch: e.tensor_copy(
                    cwT[:, gch * 128:(gch + 1) * 128], mH1[0:8, 40:168]), deps=[ttr2])
                prev["cwtfree"] = tcv
                bmm = p.emit("tensor", lambda e, gch=gch, c=c: e.matmul(
                    pY[c % 2][:, 0:512], cwT[0:8, gch * 128:(gch + 1) * 128],
                    b2sb[:, 0:512], start=True, stop=True),
                    deps=[tcv, prev.get(("pyfree", c % 2))])
                bmm2 = p.emit("tensor", lambda e, gch=gch, c=c: e.matmul(
                    pY[2 + c % 2][:, 0:512], cwT[0:8, gch * 128:(gch + 1) * 128],
                    b2sb[:, 512:1024], start=True, stop=True),
                    deps=[])
                ev1 = kb.v(lambda e, c=c: e.tensor_copy(ystage[:, c * D:c * D + 512],
                                                        pY[c % 2][:, 0:512]), deps=[bmm, prev.get("ystagefree")])
                ev2 = kb.v(lambda e, c=c: e.tensor_copy(ystage[:, c * D + 512:(c + 1) * D],
                                                        pY[2 + c % 2][:, 0:512]), deps=[bmm2])
                prev[("pyfree", c % 2)] = ev1
                prev[("pyfree", 2 + c % 2)] = ev2
                prev["mh1free"] = tcv
            tini = kb.dma("gpsimd", acc_d[tg * 512:(tg + 1) * 512, :].rearrange("(c p) d -> p c d", p=128),
                          ystage[:, :].rearrange("p (c d) -> p c d", d=D),
                          deps=[prev[("pyfree", 0)], prev[("pyfree", 1)],
                                prev[("pyfree", 2)], prev[("pyfree", 3)]])
            accdma[tg] = tini
            prev["ystagefree"] = tini
            prev["xg32free"] = p.fence("tensor")
        prev[("hfree", 0)] = prev[("hfree", 1)] = p.fence("vector")
        if dev:
            acc0_d = kb.dt("acc0_dev", [NTOK, D], F32, "ExternalOutput")
            cw_d = kb.dt("cw_dev", [128, 256], F32, "ExternalOutput")
            kb.dma("qdma", cw_d[:, :], cw[:, :], deps=[p.fence("vector")])
            for tg in range(8):
                snap = kb.dma("qdma", acc0_d[tg * 512:(tg + 1) * 512, :],
                              acc_d[tg * 512:(tg + 1) * 512, :], deps=[accdma[tg]])
                accdma[tg] = snap
        es_g.close()
        with nc.sbuf_tensor("w1e", [128, 8 * 4 * D], BF16) as w1e, \
             nc.sbuf_tensor("w2e", [128, 32 * D], BF16) as w2e, \
             nc.sbuf_tensor("h1m", [128, 32 * 512], BF16) as h1m:

            def load_w1(ex, deps):
                return [kb.dma("gpsimd",
                               ap3(w1e, half * 4 * 4 * D, [[4 * D, 4], [1, 4 * D]]),
                               w1_d[ex * D + half * 512:ex * D + (half + 1) * 512, :]
                               .rearrange("(kc p) m -> p kc m", p=128),
                               deps=deps) for half in range(2)]

            def load_w2(ex, deps):
                return [kb.dma("gpsimd",
                               ap3(w2e, half * 16 * D, [[D, 16], [1, D]]),
                               w2_d[ex * 4 * D + half * 2048:ex * 4 * D + (half + 1) * 2048, :]
                               .rearrange("(kc p) d -> p kc d", p=128),
                               deps=deps) for half in range(2)]

            tw1 = load_w1(0, t_w)
            tw2 = load_w2(0, t_w)
            nw1 = nw2 = None
            for ex in range(8):
                for t_ in range(8):
                    xb = xg[t_ % 2]
                    t_x = kb.dma("sync", xb[:, :].rearrange("p (kc t) -> p kc t", t=512),
                                 xn_d[:, t_ * 512:(t_ + 1) * 512].rearrange("(kc p) t -> p kc t", p=128),
                                 deps=[prev.get(("xfree", t_ % 2))])
                    hev = []
                    for m in range(32):
                        ph = pH[m % 2]
                        pairs = [(w1e[:, kc * 4 * D + m * 128: kc * 4 * D + m * 128 + 128],
                                  xb[:, kc * 512:(kc + 1) * 512]) for kc in range(8)]
                        tmm = kb.mm(ap3(ph, 0, [[1, 512]]), pairs,
                                    deps=[t_x, tw1[0], tw1[1], prev.get(("hfree", m % 2))])
                        the = kb.act(h1m[:, m * 512:(m + 1) * 512], ap3(ph, 0, [[1, 512]]),
                                     AF.Gelu, bias=b1c[:, ex * 32 + m:ex * 32 + m + 1],
                                     deps=[tmm, prev.get("h1mfree")])
                        prev[("hfree", m % 2)] = the
                        hev.append(the)
                    if t_ == 7 and ex < 7:
                        nw1 = load_w1(ex + 1, [p.fence("tensor")])
                    for tc in range(4):
                        for dh in range(2):
                            py = pY[2 * (tc % 2) + dh]

                            def f2(e, tc=tc, dh=dh, py=py):
                                r = None
                                for kc in range(32):
                                    r = e.matmul(ap3(py, 0, [[1, 512]]),
                                                 h1m[:, kc * 512 + tc * 128: kc * 512 + tc * 128 + 128],
                                                 w2e[:, kc * D + dh * 512: kc * D + dh * 512 + 512],
                                                 start=(kc == 0), stop=(kc == 31))
                                return r
                            tmm = p.emit("tensor", f2,
                                         deps=[hev[-1], tw2[0], tw2[1], prev.get(("pyfree", 2 * (tc % 2) + dh))])
                            tev = kb.v(lambda e, tc=tc, dh=dh, py=py, t_=t_, ex=ex: e.tensor_scalar(
                                ystage[:, tc * D + dh * 512: tc * D + (dh + 1) * 512],
                                ap3(py, 0, [[1, 512]]),
                                cw[:, (t_ * 4 + tc) * 8 + ex:(t_ * 4 + tc) * 8 + ex + 1],
                                None, op0=ALU.mult),
                                deps=[tmm, prev.get("ystagefree")])
                            prev[("pyfree", 2 * (tc % 2) + dh)] = tev
                    if dev and ex == 0 and t_ == 0:
                        ys_d = kb.dt("ys00_dev", [128, 4 * D], F32, "ExternalOutput")
                        h1_d = kb.dt("h1m_dev", [128, 4 * 512], F32, "ExternalOutput")
                        kb.dma("qdma", ys_d[:, :], ystage[:, :],
                               deps=[prev[("pyfree", 0)], prev[("pyfree", 1)],
                                     prev[("pyfree", 2)], prev[("pyfree", 3)]])
                        tqh = kb.v(lambda e: e.tensor_copy(ystage[:, 0:0 + 2048] if False else ystage[:, 0:2048], h1m[:, 0:2048]), deps=[hev[-1]]) if False else None
                        h1tap = kb.dt("h1tap", [128, 2048], BF16, "ExternalOutput")
                        kb.dma("qdma", h1tap[:, :], h1m[:, 0:2048], deps=[hev[-1]])
                    tacc = kb.dma("gpsimd", acc_d[t_ * 512:(t_ + 1) * 512, :].rearrange("(c p) d -> p c d", p=128),
                                  ystage[:, :].rearrange("p (c d) -> p c d", d=D),
                                  deps=[accdma.get(t_), prev[("pyfree", 0)], prev[("pyfree", 1)],
                                        prev[("pyfree", 2)], prev[("pyfree", 3)]] +
                                       ([p.fence("qdma")] if (dev and ex == 0 and t_ == 0) else []), accum=True)
                    if t_ == 7 and ex < 7:
                        nw2 = load_w2(ex + 1, [p.fence("tensor")])
                    accdma[t_] = tacc
                    prev["ystagefree"] = tacc
                    prev["h1mfree"] = p.fence("tensor")
                    prev[("xfree", t_ % 2)] = prev["h1mfree"]
                if ex < 7:
                    tw1, tw2 = nw1, nw2
        with nc.sbuf_tensor("sT", [128, 8 * 512], F32) as sT, \
             nc.sbuf_tensor("stok", [128, 4 * D], F32) as stok, \
             nc.sbuf_tensor("x2t", [128, 4 * D], F32) as x2t:
            last = None
            for t_ in range(8):
                t_acc = kb.dma("sync", ystage[:, :].rearrange("p (c d) -> p c d", d=D),
                               acc_d[t_ * 512:(t_ + 1) * 512, :].rearrange("(c p) d -> p c d", p=128),
                               deps=[accdma[t_], prev.get("ystagefree"), prev.get("ystagefree2"), last])
                t_s = kb.dma("sync", sT[:, :].rearrange("p (kc t) -> p kc t", t=512),
                             s_d[:, t_ * 512:(t_ + 1) * 512].rearrange("(kc p) t -> p kc t", p=128),
                             deps=[prev.get("sTfree")])
                t_x2 = kb.dma("sync", x2t[:, :].rearrange("p (c d) -> p c d", d=D),
                              x2_d[t_ * 512:(t_ + 1) * 512, :].rearrange("(c p) d -> p c d", p=128),
                              deps=[prev.get("x2tfree")])
                tg_ = kb.act(ystage[:, :], ystage[:, :], AF.Silu, deps=[t_acc])
                trd = prev.get("ph0free")
                hguard = [prev.get(("hfree", 0)), prev.get(("hfree", 1))]
                for tc in range(4):
                    for dq in range(2):
                        ttr = None
                        for j in range(4):
                            ssrc = sT[:, (dq * 4 + j) * 512 + tc * 128: (dq * 4 + j) * 512 + tc * 128 + 128]
                            ttr = p.emit("tensor", lambda e, ssrc=ssrc, j=j: e.matmul(
                                mH0[:, j * 128:(j + 1) * 128], ssrc, eye32[:, :],
                                start=True, stop=True, is_transpose=True),
                                deps=[t_s, trd, ttr] + hguard)
                        dstp = ap3(stok, tc * D + dq * 4 * 128, [[128, 4], [1, 128]])
                        trd = kb.v(lambda e, dstp=dstp: e.tensor_copy(
                            dstp, ap3(mH0, 0, [[128, 4], [1, 128]])), deps=[ttr])
                prev["ph0free"] = trd
                tm1 = kb.v(lambda e: e.tensor_tensor(ystage[:, :], ystage[:, :], stok[:, :],
                                                     op=ALU.mult), deps=[tg_, trd])
                tm2 = kb.v(lambda e: e.tensor_tensor(x2t[:, :], x2t[:, :], ystage[:, :],
                                                     op=ALU.add), deps=[tm1, t_x2])
                last = kb.dma("qdma", x3_d[t_ * 512:(t_ + 1) * 512, :].rearrange("(c p) d -> p c d", p=128),
                              x2t[:, :].rearrange("p (c d) -> p c d", d=D), deps=[tm2])
                prev["ystagefree2"] = tm2
                prev["sTfree"] = last
                prev["x2tfree"] = last
    es3.close()
    return last



def make_e16f():
    e = np.zeros((128, 128), np.float32)
    for mc in range(8):
        for p_ in range(128):
            e[p_, mc * 16 + 2 * mc + p_ // 64] = 1.0
    return e


def make_l2_inputs(inputs, core, xn1t16_all, xn1t32_all):
    b, h = core // 2, core % 2
    m = {}
    if xn1t16_all[core] is not None:
        own = np.asarray(xn1t16_all[core])
        partner = np.asarray(xn1t16_all[2 * b + (1 - h)])
        m["xnf"] = np.ascontiguousarray(np.concatenate([own, partner], axis=1))
        m["xn32"] = np.ascontiguousarray(np.asarray(xn1t32_all[core]))
    m.update({
        "wq": np.ascontiguousarray(np.asarray(inputs["L1_attn_wq"], np.float32).reshape(D, D)),
        "wk": np.ascontiguousarray(np.asarray(inputs["L1_attn_wk"], np.float32).reshape(D, D)),
        "wv": np.ascontiguousarray(np.asarray(inputs["L1_attn_wv"], np.float32).reshape(D, D)),
        "wo": np.asarray(inputs["L1_attn_wo"], np.float32),
        "omT": np.ascontiguousarray(np.concatenate([np.asarray(inputs["L1_attn_omega"], np.float32).T] * 2, axis=0)),
        "e16f": make_e16f(),
        "ln1g_c": _vec_c(np.asarray(inputs["L1_attn_ln1_g"], np.float32)),
        "ln1b_c": _vec_c(np.asarray(inputs["L1_attn_ln1_b"], np.float32)),
        "ln2g_c": _vec_c(np.asarray(inputs["L1_attn_ln2_g"], np.float32)),
        "ln2b_c": _vec_c(np.asarray(inputs["L1_attn_ln2_b"], np.float32)),
        "b1_c": _vec_c(np.asarray(inputs["L1_attn_b1"], np.float32)),
        "b2_c": _vec_c(np.asarray(inputs["L1_attn_b2"], np.float32)),
        "w1": np.asarray(inputs["L1_attn_w1"], np.float32),
        "w2": np.asarray(inputs["L1_attn_w2"], np.float32),
        "eye16": np.eye(128).astype(ml_dtypes.bfloat16),
        "eye32": np.eye(128, dtype=np.float32),
        "onesb": np.ones((128, 1), ml_dtypes.bfloat16),
        "onesf": np.ones((1, 128), np.float32),
    })
    return m


def build_l2_program(dev=False, na=NT1, nb=8, skip=(), io=False):
    kb = KB({"dev": dev, "skip": set(skip), "io": io})
    with kb.ctx:
        build_l1attn(kb, dev=dev, na=na, nb=nb)
        kb.p.finalize(final_waits=[kb.p.fence(e) for e in Prog.ENGINES])
    return kb


def make_l3_inputs(inputs, core, xn1t16_all, x2_all, s_all, x32m_all):
    b1 = np.asarray(inputs["L1_ffn_b1"], np.float32)
    m = {}
    if xn1t16_all[core] is not None:
        m["xn16"] = np.ascontiguousarray(np.asarray(xn1t16_all[core]))
        m["xn32m"] = np.ascontiguousarray(np.asarray(x32m_all[core], np.float32))
        m["x2"] = np.ascontiguousarray(np.asarray(x2_all[core], np.float32))
        m["s_in"] = np.ascontiguousarray(np.asarray(s_all[core], np.float32))
    m.update({
        "wgT": np.ascontiguousarray(np.asarray(inputs["L1_ffn_w_gate"], np.float32).T),
        "mw1": np.ascontiguousarray(np.asarray(inputs["L1_ffn_W1"], np.float32).reshape(8 * D, 4 * D)),
        "mw2": np.ascontiguousarray(np.asarray(inputs["L1_ffn_W2"], np.float32).reshape(8 * 4 * D, D)),
        "mb1_c": np.concatenate([_vec_c(b1[e]) for e in range(8)], axis=1),
        "mb2e": np.asarray(inputs["L1_ffn_b2"], np.float32),
        "eye32": np.eye(128, dtype=np.float32),
    })
    return m


def build_l3_program(dev=False):
    kb = KB({"dev": dev})
    with kb.ctx:
        build_moe(kb, dev=dev)
        kb.p.finalize(final_waits=[kb.p.fence(e) for e in Prog.ENGINES])
    return kb


def build_l0_program(dev=False, io=False):
    kb = KB({"dev": dev, "io": io})
    with kb.ctx:
        last = build_layer0(kb, dev=dev)
        kb.p.finalize(final_waits=[kb.p.fence(e) for e in Prog.ENGINES])
    return kb


def build_fused_program(dev=False):
    kb = KB({"fused": True, "io": dev})
    with kb.ctx:
        build_layer0(kb, dev=False)
        kb.cfg["phase_dep_l2"] = [kb.p.fence(s_) for s_ in Prog.STREAMS]
        build_l1attn(kb, dev=False, na=8)
        kb.cfg["phase_dep_l3"] = [kb.p.fence(s_) for s_ in Prog.STREAMS]
        build_moe(kb, dev=False)
        kb.p.finalize(final_waits=[kb.p.fence(e) for e in Prog.ENGINES])
    return kb


def make_fused_inputs(inputs, core):
    m = make_l0_inputs(inputs, core)
    m.update(make_l2_inputs(inputs, core, [None] * 8, [None] * 8))
    m.update(make_l3_inputs(inputs, core, [None] * 8, [None] * 8, [None] * 8, [None] * 8))
    return m


def kernel(**inputs):
    cores = list(range(NCORE))
    kb = build_fused_program()
    res = run_bass_kernel_spmd(kb.nc, [make_fused_inputs(inputs, c) for c in cores],
                               core_ids=cores)
    out = np.zeros((B, L, D), np.float32)
    for c in cores:
        b, h = c // 2, c % 2
        out[b, h * NTOK:(h + 1) * NTOK] = np.asarray(res.results[c]["x3"])
    return out


def kernel_unfused(**inputs):
    cores = list(range(NCORE))
    kb0 = build_l0_program(io=True)
    res0 = run_bass_kernel_spmd(kb0.nc, [make_l0_inputs(inputs, c) for c in cores],
                                core_ids=cores)
    x2_all = [np.asarray(res0.results[c]["x2_d"]) for c in cores]
    x32_all = [np.asarray(res0.results[c]["xn1t32_d"]) for c in cores]
    x16_all = [np.asarray(res0.results[c]["xn1t16_d"]) for c in cores]
    kb2 = build_l2_program(io=True)
    res2 = run_bass_kernel_spmd(kb2.nc, [make_l2_inputs(inputs, c, x16_all, x32_all) for c in cores],
                                core_ids=cores)
    s_all = [np.asarray(res2.results[c]["s_d"]) for c in cores]
    kb3 = build_l3_program()
    res3 = run_bass_kernel_spmd(kb3.nc, [make_l3_inputs(inputs, c, x16_all, x2_all, s_all, x32_all)
                                         for c in cores], core_ids=cores)
    out = np.zeros((B, L, D), np.float32)
    for c in cores:
        b, h = c // 2, c % 2
        out[b, h * NTOK:(h + 1) * NTOK] = np.asarray(res3.results[c]["x3"])
    return out
